# version 06
# baseline (speedup 1.0000x reference)
import math
from contextlib import ExitStack

import numpy as np
import ml_dtypes
import concourse.bass as bass
import concourse.mybir as mybir
from concourse.bass_utils import run_bass_kernel_spmd

F32 = mybir.dt.float32
BF16 = mybir.dt.bfloat16
AF = mybir.ActivationFunctionType
ALU = mybir.AluOpType
AX = mybir.AxisListType

NT = 4096
D = 1024
KT = D // 128
TT = NT // 128
CH = NT // 512
H = 16
DH = 64
MF = 256
E = 16
DFF = 2048
DHID = 4096
KH = DHID // 128

DN = float(DH) ** -0.25
RATIO = float(MF) ** -0.5
LNRATIO = math.log(RATIO)
CEPS = RATIO * 1e-4
LNEPS = 1e-5

ALL_PHASES = ["ln0", "rout", "a1", "stab", "a2", "a3", "a4", "moe",
              "comb", "gram", "amat", "attn1", "f1", "f2", "f3", "f4"]


class Ctx:
    def __init__(self, nc, st):
        self.nc = nc
        self.st = st
        self.t = {}
        self.cms = {}
        self.phase_sems = []

    def sbuf(self, name, shape, dtype):
        cm = self.nc.sbuf_tensor("sb_" + name, shape, dtype)
        h = cm.__enter__()
        self.cms[name] = cm
        self.t[name] = h
        return h

    def free(self, *names):
        for name in names:
            cm = self.cms.pop(name, None)
            if cm is not None:
                cm.__exit__(None, None, None)
            self.t.pop(name, None)

    def sem(self, name):
        h = self.nc.alloc_semaphore(name=name)
        self.phase_sems.append(h)
        return h

    def end_phase(self):
        if self.phase_sems:
            self.nc.clear_and_free_semaphores(self.phase_sems)
            self.nc.all_engine_barrier()
            self.phase_sems = []


def build_program(upto=None, dbg=()):
    nc = bass.Bass("TRN2", target_bir_lowering=False)
    phases = ALL_PHASES if upto is None else ALL_PHASES[:ALL_PHASES.index(upto) + 1]

    x_in = nc.dram_tensor("x", [NT, D], F32, kind="ExternalInput")
    wq_in = nc.dram_tensor("wq", [D, D], F32, kind="ExternalInput")
    wk_in = nc.dram_tensor("wk", [D, D], F32, kind="ExternalInput")
    wv_in = nc.dram_tensor("wv", [D, D], F32, kind="ExternalInput")
    wo_in = nc.dram_tensor("wo", [D, D], F32, kind="ExternalInput")
    bo_in = nc.dram_tensor("bo", [128, KT], F32, kind="ExternalInput")
    projT_in = nc.dram_tensor("projT", [DH, MF], F32, kind="ExternalInput")
    wg_in = nc.dram_tensor("wg", [D, E], F32, kind="ExternalInput")
    bg_in = nc.dram_tensor("bg", [E, 1], F32, kind="ExternalInput")
    gbias_in = nc.dram_tensor("gbias", [E, 1], F32, kind="ExternalInput")
    w1e_in = nc.dram_tensor("w1e", [E, D, DFF], F32, kind="ExternalInput")
    b1e_in = nc.dram_tensor("b1e", [E, 128, DFF // 128], F32, kind="ExternalInput")
    w2e_in = nc.dram_tensor("w2e", [E, DFF, D], F32, kind="ExternalInput")
    b2e_in = nc.dram_tensor("b2e", [E, 128, KT], F32, kind="ExternalInput")
    q1_in = nc.dram_tensor("q1", [D, D], F32, kind="ExternalInput")
    w1T_in = nc.dram_tensor("w1T", [D, DHID], F32, kind="ExternalInput")
    w2T_in = nc.dram_tensor("w2T", [D, DHID], F32, kind="ExternalInput")
    h1T_in = nc.dram_tensor("h1T", [DHID, DHID], F32, kind="ExternalInput")
    h2T_in = nc.dram_tensor("h2T", [DHID, DHID], F32, kind="ExternalInput")
    w3T_in = nc.dram_tensor("w3T", [DHID, D], F32, kind="ExternalInput")
    s1_in = nc.dram_tensor("s1", [128, KH], F32, kind="ExternalInput")
    s2_in = nc.dram_tensor("s2", [128, KH], F32, kind="ExternalInput")
    sh1_in = nc.dram_tensor("sh1", [128, KH], F32, kind="ExternalInput")
    sh2_in = nc.dram_tensor("sh2", [128, KH], F32, kind="ExternalInput")
    s3_in = nc.dram_tensor("s3", [128, KT], F32, kind="ExternalInput")
    identf_in = nc.dram_tensor("identf", [128, 128], F32, kind="ExternalInput")
    identb_in = nc.dram_tensor("identb", [128, 128], BF16, kind="ExternalInput")
    sel64_in = nc.dram_tensor("sel64", [128, 2], F32, kind="ExternalInput")
    ones64s_in = nc.dram_tensor("ones64s", [DH, 1], F32, kind="ExternalInput")
    selE_in = nc.dram_tensor("selE", [E, E * 128], BF16, kind="ExternalInput")
    onesr_in = nc.dram_tensor("onesr", [1, 128], BF16, kind="ExternalInput")
    onesrf_in = nc.dram_tensor("onesrf", [1, 128], F32, kind="ExternalInput")

    out_t = nc.dram_tensor("out", [NT, D], F32, kind="ExternalOutput")

    xn_fm = nc.dram_tensor("xn_fm", [D, NT], BF16)
    attn_s = nc.dram_tensor("attn_s", [NT, D], F32)
    moe_s = nc.dram_tensor("moe_s", [NT, D], F32)
    x1_s = nc.dram_tensor("x1_s", [NT, D], F32)
    attn1_s = nc.dram_tensor("attn1_s", [NT, D], F32)
    h_s = nc.dram_tensor("h_s", [DHID, NT], BF16)
    g1_s = nc.dram_tensor("g1_s", [DHID, NT], BF16)
    g2_s = nc.dram_tensor("g2_s", [DHID, NT], BF16)
    cw_s = nc.dram_tensor("cw_s", [E, NT], BF16)
    dd_dbg = nc.dram_tensor("dd_dbg", [1, 512], F32)
    cc_in_b = nc.dram_tensor("cc_in_b", [1, 512], F32)
    cc_out_b = nc.dram_tensor("cc_out_b", [1, 512], F32, addr_space="Shared")

    scr = {"xn_fm": xn_fm, "attn_s": attn_s, "moe_s": moe_s, "x1_s": x1_s,
           "attn1_s": attn1_s, "h_s": h_s, "g1_s": g1_s, "g2_s": g2_s,
           "stab": cc_out_b, "dd_dbg": dd_dbg, "cw_s": cw_s}
    dbg_outs = {}
    for name in dbg:
        src = scr[name]
        dbg_outs[name] = (nc.dram_tensor("dbg_" + name, list(src.shape), src.dtype,
                                         kind="ExternalOutput"), src)

    with ExitStack() as st:
        c = Ctx(nc, st)
        c.dram = dict(x=x_in, wq=wq_in, wk=wk_in, wv=wv_in, wo=wo_in, bo=bo_in,
                      projT=projT_in, wg=wg_in, bg=bg_in, gbias=gbias_in,
                      w1e=w1e_in, b1e=b1e_in, w2e=w2e_in, b2e=b2e_in, q1=q1_in,
                      w1T=w1T_in, w2T=w2T_in, h1T=h1T_in, h2T=h2T_in, w3T=w3T_in,
                      s1=s1_in, s2=s2_in, sh1=sh1_in, sh2=sh2_in, s3=s3_in,
                      out=out_t, xn_fm=xn_fm, attn_s=attn_s, moe_s=moe_s,
                      x1_s=x1_s, attn1_s=attn1_s, h_s=h_s, g1_s=g1_s, g2_s=g2_s,
                      cc_in=cc_in_b, cc_out=cc_out_b, dd_dbg=dd_dbg, cw_s=cw_s)

        identf = c.sbuf("identf", [128, 128], F32)
        identb = c.sbuf("identb", [128, 128], BF16)
        sel64 = c.sbuf("sel64", [128, 2], F32)
        ones64s = c.sbuf("ones64s", [DH, 1], F32)
        selE = c.sbuf("selE", [E, E * 128], BF16)
        onesr = c.sbuf("onesr", [1, 128], BF16)
        onesrf = c.sbuf("onesrf", [1, 128], F32)
        projTdn = c.sbuf("projTdn", [128, MF], BF16)
        cst = c.sbuf("cst", [128, 6], F32)
        s_pre = c.sem("s_pre")
        s_preg = c.sem("s_preg")

        with nc.Block() as blk:
            @blk.sync
            def _(s):
                s.dma_start(out=identf[:, :], in_=identf_in.ap()).then_inc(s_pre, 16)
                s.dma_start(out=identb[:, :], in_=identb_in.ap()).then_inc(s_pre, 16)
                s.dma_start(out=sel64[:, :], in_=sel64_in.ap()).then_inc(s_pre, 16)
                s.dma_start(out=ones64s[:, :], in_=ones64s_in.ap()).then_inc(s_pre, 16)
                s.dma_start(out=selE[:, :], in_=selE_in.ap()).then_inc(s_pre, 16)
                s.dma_start(out=onesr[:, :], in_=onesr_in.ap()).then_inc(s_pre, 16)
                s.dma_start(out=onesrf[:, :], in_=onesrf_in.ap()).then_inc(s_pre, 16)

            @blk.gpsimd
            def _(g):
                g.dma_start(out=projTdn[0:DH, :], in_=projT_in.ap()).then_inc(s_preg, 16)
                g.dma_start(out=projTdn[DH:128, :], in_=projT_in.ap()).then_inc(s_preg, 16)

            @blk.vector
            def _(v):
                v.memset(cst[:, 0:1], LNEPS)
                v.memset(cst[:, 1:2], -0.5)
                v.memset(cst[:, 2:3], -1.0)
                v.memset(cst[:, 3:4], 1.0)
                v.memset(cst[:, 4:5], LNRATIO)
                v.memset(cst[:, 5:6], CEPS)
                v.wait_ge(s_pre, 16 * 7)
                v.wait_ge(s_preg, 32)
                v.tensor_scalar(projTdn[:, :], projTdn[:, :], DN, None, op0=ALU.mult)

        c.end_phase()
        cwT = c.sbuf("cwT", [E, NT], BF16)
        stabc = c.sbuf("stabc", [128, 1], F32)
        diagk = c.sbuf("diagk", [128, H * TT], F32)
        ctx_sb = c.sbuf("ctx", [128, H * 130], BF16)
        xnT = c.sbuf("xnT", [128, KT * NT], BF16)

        if "ln0" in phases:
            emit_ln0(c)
            c.end_phase()
        if "rout" in phases:
            emit_routing(c)
            c.end_phase()
        if "a1" in phases:
            emit_a1_k(c)
            c.end_phase()
        if "stab" in phases:
            emit_stab(c)
            c.end_phase()
        if "a2" in phases:
            emit_a2_ctx(c)
            c.end_phase()
        if "a3" in phases:
            emit_a3_q(c)
            c.end_phase()
        if "a4" in phases:
            emit_a4_wo(c)
            c.end_phase()
        if "moe" in phases:
            emit_moe(c)
            c.end_phase()
        if "comb" in phases:
            emit_combine_ln1(c)
            c.end_phase()
        if "gram" in phases:
            emit_gram(c)
            c.end_phase()
        if "amat" in phases:
            emit_amat(c)
            c.end_phase()
        if "attn1" in phases:
            emit_attn1(c)
            c.end_phase()
        if "f1" in phases:
            emit_f1(c)
            c.end_phase()
        if "f2" in phases:
            emit_f23(c, "h_s", "h1T", "sh1", "g1_s", tag="f2")
            c.end_phase()
        if "f3" in phases:
            emit_f23(c, "g1_s", "h2T", "sh2", "g2_s", tag="f3")
            c.end_phase()
        if "f4" in phases:
            emit_f4(c)
            c.end_phase()

        s_fin = c.sem("s_fin")
        with nc.Block() as blk:
            @blk.sync
            def _(s):
                n = 0
                for name, (dst, src) in dbg_outs.items():
                    s.dma_start(out=dst.ap(), in_=src.ap()).then_inc(s_fin, 16)
                    n += 16
                if "f4" not in phases:
                    s.dma_start(out=out_t.ap(), in_=x_in.ap()).then_inc(s_fin, 16)
                    n += 16
                if n:
                    s.wait_ge(s_fin, n)
    return nc


def emit_ln0(c):
    nc = c.nc
    xnT, identb = c.t["xnT"], c.t["identb"]
    x_in, xn_fm = c.dram["x"], c.dram["xn_fm"]
    with ExitStack() as st, nc.Block() as blk:
        xt = [st.enter_context(nc.sbuf_tensor(f"ln_x{j}", [128, D], F32)) for j in range(2)]
        xnt = [st.enter_context(nc.sbuf_tensor(f"ln_n{j}", [128, D], BF16)) for j in range(2)]
        stat = [st.enter_context(nc.sbuf_tensor(f"ln_s{j}", [128, 8], F32)) for j in range(2)]
        sq = st.enter_context(nc.sbuf_tensor("ln_sq", [128, D], F32))
        ps = [st.enter_context(nc.psum_tensor(f"ln_ps{j}", [128, 1024], BF16)) for j in range(2)]
        s_ld = [c.sem("ln_ld0"), c.sem("ln_ld1")]
        s_sv = c.sem("ln_sv")
        s_st = c.sem("ln_st")
        s_n = c.sem("ln_n")
        s_tr = c.sem("ln_tr")
        s_ev = c.sem("ln_ev")
        s_out = c.sem("ln_out")

        @blk.sync
        def _(s):
            for i in range(TT):
                if i >= 2:
                    s.wait_ge(s_n, i - 1)
                s.dma_start(out=xt[i % 2][:, :], in_=x_in.ap()[i * 128:(i + 1) * 128, :]).then_inc(s_ld[i % 2], 16)

        @blk.vector
        def _(v):
            for i in range(TT):
                b = i % 2
                v.wait_ge(s_ld[b], 16 * (i // 2 + 1))
                if i >= 2:
                    v.wait_ge(s_tr, 8 * (i - 1))
                v.reduce_sum(stat[b][:, 0:1], xt[b][:, :], axis=AX.X).then_inc(s_sv, 1)
                v.wait_ge(s_st, 3 * i + 1)
                v.wait_ge(s_sv, 3 * i + 1)
                v.tensor_scalar(stat[b][:, 2:3], stat[b][:, 0:1], 1.0 / D, None, op0=ALU.mult).then_inc(s_sv, 1)
                v.wait_ge(s_sv, 3 * i + 2)
                v.tensor_tensor(stat[b][:, 3:4], stat[b][:, 2:3], stat[b][:, 2:3], op=ALU.mult).then_inc(s_sv, 1)
                v.wait_ge(s_sv, 3 * i + 3)
                v.scalar_tensor_tensor(stat[b][:, 3:4], stat[b][:, 1:2], 1.0 / D, stat[b][:, 3:4],
                                       op0=ALU.mult, op1=ALU.subtract).then_inc(s_st, 1)
                v.wait_ge(s_st, 3 * i + 3)
                v.scalar_tensor_tensor(stat[b][:, 6:7], stat[b][:, 2:3], -1.0, stat[b][:, 5:6],
                                       op0=ALU.mult, op1=ALU.mult)
                v.tensor_scalar(xnt[b][:, :], xt[b][:, :], stat[b][:, 5:6], stat[b][:, 6:7],
                                op0=ALU.mult, op1=ALU.add).then_inc(s_n, 1)
                if i >= 1:
                    v.wait_ge(s_tr, 8 * i)
                    pb = ps[(i - 1) % 2]
                    for kt in range(KT):
                        v.tensor_copy(xnT[:, kt * NT + (i - 1) * 128: kt * NT + i * 128],
                                      pb[:, kt * 128:(kt + 1) * 128]).then_inc(s_ev, 1)
            v.wait_ge(s_tr, 8 * TT)
            pb = ps[(TT - 1) % 2]
            for kt in range(KT):
                v.tensor_copy(xnT[:, kt * NT + (TT - 1) * 128: kt * NT + TT * 128],
                              pb[:, kt * 128:(kt + 1) * 128]).then_inc(s_ev, 1)

        @blk.scalar
        def _(sc):
            for i in range(TT):
                b = i % 2
                sc.wait_ge(s_ld[b], 16 * (i // 2 + 1))
                if i >= 2:
                    sc.wait_ge(s_n, i - 1)
                sc.activation(sq[:, :], xt[b][:, :], AF.Square, accum_out=stat[b][:, 1:2]).then_inc(s_st, 1)
                sc.wait_ge(s_st, 3 * i + 2)
                sc.activation(stat[b][:, 4:5], stat[b][:, 3:4], AF.Ln, bias=c.t["cst"][:, 0:1])
                sc.activation(stat[b][:, 5:6], stat[b][:, 4:5], AF.Exp, scale=c.t["cst"][:, 1:2]).then_inc(s_st, 1)

        @blk.tensor
        def _(t):
            for i in range(TT):
                b = i % 2
                t.wait_ge(s_n, i + 1)
                if i >= 2:
                    t.wait_ge(s_ev, 8 * (i - 1))
                for kt in range(KT):
                    t.transpose(ps[i % 2][:, kt * 128:(kt + 1) * 128],
                                xnt[b][:, kt * 128:(kt + 1) * 128],
                                identb[:, :]).then_inc(s_tr, 1)

        @blk.gpsimd
        def _(g):
            g.wait_ge(s_ev, 8 * TT)
            for kt in range(KT):
                g.dma_start(out=xn_fm.ap()[kt * 128:(kt + 1) * 128, :],
                            in_=xnT[:, kt * NT:(kt + 1) * NT]).then_inc(s_out, 16)
            g.wait_ge(s_out, 16 * KT)


def emit_routing(c):
    nc = c.nc
    xnT, identf, cwT = c.t["xnT"], c.t["identf"], c.t["cwT"]
    wg_in, bg_in, gbias_in = c.dram["wg"], c.dram["bg"], c.dram["gbias"]
    with ExitStack() as st, nc.Block() as blk:
        wg_sb = st.enter_context(nc.sbuf_tensor("wg_sb", [128, KT * E], BF16))
        bgb = st.enter_context(nc.sbuf_tensor("bgb", [E, 2], F32))
        lgT = st.enter_context(nc.sbuf_tensor("lgT", [E, NT], F32))
        lt = st.enter_context(nc.sbuf_tensor("rt_lt", [128, E], F32))
        m1 = st.enter_context(nc.sbuf_tensor("rt_m1", [128, E], F32))
        m2 = st.enter_context(nc.sbuf_tensor("rt_m2", [128, E], F32))
        cwt = st.enter_context(nc.sbuf_tensor("rt_cw", [128, E], F32))
        vv = st.enter_context(nc.sbuf_tensor("rt_v", [128, 4], F32))
        ww = st.enter_context(nc.sbuf_tensor("rt_w", [128, 2], F32))
        psl = st.enter_context(nc.psum_tensor("rt_psl", [E, 512], F32))
        pst = [st.enter_context(nc.psum_tensor(f"rt_pst{j}", [128, 512], F32)) for j in range(2)]
        psb = [st.enter_context(nc.psum_tensor(f"rt_psb{j}", [128, 512], F32)) for j in range(2)]
        s_ld = c.sem("rt_ld")
        s_sv = c.sem("rt_sv")
        s_mm = c.sem("rt_mm")
        s_bt = c.sem("rt_btm")
        s_ev = c.sem("rt_ev")
        s_r = c.sem("rt_r")
        s_r2 = c.sem("rt_r2")
        s_tb = c.sem("rt_tb")

        s_ldg = c.sem("rt_ldg")

        @blk.sync
        def _(s):
            s.dma_start(out=bgb[:, 0:1], in_=bg_in.ap()).then_inc(s_ld, 16)
            s.dma_start(out=bgb[:, 1:2], in_=gbias_in.ap()).then_inc(s_ld, 16)

        @blk.tensor
        def _(t):
            t.wait_ge(s_ldg, 16)
            for ch in range(CH):
                if ch >= 1:
                    t.wait_ge(s_ev, ch)
                for kt in range(KT):
                    ins = t.matmul(psl[:, :], wg_sb[:, kt * E:(kt + 1) * E],
                                   xnT[:, kt * NT + ch * 512: kt * NT + ch * 512 + 512],
                                   start=(kt == 0), stop=(kt == KT - 1))
                ins.then_inc(s_mm, 1)
            t.wait_ge(s_ev, CH)
            for i in range(TT):
                if i >= 2:
                    t.wait_ge(s_r, 3 * (i - 2) + 1)
                t.transpose(pst[i % 2][:, 0:16], lgT[:, i * 128:(i + 1) * 128],
                            identf[0:E, 0:E]).then_inc(s_mm, 1)
                if i >= 1:
                    j = i - 1
                    t.wait_ge(s_r, 3 * j + 3)
                    if j >= 2:
                        t.wait_ge(s_tb, j - 1)
                    t.transpose(psb[j % 2][0:E, 0:128], cwt[:, :], identf[:, :]).then_inc(s_bt, 1)
            j = TT - 1
            t.wait_ge(s_r, 3 * j + 3)
            t.wait_ge(s_tb, j - 1)
            t.transpose(psb[j % 2][0:E, 0:128], cwt[:, :], identf[:, :]).then_inc(s_bt, 1)

        @blk.vector
        def _(v):
            v.wait_ge(s_ld, 32)
            v.tensor_tensor(bgb[:, 0:1], bgb[:, 0:1], bgb[:, 1:2], op=ALU.add)
            for ch in range(CH):
                v.wait_ge(s_mm, ch + 1)
                v.tensor_scalar(lgT[:, ch * 512:(ch + 1) * 512], psl[:, :], bgb[:, 0:1], None,
                                op0=ALU.add).then_inc(s_ev, 1)
            nsv = 0
            for i in range(TT):
                v.wait_ge(s_mm, CH + i + 1)
                def gate(ins):
                    nonlocal nsv
                    nsv += 1
                    ins.then_inc(s_sv, 1)
                    v.wait_ge(s_sv, nsv)
                v.tensor_copy(lt[:, :], pst[i % 2][:, 0:16]).then_inc(s_r, 1)
                v.wait_ge(s_r, 3 * i + 1)
                gate(v.reduce_max(vv[:, 0:1], lt[:, :], axis=AX.X))
                gate(v.tensor_scalar(m1[:, :], lt[:, :], vv[:, 0:1], None, op0=ALU.is_ge))
                gate(v.scalar_tensor_tensor(m2[:, :], m1[:, :], -1e9, lt[:, :], op0=ALU.mult, op1=ALU.add))
                gate(v.reduce_max(vv[:, 2:3], m2[:, :], axis=AX.X))
                gate(v.tensor_scalar(m2[:, :], m2[:, :], vv[:, 2:3], None, op0=ALU.is_ge))
                v.tensor_tensor(vv[:, 3:4], vv[:, 2:3], vv[:, 0:1], op=ALU.subtract).then_inc(s_r, 1)
                v.wait_ge(s_r2, i + 1)
                gate(v.tensor_scalar(m1[:, :], m1[:, :], ww[:, 0:1], None, op0=ALU.mult))
                if i >= 1:
                    v.wait_ge(s_bt, i)
                v.scalar_tensor_tensor(cwt[:, :], m2[:, :], ww[:, 1:2], m1[:, :],
                                       op0=ALU.mult, op1=ALU.add).then_inc(s_r, 1)
                if i >= 1:
                    v.wait_ge(s_bt, i)
                    v.tensor_copy(cwT[:, (i - 1) * 128: i * 128], psb[(i - 1) % 2][0:E, 0:128]).then_inc(s_tb, 1)
            v.wait_ge(s_bt, TT)
            v.tensor_copy(cwT[:, (TT - 1) * 128: TT * 128], psb[(TT - 1) % 2][0:E, 0:128]).then_inc(s_tb, 1)

        @blk.scalar
        def _(sc):
            for i in range(TT):
                sc.wait_ge(s_r, 3 * i + 2)
                sc.activation(ww[:, 1:2], vv[:, 3:4], AF.Sigmoid)
                sc.activation(ww[:, 0:1], vv[:, 3:4], AF.Sigmoid, scale=c.t["cst"][:, 2:3]).then_inc(s_r2, 1)

        s_cwd = c.sem("rt_cwd")

        @blk.gpsimd
        def _(g):
            g.dma_start(out=wg_sb[:, :].rearrange("p (kt o) -> p kt o", kt=KT),
                        in_=wg_in.ap().rearrange("(kt p) o -> p kt o", p=128)).then_inc(s_ldg, 16)
            g.wait_ge(s_tb, TT)
            g.dma_start(out=c.dram["cw_s"].ap(), in_=cwT[:, :]).then_inc(s_cwd, 16)
            g.wait_ge(s_cwd, 16)


def emit_a1_k(c):
    nc = c.nc
    xnT, sel64, diagk = c.t["xnT"], c.t["sel64"], c.t["diagk"]
    wk_in = c.dram["wk"]
    kT = c.sbuf("kT", [128, KT * NT], BF16)
    with ExitStack() as st, nc.Block() as blk:
        wk_sb = st.enter_context(nc.sbuf_tensor("wk_sb", [128, KT * D], BF16))
        sqb = st.enter_context(nc.sbuf_tensor("a1_sq", [128, 512], F32))
        psk = [st.enter_context(nc.psum_tensor(f"a1_ps{j}", [128, 512], F32)) for j in range(4)]
        psd = [st.enter_context(nc.psum_tensor(f"a1_pd{j}", [128, 512], F32)) for j in range(2)]
        s_ld = c.sem("a1_ld")
        s_mm = c.sem("a1_mm")
        s_ev = c.sem("a1_ev")
        s_sq = c.sem("a1_sq")
        s_dmm = c.sem("a1_dmm")
        s_dev = c.sem("a1_dev")
        sched = [(m, ch) for m in range(KT) for ch in range(CH)]

        @blk.gpsimd
        def _(g):
            g.dma_start(out=wk_sb[:, :].rearrange("p (kt o) -> p kt o", kt=KT),
                        in_=wk_in.ap().rearrange("(kt p) o -> p kt o", p=128)).then_inc(s_ld, 16)

        @blk.tensor
        def _(t):
            t.wait_ge(s_ld, 16)
            for idx, (m, ch) in enumerate(sched):
                if idx >= 4:
                    t.wait_ge(s_ev, idx - 3)
                for kt in range(KT):
                    ins = t.matmul(psk[idx % 4][:, :], wk_sb[:, kt * D + m * 128: kt * D + (m + 1) * 128],
                                   xnT[:, kt * NT + ch * 512: kt * NT + ch * 512 + 512],
                                   start=(kt == 0), stop=(kt == KT - 1))
                ins.then_inc(s_mm, 1)
            for idx, (m, ch) in enumerate(sched):
                t.wait_ge(s_sq, idx + 1)
                if idx >= 2:
                    t.wait_ge(s_dev, idx - 1)
                for j in range(4):
                    ins = t.matmul(psd[idx % 2][:, 2 * j: 2 * j + 2],
                                   sqb[:, j * 128:(j + 1) * 128], sel64[:, :],
                                   start=True, stop=True)
                ins.then_inc(s_dmm, 1)

        @blk.vector
        def _(v):
            for idx, (m, ch) in enumerate(sched):
                v.wait_ge(s_mm, idx + 1)
                v.tensor_copy(kT[:, m * NT + ch * 512: m * NT + ch * 512 + 512],
                              psk[idx % 4][:, :]).then_inc(s_ev, 1)
            for idx, (m, ch) in enumerate(sched):
                v.wait_ge(s_dmm, idx + 1)
                for j in range(4):
                    i_tile = ch * 4 + j
                    v.tensor_copy(diagk[:, (2 * m) * TT + i_tile: (2 * m) * TT + i_tile + 1],
                                  psd[idx % 2][:, 2 * j: 2 * j + 1])
                    ins = v.tensor_copy(diagk[:, (2 * m + 1) * TT + i_tile: (2 * m + 1) * TT + i_tile + 1],
                                        psd[idx % 2][:, 2 * j + 1: 2 * j + 2])
                ins.then_inc(s_dev, 1)

        @blk.scalar
        def _(sc):
            for idx, (m, ch) in enumerate(sched):
                sc.wait_ge(s_ev, idx + 1)
                if idx >= 1:
                    sc.wait_ge(s_dmm, idx)
                sc.activation(sqb[:, :], kT[:, m * NT + ch * 512: m * NT + ch * 512 + 512],
                              AF.Square).then_inc(s_sq, 1)


def emit_stab(c):
    nc = c.nc
    kT, projTdn, onesrf, stabc = c.t["kT"], c.t["projTdn"], c.t["onesrf"], c.t["stabc"]
    identf = c.t["identf"]
    cc_in, cc_out = c.dram["cc_in"], c.dram["cc_out"]
    with ExitStack() as st, nc.Block() as blk:
        mx = st.enter_context(nc.sbuf_tensor("st_mx", [128, 2], F32))
        mxv = st.enter_context(nc.sbuf_tensor("st_mxv", [1, 1], F32))
        row = st.enter_context(nc.sbuf_tensor("st_row", [1, 512], F32))
        gl = st.enter_context(nc.sbuf_tensor("st_gl", [1, 512], F32))
        psd = [st.enter_context(nc.psum_tensor(f"st_ps{j}", [128, 512], F32)) for j in range(4)]
        pst = st.enter_context(nc.psum_tensor("st_pst", [128, 512], F32))
        s_mm = c.sem("st_mm")
        s_sv = c.sem("st_sv")
        s_red = c.sem("st_red")
        s_cc = c.sem("st_cc")
        s_d = c.sem("st_d")
        sched = [(h, i) for h in range(H) for i in range(TT)]
        NS = len(sched)

        @blk.tensor
        def _(t):
            for idx, (h, i) in enumerate(sched):
                if idx >= 4:
                    t.wait_ge(s_red, idx - 3)
                po = (h % 2) * 64
                lhsT = kT[po:po + 64, (h // 2) * NT + i * 128:(h // 2) * NT + (i + 1) * 128]
                t.matmul(psd[idx % 4][:, 0:256], lhsT, projTdn[po:po + DH, :], start=True, stop=True).then_inc(s_mm, 1)
            t.wait_ge(s_red, NS)
            t.transpose(pst[0:1, 0:128], mx[:, 0:1], identf[:, :]).then_inc(s_mm, 1)
            t.wait_ge(s_d, 32)
            t.matmul(pst[:, 128:129], onesrf[:, :], gl[0:1, 0:1], start=True, stop=True).then_inc(s_mm, 1)

        @blk.vector
        def _(v):
            v.memset(mx[:, 0:1], -3e38).then_inc(s_sv, 1)
            nsv = 1
            for idx, (h, i) in enumerate(sched):
                v.wait_ge(s_mm, idx + 1)
                v.reduce_max(mx[:, 1:2], psd[idx % 4][:, 0:256], axis=AX.X).then_inc(s_sv, 1)
                nsv += 1
                v.wait_ge(s_sv, nsv)
                v.tensor_tensor(mx[:, 0:1], mx[:, 0:1], mx[:, 1:2], op=ALU.max).then_inc(s_red, 1)
                nsv += 0
                v.wait_ge(s_red, idx + 1)
            v.wait_ge(s_mm, NS + 1)
            v.reduce_max(mxv[0:1, 0:1], pst[0:1, 0:128], axis=AX.X)
            v.memset(row[0:1, :], 1.0).then_inc(s_sv, 1)
            nsv += 1
            v.wait_ge(s_sv, nsv)
            v.tensor_scalar(row[0:1, :], row[0:1, :], mxv[0:1, 0:1], None, op0=ALU.mult).then_inc(s_red, 1)
            v.wait_ge(s_mm, NS + 2)
            v.scalar_tensor_tensor(stabc[:, :], pst[:, 128:129], -1.0, c.t["cst"][:, 4:5],
                                   op0=ALU.mult, op1=ALU.add).then_inc(s_red, 1)

        @blk.gpsimd
        def _(g):
            g.wait_ge(s_red, NS + 1)
            g.dma_start(out=cc_in.ap(), in_=row[0:1, :]).then_inc(s_d, 16)
            g.wait_ge(s_d, 16)
            g.collective_compute("AllReduce", ALU.max, replica_groups=[list(range(8))],
                                 ins=[cc_in.ap()], outs=[cc_out.ap()]).then_inc(s_cc)
            g.wait_ge(s_cc, 1)
            g.dma_start(out=gl[0:1, :], in_=cc_out.ap()).then_inc(s_d, 16)
            g.wait_ge(s_d, 32)


def emit_a2_ctx(c):
    nc = c.nc
    xnT, kT, projTdn = c.t["xnT"], c.t["kT"], c.t["projTdn"]
    diagk, stabc, identb = c.t["diagk"], c.t["stabc"], c.t["identb"]
    cst = c.t["cst"]
    wv_in = c.dram["wv"]
    ctx_sb = c.t["ctx"]
    with ExitStack() as st, nc.Block() as blk:
        wv_sb = st.enter_context(nc.sbuf_tensor("a2_wv", [128, KT * D], BF16))
        vT = st.enter_context(nc.sbuf_tensor("a2_vT", [64, NT], BF16))
        vtok = st.enter_context(nc.sbuf_tensor("a2_vtok", [128, TT * 65], BF16))
        diagk2 = st.enter_context(nc.sbuf_tensor("a2_dk2", [128, H * TT], F32))
        kpf = [st.enter_context(nc.sbuf_tensor(f"a2_kpf{j}", [128, MF], F32)) for j in range(3)]
        kpb = [st.enter_context(nc.sbuf_tensor(f"a2_kpb{j}", [128, MF], BF16)) for j in range(3)]
        ps_v = [st.enter_context(nc.psum_tensor(f"a2_pv{j}", [128, 512], F32)) for j in range(1)]
        ps_t = [st.enter_context(nc.psum_tensor(f"a2_pt{j}", [128, 1024], BF16)) for j in range(2)]
        ps_dd = [st.enter_context(nc.psum_tensor(f"a2_pd{j}", [128, 512], F32)) for j in range(3)]
        ps_cx = [st.enter_context(nc.psum_tensor(f"a2_pc{j}", [128, 512], F32)) for j in range(2)]
        s_wld = c.sem("a2_wld")
        s_vmm = c.sem("a2_vmm")
        s_vev = c.sem("a2_vev")
        s_tmm = c.sem("a2_tmm")
        s_vtev = c.sem("a2_vtev")
        s_ddmm = c.sem("a2_ddmm")
        s_b = c.sem("a2_b")
        s_kexp = c.sem("a2_kexp")
        s_kpe = c.sem("a2_kpe")
        s_cmm = c.sem("a2_cmm")
        s_cev = c.sem("a2_cev")

        @blk.gpsimd
        def _(g):
            g.dma_start(out=wv_sb[:, :].rearrange("p (kt o) -> p kt o", kt=KT),
                        in_=wv_in.ap().rearrange("(kt p) o -> p kt o", p=128)).then_inc(s_wld, 16)

        @blk.tensor
        def _(t):
            t.wait_ge(s_wld, 16)
            for h in range(H):
                po = (h % 2) * 64
                mt = h // 2
                for ch in range(CH):
                    idx = h * CH + ch
                    if idx >= 1:
                        t.wait_ge(s_vev, idx)
                    for kt in range(KT):
                        ins = t.matmul(ps_v[0][0:64, :],
                                       wv_sb[:, kt * D + h * 64: kt * D + h * 64 + 64],
                                       xnT[:, kt * NT + ch * 512: kt * NT + ch * 512 + 512],
                                       start=(kt == 0), stop=(kt == KT - 1))
                    ins.then_inc(s_vmm, 1)
                for i in range(TT):
                    idx = h * TT + i
                    t.wait_ge(s_vev, h * CH + i // 4 + 1)
                    if idx >= 2:
                        t.wait_ge(s_vtev, idx - 1)
                    t.transpose(ps_t[idx % 2][0:128, 0:64], vT[0:64, i * 128:(i + 1) * 128],
                                identb[0:64, 0:64]).then_inc(s_tmm, 1)
                for i in range(TT + 1):
                    idx = h * TT + i
                    if i < TT:
                        if idx >= 3:
                            t.wait_ge(s_kexp, 2 * (idx - 2))
                        t.matmul(ps_dd[idx % 3][:, 0:MF],
                                 kT[po:po + 64, mt * NT + i * 128: mt * NT + (i + 1) * 128],
                                 projTdn[po:po + DH, :], start=True, stop=True).then_inc(s_ddmm, 1)
                    if i >= 1:
                        j = i - 1
                        jdx = h * TT + j
                        t.wait_ge(s_kexp, 2 * (jdx + 1))
                        t.wait_ge(s_vtev, jdx + 1)
                        if j == 0 and h >= 1:
                            t.wait_ge(s_cev, 2 * h)
                        for m2 in range(2):
                            ins = t.matmul(ps_cx[m2][:, 0:65],
                                           kpb[j % 3][:, m2 * 128:(m2 + 1) * 128],
                                           vtok[:, j * 65: j * 65 + 65],
                                           start=(j == 0), stop=(j == TT - 1))
                            if m2 == 0:
                                ins.then_inc(s_kpe, 1)
                        if j == TT - 1:
                            ins.then_inc(s_cmm, 1)

        @blk.vector
        def _(v):
            v.tensor_scalar(diagk2[:, :], diagk[:, :], -1.0, None, op0=ALU.mult).then_inc(s_b, 1)
            v.wait_ge(s_b, 1)
            v.tensor_scalar(diagk2[:, :], diagk2[:, :], stabc[:, 0:1], None, op0=ALU.add).then_inc(s_b, 1)
            for h in range(H):
                for ch in range(CH):
                    idx = h * CH + ch
                    v.wait_ge(s_vmm, idx + 1)
                    v.tensor_copy(vT[0:64, ch * 512:(ch + 1) * 512], ps_v[0][0:64, :]).then_inc(s_vev, 1)
                if h == 0:
                    v.memset(vtok[:, :], 1.0)
                for i in range(TT):
                    idx = h * TT + i
                    v.wait_ge(s_tmm, idx + 1)
                    v.tensor_copy(vtok[:, i * 65: i * 65 + 64], ps_t[idx % 2][0:128, 0:64]).then_inc(s_vtev, 1)
                v.wait_ge(s_cmm, h + 1)
                v.tensor_copy(ctx_sb[:, h * 130: h * 130 + 65], ps_cx[0][:, 0:65]).then_inc(s_cev, 1)
                v.tensor_copy(ctx_sb[:, h * 130 + 65: h * 130 + 130], ps_cx[1][:, 0:65]).then_inc(s_cev, 1)

        @blk.scalar
        def _(sc):
            sc.wait_ge(s_b, 2)
            for h in range(H):
                for i in range(TT):
                    idx = h * TT + i
                    sc.wait_ge(s_ddmm, idx + 1)
                    if idx >= 3:
                        sc.wait_ge(s_kpe, idx - 2)
                    sc.activation(kpf[idx % 3][:, :], ps_dd[idx % 3][:, 0:MF], AF.Exp,
                                  bias=diagk2[:, h * TT + i: h * TT + i + 1]).then_inc(s_kexp, 1)
                    sc.activation(kpb[idx % 3][:, :], kpf[idx % 3][:, :], AF.Relu,
                                  bias=cst[:, 5:6]).then_inc(s_kexp, 1)
    c.free("kT")


def emit_a3_q(c):
    nc = c.nc
    xnT, projTdn, ctx_sb = c.t["xnT"], c.t["projTdn"], c.t["ctx"]
    identb, cst, identf = c.t["identb"], c.t["cst"], c.t["identf"]
    onesr = c.t["onesr"]
    wq_in = c.dram["wq"]
    attn_inT = c.sbuf("attn_inT", [128, KT * NT], BF16)
    with ExitStack() as st, nc.Block() as blk:
        wq_sb = st.enter_context(nc.sbuf_tensor("a3_wq", [128, KT * D], BF16))
        qT = st.enter_context(nc.sbuf_tensor("a3_qT", [64, NT], BF16))
        sqq = st.enter_context(nc.sbuf_tensor("a3_sq", [64, 512], F32))
        diagq = st.enter_context(nc.sbuf_tensor("a3_dq", [128, TT], F32))
        qpf = [st.enter_context(nc.sbuf_tensor(f"a3_qpf{j}", [128, MF], F32)) for j in range(2)]
        qpb = [st.enter_context(nc.sbuf_tensor(f"a3_qpb{j}", [128, MF], BF16)) for j in range(2)]
        nmr = [st.enter_context(nc.sbuf_tensor(f"a3_nm{j}", [128, 2], F32)) for j in range(2)]
        bsb = [st.enter_context(nc.sbuf_tensor(f"a3_b{j}", [128, 1], F32)) for j in range(2)]
        qpT = st.enter_context(nc.sbuf_tensor("a3_qpT", [128, 2 * NT], BF16))
        otmp = [st.enter_context(nc.sbuf_tensor(f"a3_ot{j}", [128, 512], F32)) for j in range(2)]
        dinb = [st.enter_context(nc.sbuf_tensor(f"a3_di{j}", [1, 512], BF16)) for j in range(2)]
        dinf = [st.enter_context(nc.sbuf_tensor(f"a3_df{j}", [1, 512], F32)) for j in range(2)]
        ps_q = [st.enter_context(nc.psum_tensor(f"a3_pq{j}", [128, 512], F32)) for j in range(2)]
        ps_dq = [st.enter_context(nc.psum_tensor(f"a3_pdq{j}", [128, 512], F32)) for j in range(2)]
        ps_tqf = st.enter_context(nc.psum_tensor("a3_ptq", [128, 512], F32))
        ps_o = [st.enter_context(nc.psum_tensor(f"a3_po{j}", [128, 512], F32)) for j in range(2)]
        ps_db = st.enter_context(nc.psum_tensor("a3_pdb", [128, 512], F32))
        ps_dd4 = [ps_q[0], ps_q[1], ps_dq[0], ps_dq[1]]
        s_wld = c.sem("a3_wld")
        s_qmm = c.sem("a3_qmm")
        s_qev = c.sem("a3_qev")
        s_qsq = c.sem("a3_qsq")
        s_dqm = c.sem("a3_dqm")
        s_dqe = c.sem("a3_dqe")
        s_ddmm = c.sem("a3_ddmm")
        s_sv = c.sem("a3_sv")
        s_b = c.sem("a3_b")
        s_qexp = c.sem("a3_qexp")
        s_qpe = c.sem("a3_qpe")
        s_tqm = c.sem("a3_tqm")
        s_tqe = c.sem("a3_tqe")
        s_omm = c.sem("a3_omm")
        s_oc = c.sem("a3_oc")
        s_dnm = c.sem("a3_dnm")
        s_db = c.sem("a3_db")
        s_bmm = c.sem("a3_bmm")
        s_aev = c.sem("a3_aev")
        nsv_cnt = [0]

        @blk.gpsimd
        def _(g):
            g.dma_start(out=wq_sb[:, :].rearrange("p (kt o) -> p kt o", kt=KT),
                        in_=wq_in.ap().rearrange("(kt p) o -> p kt o", p=128)).then_inc(s_wld, 16)

        def emit_out_mm(t, h, ch):
            po = (h % 2) * 64
            idx = h * CH + ch
            t.wait_ge(s_tqe, 2 * (h * TT + 4 * ch + 4))
            if idx >= 2:
                t.wait_ge(s_aev, idx - 1)
            for m2 in range(2):
                ins = t.matmul(ps_o[idx % 2][po:po + 64, :],
                               ctx_sb[:, h * 130 + m2 * 65: h * 130 + m2 * 65 + 64],
                               qpT[:, m2 * NT + ch * 512: m2 * NT + ch * 512 + 512],
                               start=(m2 == 0), stop=(m2 == 1))
            ins.then_inc(s_omm, 1)
            if idx >= 1:
                t.wait_ge(s_aev, idx)
            for m2 in range(2):
                ins = t.matmul(ps_db[0:1, 0:512],
                               ctx_sb[:, h * 130 + m2 * 65 + 64: h * 130 + m2 * 65 + 65],
                               qpT[:, m2 * NT + ch * 512: m2 * NT + ch * 512 + 512],
                               start=(m2 == 0), stop=(m2 == 1))
            ins.then_inc(s_dnm, 1)
            t.wait_ge(s_db, idx + 1)
            t.matmul(ps_db[po:po + 64, 0:512], onesr[0:1, 0:64],
                     dinb[idx % 2][0:1, :], start=True, stop=True).then_inc(s_bmm, 1)

        @blk.tensor
        def _(t):
            t.wait_ge(s_wld, 16)
            for h in range(H):
                po = (h % 2) * 64
                for ch in range(CH):
                    idx = h * CH + ch
                    if h >= 1 and ch < 2:
                        t.wait_ge(s_qexp, 2 * h * TT)
                    if idx >= 2:
                        t.wait_ge(s_qsq, idx - 1)
                        t.wait_ge(s_qev, idx - 1)
                    for kt in range(KT):
                        ins = t.matmul(ps_q[idx % 2][0:64, :],
                                       wq_sb[:, kt * D + h * 64: kt * D + h * 64 + 64],
                                       xnT[:, kt * NT + ch * 512: kt * NT + ch * 512 + 512],
                                       start=(kt == 0), stop=(kt == KT - 1))
                    ins.then_inc(s_qmm, 1)
                    t.wait_ge(s_qsq, idx + 1)
                    if idx >= 2:
                        t.wait_ge(s_dqe, idx - 1)
                    for j in range(4):
                        ins = t.matmul(ps_dq[idx % 2][:, 2 * j: 2 * j + 2], sqq[0:64, j * 128:(j + 1) * 128],
                                       c.t["sel64"][0:64, :], start=True, stop=True)
                    ins.then_inc(s_dqm, 1)
                for i in range(TT):
                    idx = h * TT + i
                    if i < 4:
                        t.wait_ge(s_qev, (h + 1) * CH)
                        t.wait_ge(s_qsq, (h + 1) * CH)
                        t.wait_ge(s_dqe, (h + 1) * CH)
                    if idx >= 4:
                        t.wait_ge(s_qexp, 2 * (idx - 3))
                    t.matmul(ps_dd4[idx % 4][:, 0:MF],
                             qT[0:64, i * 128:(i + 1) * 128],
                             projTdn[0:DH, :], start=True, stop=True).then_inc(s_ddmm, 1)
                    if i >= 1:
                        j = i - 1
                        jdx = h * TT + j
                        t.wait_ge(s_qexp, 2 * (jdx + 1))
                        if jdx >= 1:
                            t.wait_ge(s_tqe, 2 * jdx)
                        for m2 in range(2):
                            ins = t.transpose(ps_tqf[0:128, m2 * 128: m2 * 128 + 128],
                                              qpf[j % 2][:, m2 * 128:(m2 + 1) * 128],
                                              identf[:, :])
                            ins.then_inc(s_tqm if m2 == 1 else s_qpe, 1)
                        if j >= 3 and (j - 3) % 4 == 0:
                            emit_out_mm(t, h, (j - 3) // 4)
                j = TT - 1
                jdx = h * TT + j
                t.wait_ge(s_qexp, 2 * (jdx + 1))
                t.wait_ge(s_tqe, 2 * jdx)
                for m2 in range(2):
                    ins = t.transpose(ps_tqf[0:128, m2 * 128: m2 * 128 + 128],
                                      qpf[j % 2][:, m2 * 128:(m2 + 1) * 128],
                                      identf[:, :])
                    ins.then_inc(s_tqm if m2 == 1 else s_qpe, 1)
                emit_out_mm(t, h, CH - 1)

        @blk.scalar
        def _(sc):
            for h in range(H):
                po = (h % 2) * 64
                for ch in range(CH):
                    idx = h * CH + ch
                    sc.wait_ge(s_qmm, idx + 1)
                    sc.wait_ge(s_qev, idx + 1)
                    if idx >= 1:
                        sc.wait_ge(s_dqm, idx)
                    sc.activation(sqq[0:64, :], ps_q[idx % 2][0:64, :], AF.Square).then_inc(s_qsq, 1)
                for i in range(TT):
                    idx = h * TT + i
                    sc.wait_ge(s_ddmm, idx + 1)
                    sc.wait_ge(s_b, idx + 1)
                    if idx >= 2:
                        sc.wait_ge(s_qpe, idx - 1)
                    sc.activation(qpf[idx % 2][:, :], ps_dd4[idx % 4][:, 0:MF], AF.Exp,
                                  bias=bsb[idx % 2][:, 0:1]).then_inc(s_qexp, 2)
                    if i >= 4 and (i - 4) % 4 == 0:
                        ch = (i - 4) // 4
                        cidx = h * CH + ch
                        sc.wait_ge(s_omm, cidx + 1)
                        if cidx >= 2:
                            sc.wait_ge(s_aev, cidx - 1)
                        sc.copy(otmp[cidx % 2][po:po + 64, :], ps_o[cidx % 2][po:po + 64, :]).then_inc(s_oc, 1)
                        sc.wait_ge(s_dnm, cidx + 1)
                        sc.activation(dinf[cidx % 2][0:1, :], ps_db[0:1, 0:512], AF.Ln)
                        sc.activation(dinb[cidx % 2][0:1, :], dinf[cidx % 2][0:1, :], AF.Exp,
                                      scale=cst[0:1, 2:3]).then_inc(s_db, 1)
                for ch in (CH - 1,):
                    cidx = h * CH + ch
                    sc.wait_ge(s_omm, cidx + 1)
                    if cidx >= 2:
                        sc.wait_ge(s_aev, cidx - 1)
                    sc.copy(otmp[cidx % 2][po:po + 64, :], ps_o[cidx % 2][po:po + 64, :]).then_inc(s_oc, 1)
                    sc.wait_ge(s_dnm, cidx + 1)
                    sc.activation(dinf[cidx % 2][0:1, :], ps_db[0:1, 0:512], AF.Ln)
                    sc.activation(dinb[cidx % 2][0:1, :], dinf[cidx % 2][0:1, :], AF.Exp,
                                  scale=cst[0:1, 2:3]).then_inc(s_db, 1)

        def emit_out_ev(v, h, ch):
            po = (h % 2) * 64
            mt = h // 2
            idx = h * CH + ch
            v.wait_ge(s_oc, idx + 1)
            v.wait_ge(s_bmm, idx + 1)
            v.tensor_tensor(attn_inT[po:po + 64, mt * NT + ch * 512: mt * NT + ch * 512 + 512],
                            otmp[idx % 2][po:po + 64, :],
                            ps_db[po:po + 64, 0:512], op=ALU.mult).then_inc(s_aev, 1)

        @blk.vector
        def _(v):
            for h in range(H):
                for ch in range(CH):
                    idx = h * CH + ch
                    v.wait_ge(s_qmm, idx + 1)
                    v.tensor_copy(qT[0:64, ch * 512:(ch + 1) * 512], ps_q[idx % 2][0:64, :]).then_inc(s_qev, 1)
                    v.wait_ge(s_dqm, idx + 1)
                    for j in range(3):
                        v.tensor_copy(diagq[:, ch * 4 + j: ch * 4 + j + 1], ps_dq[idx % 2][:, 2 * j: 2 * j + 1])
                    v.tensor_copy(diagq[:, ch * 4 + 3: ch * 4 + 4], ps_dq[idx % 2][:, 6:7]).then_inc(s_dqe, 1)
                v.tensor_scalar(diagq[:, :], diagq[:, :], -1.0, None, op0=ALU.mult).then_inc(s_sv, 1)
                nsv_cnt[0] += 1
                v.wait_ge(s_sv, nsv_cnt[0])
                v.tensor_scalar(diagq[:, :], diagq[:, :], LNRATIO, None, op0=ALU.add).then_inc(s_sv, 1)
                nsv_cnt[0] += 1
                v.wait_ge(s_sv, nsv_cnt[0])
                for i in range(TT):
                    idx = h * TT + i
                    v.wait_ge(s_ddmm, idx + 1)
                    v.reduce_max(nmr[idx % 2][:, 0:1], ps_dd4[idx % 4][:, 0:MF], axis=AX.X,
                                 negate=True).then_inc(s_sv, 1)
                    nsv_cnt[0] += 1
                    v.wait_ge(s_sv, nsv_cnt[0])
                    if idx >= 2:
                        v.wait_ge(s_qexp, 2 * (idx - 1))
                    v.tensor_tensor(bsb[idx % 2][:, 0:1], diagq[:, i:i + 1],
                                    nmr[idx % 2][:, 0:1], op=ALU.add).then_inc(s_b, 1)
                    if i >= 1:
                        jdx = h * TT + (i - 1)
                        v.wait_ge(s_tqm, jdx + 1)
                        for m2 in range(2):
                            v.tensor_scalar(qpT[:, m2 * NT + (i - 1) * 128: m2 * NT + i * 128],
                                            ps_tqf[0:128, m2 * 128: m2 * 128 + 128], CEPS, None,
                                            op0=ALU.add).then_inc(s_tqe, 1)
                        jj = i - 1
                        if jj >= 3 and (jj - 3) % 4 == 0:
                            emit_out_ev(v, h, (jj - 3) // 4)
                jdx = h * TT + TT - 1
                v.wait_ge(s_tqm, jdx + 1)
                for m2 in range(2):
                    v.tensor_scalar(qpT[:, m2 * NT + (TT - 1) * 128: m2 * NT + TT * 128],
                                    ps_tqf[0:128, m2 * 128: m2 * 128 + 128], CEPS, None,
                                    op0=ALU.add).then_inc(s_tqe, 1)
                emit_out_ev(v, h, CH - 1)


def emit_a4_wo(c):
    nc = c.nc
    attn_inT, identf = c.t["attn_inT"], c.t["identf"]
    wo_in, bo_in, attn_s = c.dram["wo"], c.dram["bo"], c.dram["attn_s"]
    with ExitStack() as st, nc.Block() as blk:
        wo_sb = st.enter_context(nc.sbuf_tensor("a4_wo", [128, KT * D], BF16))
        bo_sb = st.enter_context(nc.sbuf_tensor("a4_bo", [128, KT], F32))
        of = [st.enter_context(nc.sbuf_tensor(f"a4_of{j}", [128, 512], F32)) for j in range(2)]
        arow = [st.enter_context(nc.sbuf_tensor(f"a4_ar{j}", [128, 4 * D], F32)) for j in range(2)]
        ps_mm = [st.enter_context(nc.psum_tensor(f"a4_pm{j}", [128, 512], F32)) for j in range(4)]
        ps_tr = [st.enter_context(nc.psum_tensor(f"a4_pt{j}", [128, 512], F32)) for j in range(4)]
        s_wld = c.sem("a4_wld")
        s_bld = c.sem("a4_bld")
        s_mm = c.sem("a4_mm")
        s_ev = c.sem("a4_ev")
        s_trm = c.sem("a4_trm")
        s_tre = c.sem("a4_tre")
        s_st = [c.sem("a4_st0"), c.sem("a4_st1")]

        @blk.gpsimd
        def _(g):
            g.dma_start(out=wo_sb[:, :].rearrange("p (kt o) -> p kt o", kt=KT),
                        in_=wo_in.ap().rearrange("(kt p) o -> p kt o", p=128)).then_inc(s_wld, 16)

        @blk.sync
        def _(s):
            s.dma_start(out=bo_sb[:, :], in_=bo_in.ap()).then_inc(s_bld, 16)
            for cch in range(CH):
                s.wait_ge(s_tre, 32 * cch + 32)
                s.dma_start(out=attn_s.ap()[cch * 512:(cch + 1) * 512, :].rearrange("(j p) d -> p j d", p=128),
                            in_=arow[cch % 2][:, :].rearrange("p (j d) -> p j d", j=4)).then_inc(s_st[cch % 2], 16)
            s.wait_ge(s_st[0], 16 * (CH // 2))
            s.wait_ge(s_st[1], 16 * (CH // 2))

        @blk.tensor
        def _(t):
            t.wait_ge(s_wld, 16)
            for cch in range(CH):
                for m in range(KT):
                    idx = cch * KT + m
                    if idx >= 4:
                        t.wait_ge(s_ev, idx - 3)
                    for kt in range(KT):
                        ins = t.matmul(ps_mm[idx % 4][:, :],
                                       wo_sb[:, kt * D + m * 128: kt * D + (m + 1) * 128],
                                       attn_inT[:, kt * NT + cch * 512: kt * NT + cch * 512 + 512],
                                       start=(kt == 0), stop=(kt == KT - 1))
                    ins.then_inc(s_mm, 1)
                    if idx >= 1:
                        jdx = idx - 1
                        t.wait_ge(s_ev, jdx + 1)
                        if jdx >= 1:
                            t.wait_ge(s_tre, 4 * jdx)
                        for j in range(4):
                            t.transpose(ps_tr[j][:, 0:128], of[jdx % 2][:, j * 128:(j + 1) * 128],
                                        identf[:, :]).then_inc(s_trm, 1)
            jdx = CH * KT - 1
            t.wait_ge(s_ev, jdx + 1)
            t.wait_ge(s_tre, 4 * jdx)
            for j in range(4):
                t.transpose(ps_tr[j][:, 0:128], of[jdx % 2][:, j * 128:(j + 1) * 128],
                            identf[:, :]).then_inc(s_trm, 1)

        @blk.vector
        def _(v):
            v.wait_ge(s_bld, 16)
            for cch in range(CH):
                for m in range(KT):
                    idx = cch * KT + m
                    v.wait_ge(s_mm, idx + 1)
                    if idx >= 2:
                        v.wait_ge(s_trm, 4 * (idx - 1))
                    v.tensor_scalar(of[idx % 2][:, :], ps_mm[idx % 4][:, :], bo_sb[:, m: m + 1], None,
                                    op0=ALU.add).then_inc(s_ev, 1)
                    if idx >= 1:
                        jdx = idx - 1
                        jc, jm = jdx // KT, jdx % KT
                        if jc >= 2:
                            v.wait_ge(s_st[jc % 2], 16 * (jc // 2))
                        for j in range(4):
                            v.wait_ge(s_trm, 4 * jdx + j + 1)
                            v.tensor_copy(arow[jc % 2][:, j * D + jm * 128: j * D + (jm + 1) * 128],
                                          ps_tr[j][:, 0:128]).then_inc(s_tre, 1)
            jdx = CH * KT - 1
            jc, jm = jdx // KT, jdx % KT
            for j in range(4):
                v.wait_ge(s_trm, 4 * jdx + j + 1)
                v.tensor_copy(arow[jc % 2][:, j * D + jm * 128: j * D + (jm + 1) * 128],
                              ps_tr[j][:, 0:128]).then_inc(s_tre, 1)
    c.free("attn_inT", "xnT", "ctx", "diagk", "stabc")


def emit_moe(c):
    nc = c.nc
    cwT = c.t["cwT"]
    identf = c.t["identf"]
    selE = c.t["selE"]
    xn_fm, w1e_in, b1e_in = c.dram["xn_fm"], c.dram["w1e"], c.dram["b1e"]
    w2e_in, b2e_in, moe_s = c.dram["w2e"], c.dram["b2e"], c.dram["moe_s"]
    HF = 2
    NTH = NT // HF
    CCH = NTH // 512
    HM = DFF // 128
    with ExitStack() as st, nc.Block() as blk:
        xnh = st.enter_context(nc.sbuf_tensor("mo_xn", [128, KT * NTH], BF16))
        w1_sb = st.enter_context(nc.sbuf_tensor("mo_w1", [128, KT * DFF], BF16))
        w2_sb = st.enter_context(nc.sbuf_tensor("mo_w2", [128, HM * D], BF16))
        h_sb = st.enter_context(nc.sbuf_tensor("mo_h", [128, HM * 512], BF16))
        cwbc = st.enter_context(nc.sbuf_tensor("mo_cw", [128, NTH], BF16))
        acc = st.enter_context(nc.sbuf_tensor("mo_acc", [128, KT * NTH], F32))
        tmp = [st.enter_context(nc.sbuf_tensor(f"mo_tmp{j}", [128, 512], F32)) for j in range(2)]
        arow = [st.enter_context(nc.sbuf_tensor(f"mo_ar{j}", [128, D], F32)) for j in range(2)]
        b1s = st.enter_context(nc.sbuf_tensor("mo_b1", [128, HM], F32))
        b2s = st.enter_context(nc.sbuf_tensor("mo_b2", [128, KT], F32))
        ps_h = [st.enter_context(nc.psum_tensor(f"mo_ph{j}", [128, 512], F32)) for j in range(4)]
        ps_y = [st.enter_context(nc.psum_tensor(f"mo_py{j}", [128, 512], F32)) for j in range(4)]
        s_xld = c.sem("mo_xld")
        s_bld = c.sem("mo_bld")
        s_w1 = c.sem("mo_w1")
        s_w2 = c.sem("mo_w2")
        s_cwm = c.sem("mo_cwm")
        s_cwe = c.sem("mo_cwe")
        s_hmm = c.sem("mo_hmm")
        s_hev = c.sem("mo_hev")
        s_ymm = c.sem("mo_ymm")
        s_tev = c.sem("mo_tev")
        s_acc = c.sem("mo_acc")
        s_trm = c.sem("mo_trm")
        s_tre = c.sem("mo_tre")
        s_mst = [c.sem("mo_st0"), c.sem("mo_st1")]

        @blk.sync
        def _(s):
            for hf in range(HF):
                for kt in range(KT):
                    if hf >= 1 and kt == 0:
                        s.wait_ge(s_hmm, 64 * E * hf)
                    s.dma_start(out=xnh[:, kt * NTH:(kt + 1) * NTH],
                                in_=xn_fm.ap()[kt * 128:(kt + 1) * 128, hf * NTH:(hf + 1) * NTH]).then_inc(s_xld, 16)
                for e in range(E):
                    ie = hf * E + e
                    if ie >= 1:
                        s.wait_ge(s_hev, 64 * ie)
                        s.wait_ge(s_tev, 32 * ie)
                    s.dma_start(out=b1s[:, :], in_=b1e_in.ap()[e]).then_inc(s_bld, 16)
                    s.dma_start(out=b2s[:, :], in_=b2e_in.ap()[e]).then_inc(s_bld, 16)
                for i in range(E):
                    gi = hf * E + i
                    s.wait_ge(s_tre, 8 * (gi + 1))
                    s.dma_start(out=moe_s.ap()[hf * NTH + i * 128: hf * NTH + (i + 1) * 128, :],
                                in_=arow[gi % 2][:, :]).then_inc(s_mst[gi % 2], 16)
            s.wait_ge(s_mst[0], 16 * E)
            s.wait_ge(s_mst[1], 16 * E)

        @blk.gpsimd
        def _(g):
            nacc = 0
            for hf in range(HF):
                if hf >= 1:
                    g.wait_ge(s_trm, 128 * hf)
                g.memset(acc[:, :], 0.0)
                for e in range(E):
                    ie = hf * E + e
                    if ie >= 1:
                        g.wait_ge(s_hmm, 64 * ie)
                    g.dma_start(out=w1_sb[:, :].rearrange("p (kt o) -> p kt o", kt=KT),
                                in_=w1e_in.ap()[e].rearrange("(kt p) o -> p kt o", p=128)).then_inc(s_w1, 16)
                    if ie >= 1:
                        g.wait_ge(s_ymm, 32 * ie)
                    g.dma_start(out=w2_sb[:, :].rearrange("p (kt o) -> p kt o", kt=HM),
                                in_=w2e_in.ap()[e].rearrange("(kt p) o -> p kt o", p=128)).then_inc(s_w2, 16)
                    for cc in range(CCH):
                        for ym in range(KT):
                            nacc += 1
                            g.wait_ge(s_tev, nacc)
                            g.tensor_tensor(acc[:, ym * NTH + cc * 512: ym * NTH + cc * 512 + 512],
                                            acc[:, ym * NTH + cc * 512: ym * NTH + cc * 512 + 512],
                                            tmp[nacc % 2][:, :], op=ALU.add).then_inc(s_acc, 1)

        @blk.tensor
        def _(t):
            ngh = 0
            ngy = 0
            for hf in range(HF):
                for e in range(E):
                    ie = hf * E + e
                    t.wait_ge(s_tev, 32 * ie)
                    if hf >= 1 and e == 0:
                        t.wait_ge(s_trm, 128 * hf)
                    for cc in range(CCH):
                        t.wait_ge(s_cwe, 4 * ie + cc)
                        t.matmul(ps_y[0][:, 0:512], selE[:, e * 128:(e + 1) * 128],
                                 cwT[:, hf * NTH + cc * 512: hf * NTH + cc * 512 + 512],
                                 start=True, stop=True).then_inc(s_cwm, 1)
                    t.wait_ge(s_w1, 16 * (ie + 1))
                    t.wait_ge(s_xld, 16 * KT * (hf + 1))
                    for cc in range(CCH):
                        for hm in range(HM):
                            ngh += 1
                            if ngh > 4:
                                t.wait_ge(s_hev, ngh - 4)
                            for kt in range(KT):
                                ins = t.matmul(ps_h[(ngh - 1) % 4][:, :],
                                               w1_sb[:, kt * DFF + hm * 128: kt * DFF + (hm + 1) * 128],
                                               xnh[:, kt * NTH + cc * 512: kt * NTH + cc * 512 + 512],
                                               start=(kt == 0), stop=(kt == KT - 1))
                            ins.then_inc(s_hmm, 1)
                        if cc == 0:
                            t.wait_ge(s_w2, 16 * (ie + 1))
                        t.wait_ge(s_hev, 16 * (4 * ie + cc + 1))
                        for ym in range(KT):
                            ngy += 1
                            if ngy > 4:
                                t.wait_ge(s_tev, ngy - 4)
                            for kt2 in range(HM):
                                ins = t.matmul(ps_y[(ngy - 1) % 4][:, :],
                                               w2_sb[:, kt2 * D + ym * 128: kt2 * D + (ym + 1) * 128],
                                               h_sb[:, kt2 * 512:(kt2 + 1) * 512],
                                               start=(kt2 == 0), stop=(kt2 == HM - 1))
                            ins.then_inc(s_ymm, 1)
                t.wait_ge(s_acc, 512 * (hf + 1))
                for i in range(E):
                    gi = hf * E + i
                    for m in range(KT):
                        tr = gi * KT + m
                        if tr >= 2:
                            t.wait_ge(s_tre, tr - 1)
                        t.transpose(ps_h[tr % 2][:, 0:128],
                                    acc[:, m * NTH + i * 128: m * NTH + (i + 1) * 128],
                                    identf[:, :]).then_inc(s_trm, 1)

        @blk.scalar
        def _(sc):
            for hf in range(HF):
                for e in range(E):
                    ie = hf * E + e
                    sc.wait_ge(s_bld, 32 * ie + 16)
                    for cc in range(CCH):
                        for hm in range(HM):
                            gh = ie * 64 + cc * HM + hm
                            sc.wait_ge(s_hmm, gh + 1)
                            if cc >= 1 and hm == 0:
                                sc.wait_ge(s_ymm, ie * 32 + cc * KT)
                            sc.activation(h_sb[:, hm * 512:(hm + 1) * 512], ps_h[gh % 4][:, :],
                                          AF.Relu, bias=b1s[:, hm: hm + 1]).then_inc(s_hev, 1)

        @blk.vector
        def _(v):
            for hf in range(HF):
                for e in range(E):
                    ie = hf * E + e
                    for cc in range(CCH):
                        v.wait_ge(s_cwm, 4 * ie + cc + 1)
                        v.tensor_copy(cwbc[:, cc * 512:(cc + 1) * 512], ps_y[0][:, 0:512]).then_inc(s_cwe, 1)
                    v.wait_ge(s_bld, 32 * ie + 32)
                    for cc in range(CCH):
                        for ym in range(KT):
                            gy = ie * 32 + cc * KT + ym
                            v.wait_ge(s_ymm, gy + 1)
                            if gy >= 2:
                                v.wait_ge(s_acc, gy - 1)
                            v.scalar_tensor_tensor(tmp[(gy + 1) % 2][:, :], ps_y[gy % 4][:, :],
                                                   b2s[:, ym: ym + 1],
                                                   cwbc[:, cc * 512:(cc + 1) * 512],
                                                   op0=ALU.add, op1=ALU.mult).then_inc(s_tev, 1)
                for i in range(E):
                    gi = hf * E + i
                    if gi >= 2:
                        v.wait_ge(s_mst[gi % 2], 16 * (gi // 2))
                    for m in range(KT):
                        tr = gi * KT + m
                        v.wait_ge(s_trm, tr + 1)
                        v.tensor_copy(arow[gi % 2][:, m * 128:(m + 1) * 128],
                                      ps_h[tr % 2][:, 0:128]).then_inc(s_tre, 1)
    c.free("cwT") if False else None


def emit_combine_ln1(c):
    nc = c.nc
    identb, cst = c.t["identb"], c.t["cst"]
    x_in, attn_s, moe_s, x1_s = c.dram["x"], c.dram["attn_s"], c.dram["moe_s"], c.dram["x1_s"]
    c.free("cwT")
    G_bf = c.sbuf("G", [128, KT * D], BF16)
    A_bf = c.sbuf("A", [128, KT * D], BF16)
    xn1T = c.sbuf("xn1T", [128, KT * NT], BF16)
    xn1_tok = c.sbuf("xn1_tok", [128, TT * D], BF16)
    with ExitStack() as st, nc.Block() as blk:
        xt = [st.enter_context(nc.sbuf_tensor(f"cb_x{j}", [128, D], F32)) for j in range(2)]
        at = [st.enter_context(nc.sbuf_tensor(f"cb_a{j}", [128, D], F32)) for j in range(2)]
        mt = [st.enter_context(nc.sbuf_tensor(f"cb_m{j}", [128, D], F32)) for j in range(2)]
        x1t = [st.enter_context(nc.sbuf_tensor(f"cb_o{j}", [128, D], F32)) for j in range(2)]
        stat = [st.enter_context(nc.sbuf_tensor(f"cb_s{j}", [128, 8], F32)) for j in range(2)]
        sq = st.enter_context(nc.sbuf_tensor("cb_sq", [128, D], F32))
        ps = [st.enter_context(nc.psum_tensor(f"cb_ps{j}", [128, 1024], BF16)) for j in range(2)]
        s_ld = [c.sem("cb_ld0"), c.sem("cb_ld1")]
        s_sv = c.sem("cb_sv")
        s_x1 = c.sem("cb_x1")
        s_st = c.sem("cb_st")
        s_n = c.sem("cb_n")
        s_tr = c.sem("cb_tr")
        s_ev = c.sem("cb_ev")
        s_xs = [c.sem("cb_xs0"), c.sem("cb_xs1")]

        @blk.sync
        def _(s):
            for i in range(TT):
                if i >= 2:
                    s.wait_ge(s_n, i - 1)
                b = i % 2
                s.dma_start(out=xt[b][:, :], in_=x_in.ap()[i * 128:(i + 1) * 128, :]).then_inc(s_ld[b], 16)
                s.dma_start(out=at[b][:, :], in_=attn_s.ap()[i * 128:(i + 1) * 128, :]).then_inc(s_ld[b], 16)
                s.dma_start(out=mt[b][:, :], in_=moe_s.ap()[i * 128:(i + 1) * 128, :]).then_inc(s_ld[b], 16)

        @blk.gpsimd
        def _(g):
            for i in range(TT):
                g.wait_ge(s_x1, i + 1)
                g.dma_start(out=x1_s.ap()[i * 128:(i + 1) * 128, :], in_=x1t[i % 2][:, :]).then_inc(s_xs[i % 2], 16)
            g.wait_ge(s_xs[0], 16 * (TT // 2))
            g.wait_ge(s_xs[1], 16 * (TT // 2))

        @blk.vector
        def _(v):
            nsv = 0
            for i in range(TT):
                b = i % 2
                v.wait_ge(s_ld[b], 48 * (i // 2 + 1))
                if i >= 2:
                    v.wait_ge(s_xs[b], 16 * (i // 2))
                    v.wait_ge(s_tr, 8 * (i - 1))
                v.tensor_tensor(x1t[b][:, :], xt[b][:, :], at[b][:, :], op=ALU.add)
                v.tensor_tensor(x1t[b][:, :], x1t[b][:, :], mt[b][:, :], op=ALU.add).then_inc(s_x1, 1)
                v.reduce_sum(stat[b][:, 0:1], x1t[b][:, :], axis=AX.X).then_inc(s_sv, 1)
                nsv += 1
                v.wait_ge(s_st, 3 * i + 1)
                v.wait_ge(s_sv, nsv)
                v.tensor_scalar(stat[b][:, 2:3], stat[b][:, 0:1], 1.0 / D, None, op0=ALU.mult).then_inc(s_sv, 1)
                nsv += 1
                v.wait_ge(s_sv, nsv)
                v.tensor_tensor(stat[b][:, 3:4], stat[b][:, 2:3], stat[b][:, 2:3], op=ALU.mult).then_inc(s_sv, 1)
                nsv += 1
                v.wait_ge(s_sv, nsv)
                v.scalar_tensor_tensor(stat[b][:, 3:4], stat[b][:, 1:2], 1.0 / D, stat[b][:, 3:4],
                                       op0=ALU.mult, op1=ALU.subtract).then_inc(s_st, 1)
                v.wait_ge(s_st, 3 * i + 3)
                v.scalar_tensor_tensor(stat[b][:, 6:7], stat[b][:, 2:3], -1.0, stat[b][:, 5:6],
                                       op0=ALU.mult, op1=ALU.mult)
                v.tensor_scalar(xn1_tok[:, i * D:(i + 1) * D], x1t[b][:, :], stat[b][:, 5:6], stat[b][:, 6:7],
                                op0=ALU.mult, op1=ALU.add).then_inc(s_n, 1)
                if i >= 1:
                    v.wait_ge(s_tr, 8 * i)
                    pb = ps[(i - 1) % 2]
                    for kt in range(KT):
                        v.tensor_copy(xn1T[:, kt * NT + (i - 1) * 128: kt * NT + i * 128],
                                      pb[:, kt * 128:(kt + 1) * 128]).then_inc(s_ev, 1)
            v.wait_ge(s_tr, 8 * TT)
            pb = ps[(TT - 1) % 2]
            for kt in range(KT):
                v.tensor_copy(xn1T[:, kt * NT + (TT - 1) * 128: kt * NT + TT * 128],
                              pb[:, kt * 128:(kt + 1) * 128]).then_inc(s_ev, 1)

        @blk.scalar
        def _(sc):
            for i in range(TT):
                b = i % 2
                sc.wait_ge(s_x1, i + 1)
                if i >= 2:
                    sc.wait_ge(s_n, i - 1)
                sc.activation(sq[:, :], x1t[b][:, :], AF.Square, accum_out=stat[b][:, 1:2]).then_inc(s_st, 1)
                sc.wait_ge(s_st, 3 * i + 2)
                sc.activation(stat[b][:, 4:5], stat[b][:, 3:4], AF.Ln, bias=cst[:, 0:1])
                sc.activation(stat[b][:, 5:6], stat[b][:, 4:5], AF.Exp, scale=cst[:, 1:2]).then_inc(s_st, 1)

        @blk.tensor
        def _(t):
            for i in range(TT):
                t.wait_ge(s_n, i + 1)
                if i >= 2:
                    t.wait_ge(s_ev, 8 * (i - 1))
                for kt in range(KT):
                    t.transpose(ps[i % 2][:, kt * 128:(kt + 1) * 128],
                                xn1_tok[:, i * D + kt * 128: i * D + (kt + 1) * 128],
                                identb[:, :]).then_inc(s_tr, 1)


def emit_gram(c):
    nc = c.nc
    xn1_tok, G_bf = c.t["xn1_tok"], c.t["G"]
    with ExitStack() as st, nc.Block() as blk:
        ps = [st.enter_context(nc.psum_tensor(f"gr_ps{j}", [128, 512], F32)) for j in range(4)]
        s_mm = c.sem("gr_mm")
        s_ev = c.sem("gr_ev")

        @blk.tensor
        def _(t):
            for it in range(KT):
                for jc in range(2):
                    idx = it * 2 + jc
                    if idx >= 4:
                        t.wait_ge(s_ev, idx - 3)
                    for tt in range(TT):
                        ins = t.matmul(ps[idx % 4][:, :],
                                       xn1_tok[:, tt * D + it * 128: tt * D + (it + 1) * 128],
                                       xn1_tok[:, tt * D + jc * 512: tt * D + jc * 512 + 512],
                                       start=(tt == 0), stop=(tt == TT - 1))
                    ins.then_inc(s_mm, 1)

        @blk.vector
        def _(v):
            for it in range(KT):
                for jc in range(2):
                    idx = it * 2 + jc
                    v.wait_ge(s_mm, idx + 1)
                    v.tensor_copy(G_bf[:, it * D + jc * 512: it * D + jc * 512 + 512],
                                  ps[idx % 4][:, :]).then_inc(s_ev, 1)
    c.free("xn1_tok")


def emit_amat(c):
    nc = c.nc
    G_bf, A_bf = c.t["G"], c.t["A"]
    q1_in = c.dram["q1"]
    with ExitStack() as st, nc.Block() as blk:
        q1_sb = st.enter_context(nc.sbuf_tensor("am_q1", [128, KT * D], BF16))
        ps = [st.enter_context(nc.psum_tensor(f"am_ps{j}", [128, 512], F32)) for j in range(4)]
        s_ld = c.sem("am_ld")
        s_mm = c.sem("am_mm")
        s_ev = c.sem("am_ev")

        @blk.gpsimd
        def _(g):
            g.dma_start(out=q1_sb[:, :].rearrange("p (kt o) -> p kt o", kt=KT),
                        in_=q1_in.ap().rearrange("(kt p) o -> p kt o", p=128)).then_inc(s_ld, 16)

        @blk.tensor
        def _(t):
            t.wait_ge(s_ld, 16)
            for m in range(KT):
                for jc in range(2):
                    idx = m * 2 + jc
                    if idx >= 4:
                        t.wait_ge(s_ev, idx - 3)
                    for kt in range(KT):
                        ins = t.matmul(ps[idx % 4][:, :],
                                       q1_sb[:, kt * D + m * 128: kt * D + (m + 1) * 128],
                                       G_bf[:, kt * D + jc * 512: kt * D + jc * 512 + 512],
                                       start=(kt == 0), stop=(kt == KT - 1))
                    ins.then_inc(s_mm, 1)

        @blk.vector
        def _(v):
            for m in range(KT):
                for jc in range(2):
                    idx = m * 2 + jc
                    v.wait_ge(s_mm, idx + 1)
                    v.tensor_copy(A_bf[:, m * D + jc * 512: m * D + jc * 512 + 512],
                                  ps[idx % 4][:, :]).then_inc(s_ev, 1)


def emit_attn1(c):
    nc = c.nc
    A_bf, xn1T, identf = c.t["A"], c.t["xn1T"], c.t["identf"]
    attn1_s = c.dram["attn1_s"]
    with ExitStack() as st, nc.Block() as blk:
        of = [st.enter_context(nc.sbuf_tensor(f"b2_of{j}", [128, 512], F32)) for j in range(2)]
        arow = [st.enter_context(nc.sbuf_tensor(f"b2_ar{j}", [128, 4 * D], F32)) for j in range(2)]
        ps_mm = [st.enter_context(nc.psum_tensor(f"b2_pm{j}", [128, 512], F32)) for j in range(4)]
        ps_tr = [st.enter_context(nc.psum_tensor(f"b2_pt{j}", [128, 512], F32)) for j in range(4)]
        s_mm = c.sem("b2_mm")
        s_ev = c.sem("b2_ev")
        s_trm = c.sem("b2_trm")
        s_tre = c.sem("b2_tre")
        s_st = [c.sem("b2_st0"), c.sem("b2_st1")]

        @blk.sync
        def _(s):
            for cch in range(CH):
                s.wait_ge(s_tre, 32 * cch + 32)
                s.dma_start(out=attn1_s.ap()[cch * 512:(cch + 1) * 512, :].rearrange("(j p) d -> p j d", p=128),
                            in_=arow[cch % 2][:, :].rearrange("p (j d) -> p j d", j=4)).then_inc(s_st[cch % 2], 16)
            s.wait_ge(s_st[0], 16 * (CH // 2))
            s.wait_ge(s_st[1], 16 * (CH // 2))

        @blk.tensor
        def _(t):
            for cch in range(CH):
                for m in range(KT):
                    idx = cch * KT + m
                    if idx >= 4:
                        t.wait_ge(s_ev, idx - 3)
                    for kt in range(KT):
                        ins = t.matmul(ps_mm[idx % 4][:, :],
                                       A_bf[:, kt * D + m * 128: kt * D + (m + 1) * 128],
                                       xn1T[:, kt * NT + cch * 512: kt * NT + cch * 512 + 512],
                                       start=(kt == 0), stop=(kt == KT - 1))
                    ins.then_inc(s_mm, 1)
                    if idx >= 1:
                        jdx = idx - 1
                        t.wait_ge(s_ev, jdx + 1)
                        if jdx >= 1:
                            t.wait_ge(s_tre, 4 * jdx)
                        for j in range(4):
                            t.transpose(ps_tr[j][:, 0:128], of[jdx % 2][:, j * 128:(j + 1) * 128],
                                        identf[:, :]).then_inc(s_trm, 1)
            jdx = CH * KT - 1
            t.wait_ge(s_ev, jdx + 1)
            t.wait_ge(s_tre, 4 * jdx)
            for j in range(4):
                t.transpose(ps_tr[j][:, 0:128], of[jdx % 2][:, j * 128:(j + 1) * 128],
                            identf[:, :]).then_inc(s_trm, 1)

        @blk.vector
        def _(v):
            for cch in range(CH):
                for m in range(KT):
                    idx = cch * KT + m
                    v.wait_ge(s_mm, idx + 1)
                    if idx >= 2:
                        v.wait_ge(s_trm, 4 * (idx - 1))
                    v.tensor_copy(of[idx % 2][:, :], ps_mm[idx % 4][:, :]).then_inc(s_ev, 1)
                    if idx >= 1:
                        jdx = idx - 1
                        jc, jm = jdx // KT, jdx % KT
                        if jc >= 2:
                            v.wait_ge(s_st[jc % 2], 16 * (jc // 2))
                        for j in range(4):
                            v.wait_ge(s_trm, 4 * jdx + j + 1)
                            v.tensor_copy(arow[jc % 2][:, j * D + jm * 128: j * D + (jm + 1) * 128],
                                          ps_tr[j][:, 0:128]).then_inc(s_tre, 1)
            jdx = CH * KT - 1
            jc, jm = jdx // KT, jdx % KT
            for j in range(4):
                v.wait_ge(s_trm, 4 * jdx + j + 1)
                v.tensor_copy(arow[jc % 2][:, j * D + jm * 128: j * D + (jm + 1) * 128],
                              ps_tr[j][:, 0:128]).then_inc(s_tre, 1)


def emit_f1(c):
    nc = c.nc
    xn1T = c.t["xn1T"]
    w1T_in, w2T_in, s1_in, s2_in, h_s = (c.dram["w1T"], c.dram["w2T"],
                                         c.dram["s1"], c.dram["s2"], c.dram["h_s"])
    with ExitStack() as st, nc.Block() as blk:
        w1m = [st.enter_context(nc.sbuf_tensor(f"f1_w1{j}", [128, KT * 128], BF16)) for j in range(2)]
        w2m = [st.enter_context(nc.sbuf_tensor(f"f1_w2{j}", [128, KT * 128], BF16)) for j in range(2)]
        s1s = st.enter_context(nc.sbuf_tensor("f1_s1", [128, KH], F32))
        s2s = st.enter_context(nc.sbuf_tensor("f1_s2", [128, KH], F32))
        t1s = [st.enter_context(nc.sbuf_tensor(f"f1_t1{j}", [128, 512], F32)) for j in range(2)]
        hrow = [st.enter_context(nc.sbuf_tensor(f"f1_hr{j}", [128, NT], BF16)) for j in range(2)]
        ps_1 = [st.enter_context(nc.psum_tensor(f"f1_p1{j}", [128, 512], F32)) for j in range(2)]
        ps_2 = [st.enter_context(nc.psum_tensor(f"f1_p2{j}", [128, 512], F32)) for j in range(2)]
        s_sld = c.sem("f1_sld")
        s_wld = c.sem("f1_wld")
        s_mm1 = c.sem("f1_mm1")
        s_mm2 = c.sem("f1_mm2")
        s_t1 = c.sem("f1_t1")
        s_h = c.sem("f1_h")
        s_hs = [c.sem("f1_hs0"), c.sem("f1_hs1")]

        @blk.sync
        def _(s):
            s.dma_start(out=s1s[:, :], in_=s1_in.ap()).then_inc(s_sld, 16)
            s.dma_start(out=s2s[:, :], in_=s2_in.ap()).then_inc(s_sld, 16)
            for m in range(KH):
                s.wait_ge(s_h, m * CH + CH)
                s.dma_start(out=h_s.ap()[m * 128:(m + 1) * 128, :], in_=hrow[m % 2][:, :]).then_inc(s_hs[m % 2], 16)
            s.wait_ge(s_hs[0], 16 * (KH // 2))
            s.wait_ge(s_hs[1], 16 * (KH // 2))

        @blk.gpsimd
        def _(g):
            for m in range(KH):
                if m >= 2:
                    g.wait_ge(s_mm1, (m - 1) * CH)
                    g.wait_ge(s_mm2, (m - 1) * CH)
                g.dma_start(out=w1m[m % 2][:, :].rearrange("p (kt o) -> p kt o", kt=KT),
                            in_=w1T_in.ap()[:, m * 128:(m + 1) * 128].rearrange("(kt p) o -> p kt o", p=128)).then_inc(s_wld, 16)
                g.dma_start(out=w2m[m % 2][:, :].rearrange("p (kt o) -> p kt o", kt=KT),
                            in_=w2T_in.ap()[:, m * 128:(m + 1) * 128].rearrange("(kt p) o -> p kt o", p=128)).then_inc(s_wld, 16)

        @blk.tensor
        def _(t):
            for m in range(KH):
                t.wait_ge(s_wld, 32 * (m + 1))
                for ch in range(CH):
                    idx = m * CH + ch
                    if idx >= 2:
                        t.wait_ge(s_t1, idx - 1)
                        t.wait_ge(s_h, idx - 1)
                    for kt in range(KT):
                        ins = t.matmul(ps_1[idx % 2][:, :],
                                       w1m[m % 2][:, kt * 128:(kt + 1) * 128],
                                       xn1T[:, kt * NT + ch * 512: kt * NT + ch * 512 + 512],
                                       start=(kt == 0), stop=(kt == KT - 1))
                    ins.then_inc(s_mm1, 1)
                    for kt in range(KT):
                        ins = t.matmul(ps_2[idx % 2][:, :],
                                       w2m[m % 2][:, kt * 128:(kt + 1) * 128],
                                       xn1T[:, kt * NT + ch * 512: kt * NT + ch * 512 + 512],
                                       start=(kt == 0), stop=(kt == KT - 1))
                    ins.then_inc(s_mm2, 1)

        @blk.scalar
        def _(sc):
            sc.wait_ge(s_sld, 16)
            for m in range(KH):
                for ch in range(CH):
                    idx = m * CH + ch
                    sc.wait_ge(s_mm1, idx + 1)
                    if idx >= 2:
                        sc.wait_ge(s_h, idx - 1)
                    sc.activation(t1s[idx % 2][:, :], ps_1[idx % 2][:, :], AF.Silu,
                                  scale=s1s[:, m: m + 1]).then_inc(s_t1, 1)

        @blk.vector
        def _(v):
            v.wait_ge(s_sld, 32)
            for m in range(KH):
                if m >= 2:
                    v.wait_ge(s_hs[m % 2], 16 * (m // 2))
                for ch in range(CH):
                    idx = m * CH + ch
                    v.wait_ge(s_mm2, idx + 1)
                    v.wait_ge(s_t1, idx + 1)
                    v.scalar_tensor_tensor(hrow[m % 2][:, ch * 512:(ch + 1) * 512],
                                           ps_2[idx % 2][:, :], s2s[:, m: m + 1], t1s[idx % 2][:, :],
                                           op0=ALU.mult, op1=ALU.mult).then_inc(s_h, 1)
    c.free("xn1T", "A", "G")


def emit_f23(c, src_name, w_name, s_name, dst_name, tag):
    nc = c.nc
    src_s, wT_in, ss_in, dst_s = (c.dram[src_name], c.dram[w_name],
                                  c.dram[s_name], c.dram[dst_name])
    HF = 2
    NTH = NT // HF
    CCH = NTH // 512
    with ExitStack() as st, nc.Block() as blk:
        srch = st.enter_context(nc.sbuf_tensor(f"{tag}_src", [128, KH * NTH], BF16))
        wm = [st.enter_context(nc.sbuf_tensor(f"{tag}_w{j}", [128, KH * 128], BF16)) for j in range(2)]
        sss = st.enter_context(nc.sbuf_tensor(f"{tag}_s", [128, KH], F32))
        grow = [st.enter_context(nc.sbuf_tensor(f"{tag}_g{j}", [128, NTH], BF16)) for j in range(2)]
        ps = [st.enter_context(nc.psum_tensor(f"{tag}_p{j}", [128, 512], F32)) for j in range(4)]
        s_sld = c.sem(f"{tag}_sld")
        s_xld = c.sem(f"{tag}_xld")
        s_wld = c.sem(f"{tag}_wld")
        s_mm = c.sem(f"{tag}_mm")
        s_g = c.sem(f"{tag}_g")
        s_gs = [c.sem(f"{tag}_gs0"), c.sem(f"{tag}_gs1")]

        @blk.sync
        def _(s):
            s.dma_start(out=sss[:, :], in_=ss_in.ap()).then_inc(s_sld, 16)
            for hf in range(HF):
                for kt in range(KH):
                    if hf >= 1 and kt == 0:
                        s.wait_ge(s_mm, KH * CCH * hf)
                    s.dma_start(out=srch[:, kt * NTH:(kt + 1) * NTH],
                                in_=src_s.ap()[kt * 128:(kt + 1) * 128, hf * NTH:(hf + 1) * NTH]).then_inc(s_xld, 16)
                for m in range(KH):
                    gm = hf * KH + m
                    s.wait_ge(s_g, gm * CCH + CCH)
                    s.dma_start(out=dst_s.ap()[m * 128:(m + 1) * 128, hf * NTH:(hf + 1) * NTH],
                                in_=grow[gm % 2][:, :]).then_inc(s_gs[gm % 2], 16)
            s.wait_ge(s_gs[0], 16 * KH)
            s.wait_ge(s_gs[1], 16 * KH)

        @blk.gpsimd
        def _(g):
            for hf in range(HF):
                for m in range(KH):
                    gm = hf * KH + m
                    if gm >= 2:
                        g.wait_ge(s_mm, (gm - 1) * CCH)
                    g.dma_start(out=wm[gm % 2][:, :].rearrange("p (kt o) -> p kt o", kt=KH),
                                in_=wT_in.ap()[:, m * 128:(m + 1) * 128].rearrange("(kt p) o -> p kt o", p=128)).then_inc(s_wld, 16)

        @blk.tensor
        def _(t):
            for hf in range(HF):
                t.wait_ge(s_xld, 16 * KH * (hf + 1))
                for m in range(KH):
                    gm = hf * KH + m
                    t.wait_ge(s_wld, 16 * (gm + 1))
                    for cc in range(CCH):
                        idx = gm * CCH + cc
                        if idx >= 4:
                            t.wait_ge(s_g, idx - 3)
                        for kt in range(KH):
                            ins = t.matmul(ps[idx % 4][:, :],
                                           wm[gm % 2][:, kt * 128:(kt + 1) * 128],
                                           srch[:, kt * NTH + cc * 512: kt * NTH + cc * 512 + 512],
                                           start=(kt == 0), stop=(kt == KH - 1))
                        ins.then_inc(s_mm, 1)

        @blk.scalar
        def _(sc):
            sc.wait_ge(s_sld, 16)
            for hf in range(HF):
                for m in range(KH):
                    gm = hf * KH + m
                    if gm >= 2:
                        sc.wait_ge(s_gs[gm % 2], 16 * (gm // 2))
                    for cc in range(CCH):
                        idx = gm * CCH + cc
                        sc.wait_ge(s_mm, idx + 1)
                        sc.activation(grow[gm % 2][:, cc * 512:(cc + 1) * 512], ps[idx % 4][:, :],
                                      AF.Gelu, scale=sss[:, m: m + 1]).then_inc(s_g, 1)


def emit_f4(c):
    nc = c.nc
    identf = c.t["identf"]
    g2_s, w3T_in, s3_in = c.dram["g2_s"], c.dram["w3T"], c.dram["s3"]
    x1_s, attn1_s, out_t = c.dram["x1_s"], c.dram["attn1_s"], c.dram["out"]
    QF = 4
    NTQ = NT // QF
    TQ = NTQ // 128
    with ExitStack() as st, nc.Block() as blk:
        g2q = st.enter_context(nc.sbuf_tensor("f4_g2", [128, KH * NTQ], BF16))
        w3sb = st.enter_context(nc.sbuf_tensor("f4_w3", [128, KH * D], BF16))
        s3s = st.enter_context(nc.sbuf_tensor("f4_s3", [128, KT], F32))
        acc4 = st.enter_context(nc.sbuf_tensor("f4_acc", [128, KT * NTQ], F32))
        x1t = [st.enter_context(nc.sbuf_tensor(f"f4_x{j}", [128, D], F32)) for j in range(2)]
        a1t = [st.enter_context(nc.sbuf_tensor(f"f4_a{j}", [128, D], F32)) for j in range(2)]
        orow = [st.enter_context(nc.sbuf_tensor(f"f4_o{j}", [128, D], F32)) for j in range(2)]
        ps = [st.enter_context(nc.psum_tensor(f"f4_p{j}", [128, 512], F32)) for j in range(4)]
        ps_tr = [st.enter_context(nc.psum_tensor(f"f4_pt{j}", [128, 512], F32)) for j in range(2)]
        s_sld = c.sem("f4_sld")
        s_xld = c.sem("f4_xld")
        s_rld = [c.sem("f4_rl0"), c.sem("f4_rl1")]
        s_wld = c.sem("f4_wld")
        s_mm = c.sem("f4_mm")
        s_ev = c.sem("f4_ev")
        s_trm = c.sem("f4_trm")
        s_tre = c.sem("f4_tre")
        s_or = c.sem("f4_or")
        s_ost = [c.sem("f4_os0"), c.sem("f4_os1")]

        @blk.sync
        def _(s):
            s.dma_start(out=s3s[:, :], in_=s3_in.ap()).then_inc(s_sld, 16)
            for qt in range(QF):
                for kt in range(KH):
                    if qt >= 1 and kt == 0:
                        s.wait_ge(s_mm, 16 * qt)
                    s.dma_start(out=g2q[:, kt * NTQ:(kt + 1) * NTQ],
                                in_=g2_s.ap()[kt * 128:(kt + 1) * 128, qt * NTQ:(qt + 1) * NTQ]).then_inc(s_xld, 16)
                for i in range(TQ):
                    gi = qt * TQ + i
                    if gi >= 2:
                        s.wait_ge(s_or, gi - 1)
                    s.dma_start(out=x1t[gi % 2][:, :],
                                in_=x1_s.ap()[gi * 128:(gi + 1) * 128, :]).then_inc(s_rld[gi % 2], 16)
                    s.dma_start(out=a1t[gi % 2][:, :],
                                in_=attn1_s.ap()[gi * 128:(gi + 1) * 128, :]).then_inc(s_rld[gi % 2], 16)

        @blk.gpsimd
        def _(g):
            g.dma_start(out=w3sb[:, :].rearrange("p (kt o) -> p kt o", kt=KH),
                        in_=w3T_in.ap().rearrange("(kt p) o -> p kt o", p=128)).then_inc(s_wld, 16)
            for qt in range(QF):
                for i in range(TQ):
                    gi = qt * TQ + i
                    g.wait_ge(s_or, gi + 1)
                    g.dma_start(out=out_t.ap()[gi * 128:(gi + 1) * 128, :],
                                in_=orow[gi % 2][:, :]).then_inc(s_ost[gi % 2], 16)
            g.wait_ge(s_ost[0], 16 * (NT // 256))
            g.wait_ge(s_ost[1], 16 * (NT // 256))

        @blk.tensor
        def _(t):
            t.wait_ge(s_wld, 16)
            for qt in range(QF):
                t.wait_ge(s_xld, 16 * KH * (qt + 1))
                for m in range(KT):
                    for ccc in range(2):
                        idx = qt * 16 + m * 2 + ccc
                        if idx >= 4:
                            t.wait_ge(s_ev, idx - 3)
                        for kt in range(KH):
                            ins = t.matmul(ps[idx % 4][:, :],
                                           w3sb[:, kt * D + m * 128: kt * D + (m + 1) * 128],
                                           g2q[:, kt * NTQ + ccc * 512: kt * NTQ + ccc * 512 + 512],
                                           start=(kt == 0), stop=(kt == KH - 1))
                        ins.then_inc(s_mm, 1)
                t.wait_ge(s_ev, 16 * (qt + 1))
                for i in range(TQ):
                    gi = qt * TQ + i
                    for m in range(KT):
                        tr = gi * KT + m
                        if tr >= 2:
                            t.wait_ge(s_tre, tr - 1)
                        t.transpose(ps_tr[tr % 2][:, 0:128],
                                    acc4[:, m * NTQ + i * 128: m * NTQ + (i + 1) * 128],
                                    identf[:, :]).then_inc(s_trm, 1)

        @blk.vector
        def _(v):
            v.wait_ge(s_sld, 16)
            for qt in range(QF):
                for m in range(KT):
                    for ccc in range(2):
                        idx = qt * 16 + m * 2 + ccc
                        if qt >= 1 and m == 0 and ccc == 0:
                            v.wait_ge(s_trm, 64 * qt)
                        v.wait_ge(s_mm, idx + 1)
                        v.tensor_scalar(acc4[:, m * NTQ + ccc * 512: m * NTQ + ccc * 512 + 512],
                                        ps[idx % 4][:, :], s3s[:, m: m + 1], None,
                                        op0=ALU.mult).then_inc(s_ev, 1)
                for i in range(TQ):
                    gi = qt * TQ + i
                    if gi >= 2:
                        v.wait_ge(s_ost[gi % 2], 16 * (gi // 2))
                    for m in range(KT):
                        tr = gi * KT + m
                        v.wait_ge(s_trm, tr + 1)
                        v.tensor_copy(orow[gi % 2][:, m * 128:(m + 1) * 128],
                                      ps_tr[tr % 2][:, 0:128]).then_inc(s_tre, 1)
                    v.wait_ge(s_rld[gi % 2], 32 * (gi // 2 + 1))
                    v.tensor_tensor(orow[gi % 2][:, :], orow[gi % 2][:, :], x1t[gi % 2][:, :], op=ALU.add)
                    v.tensor_tensor(orow[gi % 2][:, :], orow[gi % 2][:, :], a1t[gi % 2][:, :], op=ALU.add).then_inc(s_or, 1)


def make_in_maps(inputs):
    f32 = np.float32
    asc = np.ascontiguousarray

    def pp(v, cols):
        return asc(np.asarray(v, f32).reshape(cols, 128).T)

    common = {
        "wq": asc(np.asarray(inputs["L0_attn_wq"], f32)),
        "wk": asc(np.asarray(inputs["L0_attn_wk"], f32)),
        "wv": asc(np.asarray(inputs["L0_attn_wv"], f32)),
        "wo": asc(np.asarray(inputs["L0_attn_wo"], f32)),
        "bo": pp(inputs["L0_attn_bo"], KT),
        "projT": asc(np.asarray(inputs["L0_attn_proj"], f32).T),
        "wg": asc(np.asarray(inputs["L0_ffn_Wg"], f32)),
        "bg": asc(np.asarray(inputs["L0_ffn_bg"], f32).reshape(E, 1)),
        "gbias": asc(np.asarray(inputs["L0_ffn_bias"], f32).reshape(E, 1)),
        "w1e": asc(np.asarray(inputs["L0_ffn_W1"], f32)),
        "b1e": asc(np.asarray(inputs["L0_ffn_b1"], f32).reshape(E, DFF // 128, 128).transpose(0, 2, 1)),
        "w2e": asc(np.asarray(inputs["L0_ffn_W2"], f32)),
        "b2e": asc(np.asarray(inputs["L0_ffn_b2"], f32).reshape(E, KT, 128).transpose(0, 2, 1)),
        "q1": asc(np.asarray(inputs["L1_attn_queries"], f32)),
        "w1T": asc(np.asarray(inputs["L1_ffn_w1_w"], f32).T),
        "w2T": asc(np.asarray(inputs["L1_ffn_w2_w"], f32).T),
        "h1T": asc(np.asarray(inputs["L1_ffn_h1_w"], f32).T),
        "h2T": asc(np.asarray(inputs["L1_ffn_h2_w"], f32).T),
        "w3T": asc(np.asarray(inputs["L1_ffn_w3_w"], f32).T),
        "s1": pp(inputs["L1_ffn_w1_s"], KH),
        "s2": pp(inputs["L1_ffn_w2_s"], KH),
        "sh1": pp(inputs["L1_ffn_h1_s"], KH),
        "sh2": pp(inputs["L1_ffn_h2_s"], KH),
        "s3": pp(inputs["L1_ffn_w3_s"], KT),
        "identf": np.eye(128, dtype=f32),
        "identb": np.eye(128).astype(ml_dtypes.bfloat16),
        "sel64": asc(np.kron(np.eye(2, dtype=f32), np.full((64, 1), 0.5 * DN * DN, f32))),
        "ones64s": np.full((DH, 1), 0.5 * DN * DN, f32),
        "selE": np.kron(np.eye(E), np.ones((1, 128))).astype(ml_dtypes.bfloat16),
        "onesr": np.ones((1, 128), ml_dtypes.bfloat16),
        "onesrf": np.ones((1, 128), f32),
    }
    x = np.asarray(inputs["L0_attn_x"], f32)
    return [dict(common, x=asc(x[b])) for b in range(8)]


def kernel(**inputs):
    in_maps = make_in_maps(inputs)
    nc = build_program()
    res = run_bass_kernel_spmd(nc, in_maps, core_ids=list(range(8)))
    out = np.stack([np.asarray(r["out"]) for r in res.results], axis=0)
    return out.astype(np.float32)


# version 07 (final)
# speedup vs baseline: 1.0159x; 1.0159x over previous
import math
from contextlib import ExitStack

import numpy as np
import ml_dtypes
import concourse.bass as bass
import concourse.mybir as mybir
from concourse.bass_utils import run_bass_kernel_spmd

F32 = mybir.dt.float32
BF16 = mybir.dt.bfloat16
AF = mybir.ActivationFunctionType
ALU = mybir.AluOpType
AX = mybir.AxisListType

NT = 4096
D = 1024
KT = D // 128
TT = NT // 128
CH = NT // 512
H = 16
DH = 64
MF = 256
E = 16
DFF = 2048
DHID = 4096
KH = DHID // 128

DN = float(DH) ** -0.25
RATIO = float(MF) ** -0.5
LNRATIO = math.log(RATIO)
CEPS = RATIO * 1e-4
LNEPS = 1e-5

ALL_PHASES = ["ln0", "rout", "a1", "stab", "a2", "a3", "a4", "moe",
              "comb", "gram", "amat", "attn1", "f1", "f2", "f3", "f4"]


class Ctx:
    def __init__(self, nc, st):
        self.nc = nc
        self.st = st
        self.t = {}
        self.cms = {}
        self.phase_sems = []

    def sbuf(self, name, shape, dtype):
        cm = self.nc.sbuf_tensor("sb_" + name, shape, dtype)
        h = cm.__enter__()
        self.cms[name] = cm
        self.t[name] = h
        return h

    def free(self, *names):
        for name in names:
            cm = self.cms.pop(name, None)
            if cm is not None:
                cm.__exit__(None, None, None)
            self.t.pop(name, None)

    def sem(self, name):
        h = self.nc.alloc_semaphore(name=name)
        self.phase_sems.append(h)
        return h

    def end_phase(self):
        if self.phase_sems:
            self.nc.clear_and_free_semaphores(self.phase_sems)
            self.nc.all_engine_barrier()
            self.phase_sems = []


def build_program(upto=None, dbg=()):
    nc = bass.Bass("TRN2", target_bir_lowering=False)
    phases = ALL_PHASES if upto is None else ALL_PHASES[:ALL_PHASES.index(upto) + 1]

    x_in = nc.dram_tensor("x", [NT, D], F32, kind="ExternalInput")
    wq_in = nc.dram_tensor("wq", [D, D], F32, kind="ExternalInput")
    wk_in = nc.dram_tensor("wk", [D, D], F32, kind="ExternalInput")
    wv_in = nc.dram_tensor("wv", [D, D], F32, kind="ExternalInput")
    wo_in = nc.dram_tensor("wo", [D, D], F32, kind="ExternalInput")
    bo_in = nc.dram_tensor("bo", [128, KT], F32, kind="ExternalInput")
    projT_in = nc.dram_tensor("projT", [DH, MF], F32, kind="ExternalInput")
    wg_in = nc.dram_tensor("wg", [D, E], F32, kind="ExternalInput")
    bg_in = nc.dram_tensor("bg", [E, 1], F32, kind="ExternalInput")
    gbias_in = nc.dram_tensor("gbias", [E, 1], F32, kind="ExternalInput")
    w1e_in = nc.dram_tensor("w1e", [E, D, DFF], F32, kind="ExternalInput")
    b1e_in = nc.dram_tensor("b1e", [E, 128, DFF // 128], F32, kind="ExternalInput")
    w2e_in = nc.dram_tensor("w2e", [E, DFF, D], F32, kind="ExternalInput")
    b2e_in = nc.dram_tensor("b2e", [E, 128, KT], F32, kind="ExternalInput")
    q1_in = nc.dram_tensor("q1", [D, D], F32, kind="ExternalInput")
    w1T_in = nc.dram_tensor("w1T", [D, DHID], F32, kind="ExternalInput")
    w2T_in = nc.dram_tensor("w2T", [D, DHID], F32, kind="ExternalInput")
    h1T_in = nc.dram_tensor("h1T", [DHID, DHID], F32, kind="ExternalInput")
    h2T_in = nc.dram_tensor("h2T", [DHID, DHID], F32, kind="ExternalInput")
    w3T_in = nc.dram_tensor("w3T", [DHID, D], F32, kind="ExternalInput")
    s1_in = nc.dram_tensor("s1", [128, KH], F32, kind="ExternalInput")
    s2_in = nc.dram_tensor("s2", [128, KH], F32, kind="ExternalInput")
    sh1_in = nc.dram_tensor("sh1", [128, KH], F32, kind="ExternalInput")
    sh2_in = nc.dram_tensor("sh2", [128, KH], F32, kind="ExternalInput")
    s3_in = nc.dram_tensor("s3", [128, KT], F32, kind="ExternalInput")
    identf_in = nc.dram_tensor("identf", [128, 128], F32, kind="ExternalInput")
    identb_in = nc.dram_tensor("identb", [128, 128], BF16, kind="ExternalInput")
    sel64_in = nc.dram_tensor("sel64", [128, 2], F32, kind="ExternalInput")
    ones64s_in = nc.dram_tensor("ones64s", [DH, 1], F32, kind="ExternalInput")
    selE_in = nc.dram_tensor("selE", [E, E * 128], BF16, kind="ExternalInput")
    onesr_in = nc.dram_tensor("onesr", [1, 128], BF16, kind="ExternalInput")
    onesrf_in = nc.dram_tensor("onesrf", [1, 128], F32, kind="ExternalInput")

    out_t = nc.dram_tensor("out", [NT, D], F32, kind="ExternalOutput")

    xn_fm = nc.dram_tensor("xn_fm", [D, NT], BF16)
    attn_s = nc.dram_tensor("attn_s", [NT, D], F32)
    moe_s = nc.dram_tensor("moe_s", [NT, D], F32)
    x1_s = nc.dram_tensor("x1_s", [NT, D], F32)
    attn1_s = nc.dram_tensor("attn1_s", [NT, D], F32)
    h_s = nc.dram_tensor("h_s", [DHID, NT], BF16)
    g1_s = nc.dram_tensor("g1_s", [DHID, NT], BF16)
    g2_s = nc.dram_tensor("g2_s", [DHID, NT], BF16)
    cw_s = nc.dram_tensor("cw_s", [E, NT], BF16)
    dd_dbg = nc.dram_tensor("dd_dbg", [1, 512], F32)
    cc_in_b = nc.dram_tensor("cc_in_b", [1, 512], F32)
    cc_out_b = nc.dram_tensor("cc_out_b", [1, 512], F32, addr_space="Shared")

    scr = {"xn_fm": xn_fm, "attn_s": attn_s, "moe_s": moe_s, "x1_s": x1_s,
           "attn1_s": attn1_s, "h_s": h_s, "g1_s": g1_s, "g2_s": g2_s,
           "stab": cc_out_b, "dd_dbg": dd_dbg, "cw_s": cw_s}
    dbg_outs = {}
    for name in dbg:
        src = scr[name]
        dbg_outs[name] = (nc.dram_tensor("dbg_" + name, list(src.shape), src.dtype,
                                         kind="ExternalOutput"), src)

    with ExitStack() as st:
        c = Ctx(nc, st)
        c.dram = dict(x=x_in, wq=wq_in, wk=wk_in, wv=wv_in, wo=wo_in, bo=bo_in,
                      projT=projT_in, wg=wg_in, bg=bg_in, gbias=gbias_in,
                      w1e=w1e_in, b1e=b1e_in, w2e=w2e_in, b2e=b2e_in, q1=q1_in,
                      w1T=w1T_in, w2T=w2T_in, h1T=h1T_in, h2T=h2T_in, w3T=w3T_in,
                      s1=s1_in, s2=s2_in, sh1=sh1_in, sh2=sh2_in, s3=s3_in,
                      out=out_t, xn_fm=xn_fm, attn_s=attn_s, moe_s=moe_s,
                      x1_s=x1_s, attn1_s=attn1_s, h_s=h_s, g1_s=g1_s, g2_s=g2_s,
                      cc_in=cc_in_b, cc_out=cc_out_b, dd_dbg=dd_dbg, cw_s=cw_s)

        identf = c.sbuf("identf", [128, 128], F32)
        identb = c.sbuf("identb", [128, 128], BF16)
        sel64 = c.sbuf("sel64", [128, 2], F32)
        ones64s = c.sbuf("ones64s", [DH, 1], F32)
        selE = c.sbuf("selE", [E, E * 128], BF16)
        onesr = c.sbuf("onesr", [1, 128], BF16)
        onesrf = c.sbuf("onesrf", [1, 128], F32)
        projTdn = c.sbuf("projTdn", [128, MF], BF16)
        cst = c.sbuf("cst", [128, 6], F32)
        s_pre = c.sem("s_pre")
        s_preg = c.sem("s_preg")

        with nc.Block() as blk:
            @blk.sync
            def _(s):
                s.dma_start(out=identf[:, :], in_=identf_in.ap()).then_inc(s_pre, 16)
                s.dma_start(out=identb[:, :], in_=identb_in.ap()).then_inc(s_pre, 16)
                s.dma_start(out=sel64[:, :], in_=sel64_in.ap()).then_inc(s_pre, 16)
                s.dma_start(out=ones64s[:, :], in_=ones64s_in.ap()).then_inc(s_pre, 16)
                s.dma_start(out=selE[:, :], in_=selE_in.ap()).then_inc(s_pre, 16)
                s.dma_start(out=onesr[:, :], in_=onesr_in.ap()).then_inc(s_pre, 16)
                s.dma_start(out=onesrf[:, :], in_=onesrf_in.ap()).then_inc(s_pre, 16)

            @blk.gpsimd
            def _(g):
                g.dma_start(out=projTdn[0:DH, :], in_=projT_in.ap()).then_inc(s_preg, 16)
                g.dma_start(out=projTdn[DH:128, :], in_=projT_in.ap()).then_inc(s_preg, 16)

            @blk.vector
            def _(v):
                v.memset(cst[:, 0:1], LNEPS)
                v.memset(cst[:, 1:2], -0.5)
                v.memset(cst[:, 2:3], -1.0)
                v.memset(cst[:, 3:4], 1.0)
                v.memset(cst[:, 4:5], LNRATIO)
                v.memset(cst[:, 5:6], CEPS)
                v.wait_ge(s_pre, 16 * 7)
                v.wait_ge(s_preg, 32)
                v.tensor_scalar(projTdn[:, :], projTdn[:, :], DN, None, op0=ALU.mult)

        c.end_phase()
        cwT = c.sbuf("cwT", [E, NT], BF16)
        stabc = c.sbuf("stabc", [128, 1], F32)
        diagk = c.sbuf("diagk", [128, H * TT], F32)
        ctx_sb = c.sbuf("ctx", [128, H * 130], BF16)
        xnT = c.sbuf("xnT", [128, KT * NT], BF16)

        if "ln0" in phases:
            emit_ln0(c)
            c.end_phase()
        if "rout" in phases:
            emit_routing(c)
            c.end_phase()
        if "a1" in phases:
            emit_a1_k(c)
            c.end_phase()
        if "stab" in phases:
            emit_stab(c)
            c.end_phase()
        if "a2" in phases:
            emit_a2_ctx(c)
            c.end_phase()
        if "a3" in phases:
            emit_a3_q(c)
            c.end_phase()
        if "a4" in phases:
            emit_a4_wo(c)
            c.end_phase()
        if "moe" in phases:
            emit_moe(c)
            c.end_phase()
        if "comb" in phases:
            emit_combine_ln1(c)
            c.end_phase()
        if "gram" in phases:
            emit_gram(c)
            c.end_phase()
        if "amat" in phases:
            emit_amat(c)
            c.end_phase()
        if "attn1" in phases:
            emit_attn1(c)
            c.end_phase()
        if "f1" in phases:
            emit_f1(c)
            c.end_phase()
        if "f2" in phases:
            emit_f23(c, "h_s", "h1T", "sh1", "g1_s", tag="f2")
            c.end_phase()
        if "f3" in phases:
            emit_f23(c, "g1_s", "h2T", "sh2", "g2_s", tag="f3")
            c.end_phase()
        if "f4" in phases:
            emit_f4(c)
            c.end_phase()

        s_fin = c.sem("s_fin")
        with nc.Block() as blk:
            @blk.sync
            def _(s):
                n = 0
                for name, (dst, src) in dbg_outs.items():
                    s.dma_start(out=dst.ap(), in_=src.ap()).then_inc(s_fin, 16)
                    n += 16
                if "f4" not in phases:
                    s.dma_start(out=out_t.ap(), in_=x_in.ap()).then_inc(s_fin, 16)
                    n += 16
                if n:
                    s.wait_ge(s_fin, n)
    return nc


def emit_ln0(c):
    nc = c.nc
    xnT, identb = c.t["xnT"], c.t["identb"]
    x_in, xn_fm = c.dram["x"], c.dram["xn_fm"]
    with ExitStack() as st, nc.Block() as blk:
        xt = [st.enter_context(nc.sbuf_tensor(f"ln_x{j}", [128, D], F32)) for j in range(2)]
        xnt = [st.enter_context(nc.sbuf_tensor(f"ln_n{j}", [128, D], BF16)) for j in range(2)]
        stat = [st.enter_context(nc.sbuf_tensor(f"ln_s{j}", [128, 8], F32)) for j in range(2)]
        sq = st.enter_context(nc.sbuf_tensor("ln_sq", [128, D], F32))
        ps = [st.enter_context(nc.psum_tensor(f"ln_ps{j}", [128, 1024], BF16)) for j in range(2)]
        s_ld = [c.sem("ln_ld0"), c.sem("ln_ld1")]
        s_sv = c.sem("ln_sv")
        s_st = c.sem("ln_st")
        s_n = c.sem("ln_n")
        s_tr = c.sem("ln_tr")
        s_ev = c.sem("ln_ev")
        s_out = c.sem("ln_out")

        @blk.sync
        def _(s):
            for i in range(TT):
                if i >= 2:
                    s.wait_ge(s_n, i - 1)
                s.dma_start(out=xt[i % 2][:, :], in_=x_in.ap()[i * 128:(i + 1) * 128, :]).then_inc(s_ld[i % 2], 16)

        @blk.vector
        def _(v):
            for i in range(TT):
                b = i % 2
                v.wait_ge(s_ld[b], 16 * (i // 2 + 1))
                if i >= 2:
                    v.wait_ge(s_tr, 8 * (i - 1))
                v.reduce_sum(stat[b][:, 0:1], xt[b][:, :], axis=AX.X).then_inc(s_sv, 1)
                v.wait_ge(s_st, 3 * i + 1)
                v.wait_ge(s_sv, 3 * i + 1)
                v.tensor_scalar(stat[b][:, 2:3], stat[b][:, 0:1], 1.0 / D, None, op0=ALU.mult).then_inc(s_sv, 1)
                v.wait_ge(s_sv, 3 * i + 2)
                v.tensor_tensor(stat[b][:, 3:4], stat[b][:, 2:3], stat[b][:, 2:3], op=ALU.mult).then_inc(s_sv, 1)
                v.wait_ge(s_sv, 3 * i + 3)
                v.scalar_tensor_tensor(stat[b][:, 3:4], stat[b][:, 1:2], 1.0 / D, stat[b][:, 3:4],
                                       op0=ALU.mult, op1=ALU.subtract).then_inc(s_st, 1)
                v.wait_ge(s_st, 3 * i + 3)
                v.scalar_tensor_tensor(stat[b][:, 6:7], stat[b][:, 2:3], -1.0, stat[b][:, 5:6],
                                       op0=ALU.mult, op1=ALU.mult)
                v.tensor_scalar(xnt[b][:, :], xt[b][:, :], stat[b][:, 5:6], stat[b][:, 6:7],
                                op0=ALU.mult, op1=ALU.add).then_inc(s_n, 1)
                if i >= 1:
                    v.wait_ge(s_tr, 8 * i)
                    pb = ps[(i - 1) % 2]
                    for kt in range(KT):
                        v.tensor_copy(xnT[:, kt * NT + (i - 1) * 128: kt * NT + i * 128],
                                      pb[:, kt * 128:(kt + 1) * 128]).then_inc(s_ev, 1)
            v.wait_ge(s_tr, 8 * TT)
            pb = ps[(TT - 1) % 2]
            for kt in range(KT):
                v.tensor_copy(xnT[:, kt * NT + (TT - 1) * 128: kt * NT + TT * 128],
                              pb[:, kt * 128:(kt + 1) * 128]).then_inc(s_ev, 1)

        @blk.scalar
        def _(sc):
            for i in range(TT):
                b = i % 2
                sc.wait_ge(s_ld[b], 16 * (i // 2 + 1))
                if i >= 2:
                    sc.wait_ge(s_n, i - 1)
                sc.activation(sq[:, :], xt[b][:, :], AF.Square, accum_out=stat[b][:, 1:2]).then_inc(s_st, 1)
                sc.wait_ge(s_st, 3 * i + 2)
                sc.activation(stat[b][:, 4:5], stat[b][:, 3:4], AF.Ln, bias=c.t["cst"][:, 0:1])
                sc.activation(stat[b][:, 5:6], stat[b][:, 4:5], AF.Exp, scale=c.t["cst"][:, 1:2]).then_inc(s_st, 1)

        @blk.tensor
        def _(t):
            for i in range(TT):
                b = i % 2
                t.wait_ge(s_n, i + 1)
                if i >= 2:
                    t.wait_ge(s_ev, 8 * (i - 1))
                for kt in range(KT):
                    t.transpose(ps[i % 2][:, kt * 128:(kt + 1) * 128],
                                xnt[b][:, kt * 128:(kt + 1) * 128],
                                identb[:, :]).then_inc(s_tr, 1)

        @blk.gpsimd
        def _(g):
            g.wait_ge(s_ev, 8 * TT)
            for kt in range(KT):
                g.dma_start(out=xn_fm.ap()[kt * 128:(kt + 1) * 128, :],
                            in_=xnT[:, kt * NT:(kt + 1) * NT]).then_inc(s_out, 16)
            g.wait_ge(s_out, 16 * KT)


def emit_routing(c):
    nc = c.nc
    xnT, identf, cwT = c.t["xnT"], c.t["identf"], c.t["cwT"]
    wg_in, bg_in, gbias_in = c.dram["wg"], c.dram["bg"], c.dram["gbias"]
    with ExitStack() as st, nc.Block() as blk:
        wg_sb = st.enter_context(nc.sbuf_tensor("wg_sb", [128, KT * E], BF16))
        bgb = st.enter_context(nc.sbuf_tensor("bgb", [E, 2], F32))
        lgT = st.enter_context(nc.sbuf_tensor("lgT", [E, NT], F32))
        lt = st.enter_context(nc.sbuf_tensor("rt_lt", [128, E], F32))
        m1 = st.enter_context(nc.sbuf_tensor("rt_m1", [128, E], F32))
        m2 = st.enter_context(nc.sbuf_tensor("rt_m2", [128, E], F32))
        cwt = st.enter_context(nc.sbuf_tensor("rt_cw", [128, E], F32))
        vv = st.enter_context(nc.sbuf_tensor("rt_v", [128, 4], F32))
        ww = st.enter_context(nc.sbuf_tensor("rt_w", [128, 2], F32))
        psl = st.enter_context(nc.psum_tensor("rt_psl", [E, 512], F32))
        pst = [st.enter_context(nc.psum_tensor(f"rt_pst{j}", [128, 512], F32)) for j in range(2)]
        psb = [st.enter_context(nc.psum_tensor(f"rt_psb{j}", [128, 512], F32)) for j in range(2)]
        s_ld = c.sem("rt_ld")
        s_sv = c.sem("rt_sv")
        s_mm = c.sem("rt_mm")
        s_bt = c.sem("rt_btm")
        s_ev = c.sem("rt_ev")
        s_r = c.sem("rt_r")
        s_r2 = c.sem("rt_r2")
        s_tb = c.sem("rt_tb")

        s_ldg = c.sem("rt_ldg")

        @blk.sync
        def _(s):
            s.dma_start(out=bgb[:, 0:1], in_=bg_in.ap()).then_inc(s_ld, 16)
            s.dma_start(out=bgb[:, 1:2], in_=gbias_in.ap()).then_inc(s_ld, 16)

        @blk.tensor
        def _(t):
            t.wait_ge(s_ldg, 16)
            for ch in range(CH):
                if ch >= 1:
                    t.wait_ge(s_ev, ch)
                for kt in range(KT):
                    ins = t.matmul(psl[:, :], wg_sb[:, kt * E:(kt + 1) * E],
                                   xnT[:, kt * NT + ch * 512: kt * NT + ch * 512 + 512],
                                   start=(kt == 0), stop=(kt == KT - 1))
                ins.then_inc(s_mm, 1)
            t.wait_ge(s_ev, CH)
            for i in range(TT):
                if i >= 2:
                    t.wait_ge(s_r, 3 * (i - 2) + 1)
                t.transpose(pst[i % 2][:, 0:16], lgT[:, i * 128:(i + 1) * 128],
                            identf[0:E, 0:E]).then_inc(s_mm, 1)
                if i >= 1:
                    j = i - 1
                    t.wait_ge(s_r, 3 * j + 3)
                    if j >= 2:
                        t.wait_ge(s_tb, j - 1)
                    t.transpose(psb[j % 2][0:E, 0:128], cwt[:, :], identf[:, :]).then_inc(s_bt, 1)
            j = TT - 1
            t.wait_ge(s_r, 3 * j + 3)
            t.wait_ge(s_tb, j - 1)
            t.transpose(psb[j % 2][0:E, 0:128], cwt[:, :], identf[:, :]).then_inc(s_bt, 1)

        @blk.vector
        def _(v):
            v.wait_ge(s_ld, 32)
            v.tensor_tensor(bgb[:, 0:1], bgb[:, 0:1], bgb[:, 1:2], op=ALU.add)
            for ch in range(CH):
                v.wait_ge(s_mm, ch + 1)
                v.tensor_scalar(lgT[:, ch * 512:(ch + 1) * 512], psl[:, :], bgb[:, 0:1], None,
                                op0=ALU.add).then_inc(s_ev, 1)
            nsv = 0
            for i in range(TT):
                v.wait_ge(s_mm, CH + i + 1)
                def gate(ins):
                    nonlocal nsv
                    nsv += 1
                    ins.then_inc(s_sv, 1)
                    v.wait_ge(s_sv, nsv)
                v.tensor_copy(lt[:, :], pst[i % 2][:, 0:16]).then_inc(s_r, 1)
                v.wait_ge(s_r, 3 * i + 1)
                gate(v.reduce_max(vv[:, 0:1], lt[:, :], axis=AX.X))
                gate(v.tensor_scalar(m1[:, :], lt[:, :], vv[:, 0:1], None, op0=ALU.is_ge))
                gate(v.scalar_tensor_tensor(m2[:, :], m1[:, :], -1e9, lt[:, :], op0=ALU.mult, op1=ALU.add))
                gate(v.reduce_max(vv[:, 2:3], m2[:, :], axis=AX.X))
                gate(v.tensor_scalar(m2[:, :], m2[:, :], vv[:, 2:3], None, op0=ALU.is_ge))
                v.tensor_tensor(vv[:, 3:4], vv[:, 2:3], vv[:, 0:1], op=ALU.subtract).then_inc(s_r, 1)
                v.wait_ge(s_r2, i + 1)
                gate(v.tensor_scalar(m1[:, :], m1[:, :], ww[:, 0:1], None, op0=ALU.mult))
                if i >= 1:
                    v.wait_ge(s_bt, i)
                v.scalar_tensor_tensor(cwt[:, :], m2[:, :], ww[:, 1:2], m1[:, :],
                                       op0=ALU.mult, op1=ALU.add).then_inc(s_r, 1)
                if i >= 1:
                    v.wait_ge(s_bt, i)
                    v.tensor_copy(cwT[:, (i - 1) * 128: i * 128], psb[(i - 1) % 2][0:E, 0:128]).then_inc(s_tb, 1)
            v.wait_ge(s_bt, TT)
            v.tensor_copy(cwT[:, (TT - 1) * 128: TT * 128], psb[(TT - 1) % 2][0:E, 0:128]).then_inc(s_tb, 1)

        @blk.scalar
        def _(sc):
            for i in range(TT):
                sc.wait_ge(s_r, 3 * i + 2)
                sc.activation(ww[:, 1:2], vv[:, 3:4], AF.Sigmoid)
                sc.activation(ww[:, 0:1], vv[:, 3:4], AF.Sigmoid, scale=c.t["cst"][:, 2:3]).then_inc(s_r2, 1)

        s_cwd = c.sem("rt_cwd")

        @blk.gpsimd
        def _(g):
            g.dma_start(out=wg_sb[:, :].rearrange("p (kt o) -> p kt o", kt=KT),
                        in_=wg_in.ap().rearrange("(kt p) o -> p kt o", p=128)).then_inc(s_ldg, 16)
            g.wait_ge(s_tb, TT)
            g.dma_start(out=c.dram["cw_s"].ap(), in_=cwT[:, :]).then_inc(s_cwd, 16)
            g.wait_ge(s_cwd, 16)


def emit_a1_k(c):
    nc = c.nc
    xnT, sel64, diagk = c.t["xnT"], c.t["sel64"], c.t["diagk"]
    wk_in = c.dram["wk"]
    kT = c.sbuf("kT", [128, KT * NT], BF16)
    with ExitStack() as st, nc.Block() as blk:
        wk_sb = st.enter_context(nc.sbuf_tensor("wk_sb", [128, KT * D], BF16))
        sqb = st.enter_context(nc.sbuf_tensor("a1_sq", [128, 512], F32))
        psk = [st.enter_context(nc.psum_tensor(f"a1_ps{j}", [128, 512], F32)) for j in range(4)]
        psd = [st.enter_context(nc.psum_tensor(f"a1_pd{j}", [128, 512], F32)) for j in range(2)]
        s_ld = c.sem("a1_ld")
        s_mm = c.sem("a1_mm")
        s_ev = c.sem("a1_ev")
        s_sq = c.sem("a1_sq")
        s_dmm = c.sem("a1_dmm")
        s_dev = c.sem("a1_dev")
        sched = [(m, ch) for m in range(KT) for ch in range(CH)]

        @blk.gpsimd
        def _(g):
            g.dma_start(out=wk_sb[:, :].rearrange("p (kt o) -> p kt o", kt=KT),
                        in_=wk_in.ap().rearrange("(kt p) o -> p kt o", p=128)).then_inc(s_ld, 16)

        @blk.tensor
        def _(t):
            t.wait_ge(s_ld, 16)
            for idx, (m, ch) in enumerate(sched):
                if idx >= 4:
                    t.wait_ge(s_ev, idx - 3)
                for kt in range(KT):
                    ins = t.matmul(psk[idx % 4][:, :], wk_sb[:, kt * D + m * 128: kt * D + (m + 1) * 128],
                                   xnT[:, kt * NT + ch * 512: kt * NT + ch * 512 + 512],
                                   start=(kt == 0), stop=(kt == KT - 1))
                ins.then_inc(s_mm, 1)
            for idx, (m, ch) in enumerate(sched):
                t.wait_ge(s_sq, idx + 1)
                if idx >= 2:
                    t.wait_ge(s_dev, idx - 1)
                for j in range(4):
                    ins = t.matmul(psd[idx % 2][:, 2 * j: 2 * j + 2],
                                   sqb[:, j * 128:(j + 1) * 128], sel64[:, :],
                                   start=True, stop=True)
                ins.then_inc(s_dmm, 1)

        @blk.vector
        def _(v):
            for idx, (m, ch) in enumerate(sched):
                v.wait_ge(s_mm, idx + 1)
                v.tensor_copy(kT[:, m * NT + ch * 512: m * NT + ch * 512 + 512],
                              psk[idx % 4][:, :]).then_inc(s_ev, 1)
            for idx, (m, ch) in enumerate(sched):
                v.wait_ge(s_dmm, idx + 1)
                for j in range(4):
                    i_tile = ch * 4 + j
                    v.tensor_copy(diagk[:, (2 * m) * TT + i_tile: (2 * m) * TT + i_tile + 1],
                                  psd[idx % 2][:, 2 * j: 2 * j + 1])
                    ins = v.tensor_copy(diagk[:, (2 * m + 1) * TT + i_tile: (2 * m + 1) * TT + i_tile + 1],
                                        psd[idx % 2][:, 2 * j + 1: 2 * j + 2])
                ins.then_inc(s_dev, 1)

        @blk.scalar
        def _(sc):
            for idx, (m, ch) in enumerate(sched):
                sc.wait_ge(s_ev, idx + 1)
                if idx >= 1:
                    sc.wait_ge(s_dmm, idx)
                sc.activation(sqb[:, :], kT[:, m * NT + ch * 512: m * NT + ch * 512 + 512],
                              AF.Square).then_inc(s_sq, 1)


def emit_stab(c):
    nc = c.nc
    kT, projTdn, onesrf, stabc = c.t["kT"], c.t["projTdn"], c.t["onesrf"], c.t["stabc"]
    identf = c.t["identf"]
    cc_in, cc_out = c.dram["cc_in"], c.dram["cc_out"]
    with ExitStack() as st, nc.Block() as blk:
        mx = st.enter_context(nc.sbuf_tensor("st_mx", [128, 2], F32))
        mxv = st.enter_context(nc.sbuf_tensor("st_mxv", [1, 1], F32))
        row = st.enter_context(nc.sbuf_tensor("st_row", [1, 512], F32))
        gl = st.enter_context(nc.sbuf_tensor("st_gl", [1, 512], F32))
        psd = [st.enter_context(nc.psum_tensor(f"st_ps{j}", [128, 512], F32)) for j in range(4)]
        pst = st.enter_context(nc.psum_tensor("st_pst", [128, 512], F32))
        s_mm = c.sem("st_mm")
        s_sv = c.sem("st_sv")
        s_red = c.sem("st_red")
        s_cc = c.sem("st_cc")
        s_d = c.sem("st_d")
        sched = [(h, i) for h in range(H) for i in range(TT)]
        NS = len(sched)

        @blk.tensor
        def _(t):
            for idx, (h, i) in enumerate(sched):
                if idx >= 4:
                    t.wait_ge(s_red, idx - 3)
                po = (h % 2) * 64
                lhsT = kT[po:po + 64, (h // 2) * NT + i * 128:(h // 2) * NT + (i + 1) * 128]
                t.matmul(psd[idx % 4][:, 0:256], lhsT, projTdn[po:po + DH, :], start=True, stop=True).then_inc(s_mm, 1)
            t.wait_ge(s_red, NS)
            t.transpose(pst[0:1, 0:128], mx[:, 0:1], identf[:, :]).then_inc(s_mm, 1)
            t.wait_ge(s_d, 32)
            t.matmul(pst[:, 128:129], onesrf[:, :], gl[0:1, 0:1], start=True, stop=True).then_inc(s_mm, 1)

        @blk.vector
        def _(v):
            v.memset(mx[:, 0:1], -3e38).then_inc(s_sv, 1)
            nsv = 1
            for idx, (h, i) in enumerate(sched):
                v.wait_ge(s_mm, idx + 1)
                v.reduce_max(mx[:, 1:2], psd[idx % 4][:, 0:256], axis=AX.X).then_inc(s_sv, 1)
                nsv += 1
                v.wait_ge(s_sv, nsv)
                v.tensor_tensor(mx[:, 0:1], mx[:, 0:1], mx[:, 1:2], op=ALU.max).then_inc(s_red, 1)
                nsv += 0
                v.wait_ge(s_red, idx + 1)
            v.wait_ge(s_mm, NS + 1)
            v.reduce_max(mxv[0:1, 0:1], pst[0:1, 0:128], axis=AX.X).then_inc(s_sv, 1)
            nsv += 1
            v.wait_ge(s_sv, nsv)
            v.memset(row[0:1, :], 1.0).then_inc(s_sv, 1)
            nsv += 1
            v.wait_ge(s_sv, nsv)
            v.tensor_scalar(row[0:1, :], row[0:1, :], mxv[0:1, 0:1], None, op0=ALU.mult).then_inc(s_red, 1)
            v.wait_ge(s_mm, NS + 2)
            v.scalar_tensor_tensor(stabc[:, :], pst[:, 128:129], -1.0, c.t["cst"][:, 4:5],
                                   op0=ALU.mult, op1=ALU.add).then_inc(s_red, 1)

        @blk.gpsimd
        def _(g):
            g.wait_ge(s_red, NS + 1)
            g.dma_start(out=cc_in.ap(), in_=row[0:1, :]).then_inc(s_d, 16)
            g.wait_ge(s_d, 16)
            g.collective_compute("AllReduce", ALU.max, replica_groups=[list(range(8))],
                                 ins=[cc_in.ap()], outs=[cc_out.ap()]).then_inc(s_cc)
            g.wait_ge(s_cc, 1)
            g.dma_start(out=gl[0:1, :], in_=cc_out.ap()).then_inc(s_d, 16)
            g.wait_ge(s_d, 32)


def emit_a2_ctx(c):
    nc = c.nc
    xnT, kT, projTdn = c.t["xnT"], c.t["kT"], c.t["projTdn"]
    diagk, stabc, identb = c.t["diagk"], c.t["stabc"], c.t["identb"]
    cst = c.t["cst"]
    wv_in = c.dram["wv"]
    ctx_sb = c.t["ctx"]
    with ExitStack() as st, nc.Block() as blk:
        wv_sb = st.enter_context(nc.sbuf_tensor("a2_wv", [128, KT * D], BF16))
        vT = st.enter_context(nc.sbuf_tensor("a2_vT", [64, NT], BF16))
        vtok = st.enter_context(nc.sbuf_tensor("a2_vtok", [128, TT * 65], BF16))
        diagk2 = st.enter_context(nc.sbuf_tensor("a2_dk2", [128, H * TT], F32))
        kpf = [st.enter_context(nc.sbuf_tensor(f"a2_kpf{j}", [128, MF], F32)) for j in range(3)]
        kpb = [st.enter_context(nc.sbuf_tensor(f"a2_kpb{j}", [128, MF], BF16)) for j in range(3)]
        ps_v = [st.enter_context(nc.psum_tensor(f"a2_pv{j}", [128, 512], F32)) for j in range(1)]
        ps_t = [st.enter_context(nc.psum_tensor(f"a2_pt{j}", [128, 1024], BF16)) for j in range(2)]
        ps_dd = [st.enter_context(nc.psum_tensor(f"a2_pd{j}", [128, 512], F32)) for j in range(3)]
        ps_cx = [st.enter_context(nc.psum_tensor(f"a2_pc{j}", [128, 512], F32)) for j in range(2)]
        s_wld = c.sem("a2_wld")
        s_vmm = c.sem("a2_vmm")
        s_vev = c.sem("a2_vev")
        s_tmm = c.sem("a2_tmm")
        s_vtev = c.sem("a2_vtev")
        s_ddmm = c.sem("a2_ddmm")
        s_b = c.sem("a2_b")
        s_kexp = c.sem("a2_kexp")
        s_kpe = c.sem("a2_kpe")
        s_cmm = c.sem("a2_cmm")
        s_cev = c.sem("a2_cev")

        @blk.gpsimd
        def _(g):
            g.dma_start(out=wv_sb[:, :].rearrange("p (kt o) -> p kt o", kt=KT),
                        in_=wv_in.ap().rearrange("(kt p) o -> p kt o", p=128)).then_inc(s_wld, 16)

        @blk.tensor
        def _(t):
            t.wait_ge(s_wld, 16)
            for h in range(H):
                po = (h % 2) * 64
                mt = h // 2
                for ch in range(CH):
                    idx = h * CH + ch
                    if idx >= 1:
                        t.wait_ge(s_vev, idx)
                    for kt in range(KT):
                        ins = t.matmul(ps_v[0][0:64, :],
                                       wv_sb[:, kt * D + h * 64: kt * D + h * 64 + 64],
                                       xnT[:, kt * NT + ch * 512: kt * NT + ch * 512 + 512],
                                       start=(kt == 0), stop=(kt == KT - 1))
                    ins.then_inc(s_vmm, 1)
                for i in range(TT):
                    idx = h * TT + i
                    t.wait_ge(s_vev, h * CH + i // 4 + 1)
                    if idx >= 2:
                        t.wait_ge(s_vtev, idx - 1)
                    t.transpose(ps_t[idx % 2][0:128, 0:64], vT[0:64, i * 128:(i + 1) * 128],
                                identb[0:64, 0:64]).then_inc(s_tmm, 1)
                for i in range(TT + 1):
                    idx = h * TT + i
                    if i < TT:
                        if idx >= 3:
                            t.wait_ge(s_kexp, 2 * (idx - 2))
                        t.matmul(ps_dd[idx % 3][:, 0:MF],
                                 kT[po:po + 64, mt * NT + i * 128: mt * NT + (i + 1) * 128],
                                 projTdn[po:po + DH, :], start=True, stop=True).then_inc(s_ddmm, 1)
                    if i >= 1:
                        j = i - 1
                        jdx = h * TT + j
                        t.wait_ge(s_kexp, 2 * (jdx + 1))
                        t.wait_ge(s_vtev, jdx + 1)
                        if j == 0 and h >= 1:
                            t.wait_ge(s_cev, 2 * h)
                        for m2 in range(2):
                            ins = t.matmul(ps_cx[m2][:, 0:65],
                                           kpb[j % 3][:, m2 * 128:(m2 + 1) * 128],
                                           vtok[:, j * 65: j * 65 + 65],
                                           start=(j == 0), stop=(j == TT - 1))
                            if m2 == 0:
                                ins.then_inc(s_kpe, 1)
                        if j == TT - 1:
                            ins.then_inc(s_cmm, 1)

        @blk.vector
        def _(v):
            v.tensor_scalar(diagk2[:, :], diagk[:, :], -1.0, None, op0=ALU.mult).then_inc(s_b, 1)
            v.wait_ge(s_b, 1)
            v.tensor_scalar(diagk2[:, :], diagk2[:, :], stabc[:, 0:1], None, op0=ALU.add).then_inc(s_b, 1)
            for h in range(H):
                for ch in range(CH):
                    idx = h * CH + ch
                    v.wait_ge(s_vmm, idx + 1)
                    v.tensor_copy(vT[0:64, ch * 512:(ch + 1) * 512], ps_v[0][0:64, :]).then_inc(s_vev, 1)
                if h == 0:
                    v.memset(vtok[:, :], 1.0)
                for i in range(TT):
                    idx = h * TT + i
                    v.wait_ge(s_tmm, idx + 1)
                    v.tensor_copy(vtok[:, i * 65: i * 65 + 64], ps_t[idx % 2][0:128, 0:64]).then_inc(s_vtev, 1)
                v.wait_ge(s_cmm, h + 1)
                v.tensor_copy(ctx_sb[:, h * 130: h * 130 + 65], ps_cx[0][:, 0:65]).then_inc(s_cev, 1)
                v.tensor_copy(ctx_sb[:, h * 130 + 65: h * 130 + 130], ps_cx[1][:, 0:65]).then_inc(s_cev, 1)

        @blk.scalar
        def _(sc):
            sc.wait_ge(s_b, 2)
            for h in range(H):
                for i in range(TT):
                    idx = h * TT + i
                    sc.wait_ge(s_ddmm, idx + 1)
                    if idx >= 3:
                        sc.wait_ge(s_kpe, idx - 2)
                    sc.activation(kpf[idx % 3][:, :], ps_dd[idx % 3][:, 0:MF], AF.Exp,
                                  bias=diagk2[:, h * TT + i: h * TT + i + 1]).then_inc(s_kexp, 1)
                    sc.activation(kpb[idx % 3][:, :], kpf[idx % 3][:, :], AF.Relu,
                                  bias=cst[:, 5:6]).then_inc(s_kexp, 1)
    c.free("kT")


def emit_a3_q(c):
    nc = c.nc
    xnT, projTdn, ctx_sb = c.t["xnT"], c.t["projTdn"], c.t["ctx"]
    identb, cst = c.t["identb"], c.t["cst"]
    onesr = c.t["onesr"]
    wq_in = c.dram["wq"]
    attn_inT = c.sbuf("attn_inT", [128, KT * NT], BF16)
    with ExitStack() as st, nc.Block() as blk:
        wq_sb = st.enter_context(nc.sbuf_tensor("a3_wq", [128, KT * D], BF16))
        qT = st.enter_context(nc.sbuf_tensor("a3_qT", [64, NT], BF16))
        sqq = st.enter_context(nc.sbuf_tensor("a3_sq", [64, 512], F32))
        diagq = st.enter_context(nc.sbuf_tensor("a3_dq", [128, TT], F32))
        qpf = [st.enter_context(nc.sbuf_tensor(f"a3_qpf{j}", [128, MF], F32)) for j in range(2)]
        qpb = [st.enter_context(nc.sbuf_tensor(f"a3_qpb{j}", [128, MF], BF16)) for j in range(2)]
        nmr = [st.enter_context(nc.sbuf_tensor(f"a3_nm{j}", [128, 2], F32)) for j in range(2)]
        bsb = [st.enter_context(nc.sbuf_tensor(f"a3_b{j}", [128, 1], F32)) for j in range(2)]
        qpT = st.enter_context(nc.sbuf_tensor("a3_qpT", [128, 2 * NT], BF16))
        otmp = [st.enter_context(nc.sbuf_tensor(f"a3_ot{j}", [128, 512], F32)) for j in range(2)]
        dinb = [st.enter_context(nc.sbuf_tensor(f"a3_di{j}", [1, 512], BF16)) for j in range(2)]
        dinf = [st.enter_context(nc.sbuf_tensor(f"a3_df{j}", [1, 512], F32)) for j in range(2)]
        ps_q = [st.enter_context(nc.psum_tensor(f"a3_pq{j}", [128, 512], F32)) for j in range(2)]
        ps_dq = [st.enter_context(nc.psum_tensor(f"a3_pdq{j}", [128, 512], F32)) for j in range(2)]
        ps_tq = st.enter_context(nc.psum_tensor("a3_ptq", [128, 1024], BF16))
        ps_o = [st.enter_context(nc.psum_tensor(f"a3_po{j}", [128, 512], F32)) for j in range(2)]
        ps_db = st.enter_context(nc.psum_tensor("a3_pdb", [128, 512], F32))
        ps_dd4 = [ps_q[0], ps_q[1], ps_dq[0], ps_dq[1]]
        s_wld = c.sem("a3_wld")
        s_qmm = c.sem("a3_qmm")
        s_qev = c.sem("a3_qev")
        s_qsq = c.sem("a3_qsq")
        s_dqm = c.sem("a3_dqm")
        s_dqe = c.sem("a3_dqe")
        s_ddmm = c.sem("a3_ddmm")
        s_sv = c.sem("a3_sv")
        s_b = c.sem("a3_b")
        s_qexp = c.sem("a3_qexp")
        s_qpe = c.sem("a3_qpe")
        s_tqm = c.sem("a3_tqm")
        s_tqe = c.sem("a3_tqe")
        s_omm = c.sem("a3_omm")
        s_oc = c.sem("a3_oc")
        s_dnm = c.sem("a3_dnm")
        s_db = c.sem("a3_db")
        s_bmm = c.sem("a3_bmm")
        s_aev = c.sem("a3_aev")
        nsv_cnt = [0]

        @blk.gpsimd
        def _(g):
            g.dma_start(out=wq_sb[:, :].rearrange("p (kt o) -> p kt o", kt=KT),
                        in_=wq_in.ap().rearrange("(kt p) o -> p kt o", p=128)).then_inc(s_wld, 16)

        def emit_out_mm(t, h, ch):
            po = (h % 2) * 64
            idx = h * CH + ch
            t.wait_ge(s_tqe, 2 * (h * TT + 4 * ch + 4))
            if idx >= 2:
                t.wait_ge(s_aev, idx - 1)
            for m2 in range(2):
                ins = t.matmul(ps_o[idx % 2][po:po + 64, :],
                               ctx_sb[:, h * 130 + m2 * 65: h * 130 + m2 * 65 + 64],
                               qpT[:, m2 * NT + ch * 512: m2 * NT + ch * 512 + 512],
                               start=(m2 == 0), stop=(m2 == 1))
            ins.then_inc(s_omm, 1)
            if idx >= 1:
                t.wait_ge(s_aev, idx)
            for m2 in range(2):
                ins = t.matmul(ps_db[0:1, 0:512],
                               ctx_sb[:, h * 130 + m2 * 65 + 64: h * 130 + m2 * 65 + 65],
                               qpT[:, m2 * NT + ch * 512: m2 * NT + ch * 512 + 512],
                               start=(m2 == 0), stop=(m2 == 1))
            ins.then_inc(s_dnm, 1)
            t.wait_ge(s_db, idx + 1)
            t.matmul(ps_db[po:po + 64, 0:512], onesr[0:1, 0:64],
                     dinb[idx % 2][0:1, :], start=True, stop=True).then_inc(s_bmm, 1)

        @blk.tensor
        def _(t):
            t.wait_ge(s_wld, 16)
            for h in range(H):
                po = (h % 2) * 64
                for ch in range(CH):
                    idx = h * CH + ch
                    if h >= 1 and ch < 2:
                        t.wait_ge(s_qexp, 2 * h * TT)
                    if idx >= 2:
                        t.wait_ge(s_qsq, idx - 1)
                        t.wait_ge(s_qev, idx - 1)
                    for kt in range(KT):
                        ins = t.matmul(ps_q[idx % 2][0:64, :],
                                       wq_sb[:, kt * D + h * 64: kt * D + h * 64 + 64],
                                       xnT[:, kt * NT + ch * 512: kt * NT + ch * 512 + 512],
                                       start=(kt == 0), stop=(kt == KT - 1))
                    ins.then_inc(s_qmm, 1)
                    t.wait_ge(s_qsq, idx + 1)
                    if idx >= 2:
                        t.wait_ge(s_dqe, idx - 1)
                    for j in range(4):
                        ins = t.matmul(ps_dq[idx % 2][:, 2 * j: 2 * j + 2], sqq[0:64, j * 128:(j + 1) * 128],
                                       c.t["sel64"][0:64, :], start=True, stop=True)
                    ins.then_inc(s_dqm, 1)
                for i in range(TT):
                    idx = h * TT + i
                    if i < 4:
                        t.wait_ge(s_qev, (h + 1) * CH)
                        t.wait_ge(s_qsq, (h + 1) * CH)
                        t.wait_ge(s_dqe, (h + 1) * CH)
                    if idx >= 4:
                        t.wait_ge(s_qexp, 2 * (idx - 3))
                    t.matmul(ps_dd4[idx % 4][:, 0:MF],
                             qT[0:64, i * 128:(i + 1) * 128],
                             projTdn[0:DH, :], start=True, stop=True).then_inc(s_ddmm, 1)
                    if i >= 1:
                        j = i - 1
                        jdx = h * TT + j
                        t.wait_ge(s_qexp, 2 * (jdx + 1))
                        if jdx >= 1:
                            t.wait_ge(s_tqe, 2 * jdx)
                        for m2 in range(2):
                            ins = t.transpose(ps_tq[0:128, m2 * 512: m2 * 512 + 128],
                                              qpb[j % 2][:, m2 * 128:(m2 + 1) * 128],
                                              identb[:, :])
                            ins.then_inc(s_tqm if m2 == 1 else s_qpe, 1)
                            if m2 == 1:
                                pass
                        if j >= 3 and (j - 3) % 4 == 0:
                            emit_out_mm(t, h, (j - 3) // 4)
                j = TT - 1
                jdx = h * TT + j
                t.wait_ge(s_qexp, 2 * (jdx + 1))
                t.wait_ge(s_tqe, 2 * jdx)
                for m2 in range(2):
                    ins = t.transpose(ps_tq[0:128, m2 * 512: m2 * 512 + 128],
                                      qpb[j % 2][:, m2 * 128:(m2 + 1) * 128],
                                      identb[:, :])
                    ins.then_inc(s_tqm if m2 == 1 else s_qpe, 1)
                emit_out_mm(t, h, CH - 1)

        @blk.scalar
        def _(sc):
            for h in range(H):
                po = (h % 2) * 64
                for ch in range(CH):
                    idx = h * CH + ch
                    sc.wait_ge(s_qmm, idx + 1)
                    sc.wait_ge(s_qev, idx + 1)
                    if idx >= 1:
                        sc.wait_ge(s_dqm, idx)
                    sc.activation(sqq[0:64, :], ps_q[idx % 2][0:64, :], AF.Square).then_inc(s_qsq, 1)
                for i in range(TT):
                    idx = h * TT + i
                    sc.wait_ge(s_ddmm, idx + 1)
                    sc.wait_ge(s_b, idx + 1)
                    if idx >= 2:
                        sc.wait_ge(s_qpe, idx - 1)
                    sc.activation(qpf[idx % 2][:, :], ps_dd4[idx % 4][:, 0:MF], AF.Exp,
                                  bias=bsb[idx % 2][:, 0:1]).then_inc(s_qexp, 1)
                    sc.activation(qpb[idx % 2][:, :], qpf[idx % 2][:, :], AF.Relu,
                                  bias=cst[:, 5:6]).then_inc(s_qexp, 1)
                    if i >= 4 and (i - 4) % 4 == 0:
                        ch = (i - 4) // 4
                        cidx = h * CH + ch
                        sc.wait_ge(s_omm, cidx + 1)
                        if cidx >= 2:
                            sc.wait_ge(s_aev, cidx - 1)
                        sc.copy(otmp[cidx % 2][po:po + 64, :], ps_o[cidx % 2][po:po + 64, :]).then_inc(s_oc, 1)
                        sc.wait_ge(s_dnm, cidx + 1)
                        sc.activation(dinf[cidx % 2][0:1, :], ps_db[0:1, 0:512], AF.Ln)
                        sc.activation(dinb[cidx % 2][0:1, :], dinf[cidx % 2][0:1, :], AF.Exp,
                                      scale=cst[0:1, 2:3]).then_inc(s_db, 1)
                for ch in (CH - 1,):
                    cidx = h * CH + ch
                    sc.wait_ge(s_omm, cidx + 1)
                    if cidx >= 2:
                        sc.wait_ge(s_aev, cidx - 1)
                    sc.copy(otmp[cidx % 2][po:po + 64, :], ps_o[cidx % 2][po:po + 64, :]).then_inc(s_oc, 1)
                    sc.wait_ge(s_dnm, cidx + 1)
                    sc.activation(dinf[cidx % 2][0:1, :], ps_db[0:1, 0:512], AF.Ln)
                    sc.activation(dinb[cidx % 2][0:1, :], dinf[cidx % 2][0:1, :], AF.Exp,
                                  scale=cst[0:1, 2:3]).then_inc(s_db, 1)

        def emit_out_ev(v, h, ch):
            po = (h % 2) * 64
            mt = h // 2
            idx = h * CH + ch
            v.wait_ge(s_oc, idx + 1)
            v.wait_ge(s_bmm, idx + 1)
            v.tensor_tensor(attn_inT[po:po + 64, mt * NT + ch * 512: mt * NT + ch * 512 + 512],
                            otmp[idx % 2][po:po + 64, :],
                            ps_db[po:po + 64, 0:512], op=ALU.mult).then_inc(s_aev, 1)

        @blk.vector
        def _(v):
            for h in range(H):
                for ch in range(CH):
                    idx = h * CH + ch
                    v.wait_ge(s_qmm, idx + 1)
                    v.tensor_copy(qT[0:64, ch * 512:(ch + 1) * 512], ps_q[idx % 2][0:64, :]).then_inc(s_qev, 1)
                    v.wait_ge(s_dqm, idx + 1)
                    for j in range(3):
                        v.tensor_copy(diagq[:, ch * 4 + j: ch * 4 + j + 1], ps_dq[idx % 2][:, 2 * j: 2 * j + 1])
                    v.tensor_copy(diagq[:, ch * 4 + 3: ch * 4 + 4], ps_dq[idx % 2][:, 6:7]).then_inc(s_dqe, 1)
                v.tensor_scalar(diagq[:, :], diagq[:, :], -1.0, None, op0=ALU.mult).then_inc(s_sv, 1)
                nsv_cnt[0] += 1
                v.wait_ge(s_sv, nsv_cnt[0])
                v.tensor_scalar(diagq[:, :], diagq[:, :], LNRATIO, None, op0=ALU.add).then_inc(s_sv, 1)
                nsv_cnt[0] += 1
                v.wait_ge(s_sv, nsv_cnt[0])
                for i in range(TT):
                    idx = h * TT + i
                    v.wait_ge(s_ddmm, idx + 1)
                    v.reduce_max(nmr[idx % 2][:, 0:1], ps_dd4[idx % 4][:, 0:MF], axis=AX.X,
                                 negate=True).then_inc(s_sv, 1)
                    nsv_cnt[0] += 1
                    v.wait_ge(s_sv, nsv_cnt[0])
                    if idx >= 2:
                        v.wait_ge(s_qexp, 2 * (idx - 1))
                    v.tensor_tensor(bsb[idx % 2][:, 0:1], diagq[:, i:i + 1],
                                    nmr[idx % 2][:, 0:1], op=ALU.add).then_inc(s_b, 1)
                    if i >= 1:
                        jdx = h * TT + (i - 1)
                        v.wait_ge(s_tqm, jdx + 1)
                        for m2 in range(2):
                            v.tensor_copy(qpT[:, m2 * NT + (i - 1) * 128: m2 * NT + i * 128],
                                          ps_tq[0:128, m2 * 512: m2 * 512 + 128]).then_inc(s_tqe, 1)
                        jj = i - 1
                        if jj >= 3 and (jj - 3) % 4 == 0:
                            emit_out_ev(v, h, (jj - 3) // 4)
                jdx = h * TT + TT - 1
                v.wait_ge(s_tqm, jdx + 1)
                for m2 in range(2):
                    v.tensor_copy(qpT[:, m2 * NT + (TT - 1) * 128: m2 * NT + TT * 128],
                                  ps_tq[0:128, m2 * 512: m2 * 512 + 128]).then_inc(s_tqe, 1)
                emit_out_ev(v, h, CH - 1)


def emit_a4_wo(c):
    nc = c.nc
    attn_inT, identf = c.t["attn_inT"], c.t["identf"]
    wo_in, bo_in, attn_s = c.dram["wo"], c.dram["bo"], c.dram["attn_s"]
    with ExitStack() as st, nc.Block() as blk:
        wo_sb = st.enter_context(nc.sbuf_tensor("a4_wo", [128, KT * D], BF16))
        bo_sb = st.enter_context(nc.sbuf_tensor("a4_bo", [128, KT], F32))
        of = [st.enter_context(nc.sbuf_tensor(f"a4_of{j}", [128, 512], F32)) for j in range(2)]
        arow = [st.enter_context(nc.sbuf_tensor(f"a4_ar{j}", [128, 4 * D], F32)) for j in range(2)]
        ps_mm = [st.enter_context(nc.psum_tensor(f"a4_pm{j}", [128, 512], F32)) for j in range(4)]
        ps_tr = [st.enter_context(nc.psum_tensor(f"a4_pt{j}", [128, 512], F32)) for j in range(4)]
        s_wld = c.sem("a4_wld")
        s_bld = c.sem("a4_bld")
        s_mm = c.sem("a4_mm")
        s_ev = c.sem("a4_ev")
        s_trm = c.sem("a4_trm")
        s_tre = c.sem("a4_tre")
        s_st = [c.sem("a4_st0"), c.sem("a4_st1")]

        @blk.gpsimd
        def _(g):
            g.dma_start(out=wo_sb[:, :].rearrange("p (kt o) -> p kt o", kt=KT),
                        in_=wo_in.ap().rearrange("(kt p) o -> p kt o", p=128)).then_inc(s_wld, 16)

        @blk.sync
        def _(s):
            s.dma_start(out=bo_sb[:, :], in_=bo_in.ap()).then_inc(s_bld, 16)
            for cch in range(CH):
                s.wait_ge(s_tre, 32 * cch + 32)
                s.dma_start(out=attn_s.ap()[cch * 512:(cch + 1) * 512, :].rearrange("(j p) d -> p j d", p=128),
                            in_=arow[cch % 2][:, :].rearrange("p (j d) -> p j d", j=4)).then_inc(s_st[cch % 2], 16)
            s.wait_ge(s_st[0], 16 * (CH // 2))
            s.wait_ge(s_st[1], 16 * (CH // 2))

        @blk.tensor
        def _(t):
            t.wait_ge(s_wld, 16)
            for cch in range(CH):
                for m in range(KT):
                    idx = cch * KT + m
                    if idx >= 4:
                        t.wait_ge(s_ev, idx - 3)
                    for kt in range(KT):
                        ins = t.matmul(ps_mm[idx % 4][:, :],
                                       wo_sb[:, kt * D + m * 128: kt * D + (m + 1) * 128],
                                       attn_inT[:, kt * NT + cch * 512: kt * NT + cch * 512 + 512],
                                       start=(kt == 0), stop=(kt == KT - 1))
                    ins.then_inc(s_mm, 1)
                    if idx >= 1:
                        jdx = idx - 1
                        t.wait_ge(s_ev, jdx + 1)
                        if jdx >= 1:
                            t.wait_ge(s_tre, 4 * jdx)
                        for j in range(4):
                            t.transpose(ps_tr[j][:, 0:128], of[jdx % 2][:, j * 128:(j + 1) * 128],
                                        identf[:, :]).then_inc(s_trm, 1)
            jdx = CH * KT - 1
            t.wait_ge(s_ev, jdx + 1)
            t.wait_ge(s_tre, 4 * jdx)
            for j in range(4):
                t.transpose(ps_tr[j][:, 0:128], of[jdx % 2][:, j * 128:(j + 1) * 128],
                            identf[:, :]).then_inc(s_trm, 1)

        @blk.vector
        def _(v):
            v.wait_ge(s_bld, 16)
            for cch in range(CH):
                for m in range(KT):
                    idx = cch * KT + m
                    v.wait_ge(s_mm, idx + 1)
                    if idx >= 2:
                        v.wait_ge(s_trm, 4 * (idx - 1))
                    v.tensor_scalar(of[idx % 2][:, :], ps_mm[idx % 4][:, :], bo_sb[:, m: m + 1], None,
                                    op0=ALU.add).then_inc(s_ev, 1)
                    if idx >= 1:
                        jdx = idx - 1
                        jc, jm = jdx // KT, jdx % KT
                        if jc >= 2:
                            v.wait_ge(s_st[jc % 2], 16 * (jc // 2))
                        for j in range(4):
                            v.wait_ge(s_trm, 4 * jdx + j + 1)
                            v.tensor_copy(arow[jc % 2][:, j * D + jm * 128: j * D + (jm + 1) * 128],
                                          ps_tr[j][:, 0:128]).then_inc(s_tre, 1)
            jdx = CH * KT - 1
            jc, jm = jdx // KT, jdx % KT
            for j in range(4):
                v.wait_ge(s_trm, 4 * jdx + j + 1)
                v.tensor_copy(arow[jc % 2][:, j * D + jm * 128: j * D + (jm + 1) * 128],
                              ps_tr[j][:, 0:128]).then_inc(s_tre, 1)
    c.free("attn_inT", "xnT", "ctx", "diagk", "stabc")


def emit_moe(c):
    nc = c.nc
    cwT = c.t["cwT"]
    identf = c.t["identf"]
    selE = c.t["selE"]
    xn_fm, w1e_in, b1e_in = c.dram["xn_fm"], c.dram["w1e"], c.dram["b1e"]
    w2e_in, b2e_in, moe_s = c.dram["w2e"], c.dram["b2e"], c.dram["moe_s"]
    HF = 2
    NTH = NT // HF
    CCH = NTH // 512
    HM = DFF // 128
    with ExitStack() as st, nc.Block() as blk:
        xnh = st.enter_context(nc.sbuf_tensor("mo_xn", [128, KT * NTH], BF16))
        w1_sb = st.enter_context(nc.sbuf_tensor("mo_w1", [128, KT * DFF], BF16))
        w2_sb = st.enter_context(nc.sbuf_tensor("mo_w2", [128, HM * D], BF16))
        h_sb = st.enter_context(nc.sbuf_tensor("mo_h", [128, HM * 512], BF16))
        cwbc = st.enter_context(nc.sbuf_tensor("mo_cw", [128, NTH], BF16))
        acc = st.enter_context(nc.sbuf_tensor("mo_acc", [128, KT * NTH], F32))
        tmp = [st.enter_context(nc.sbuf_tensor(f"mo_tmp{j}", [128, 512], F32)) for j in range(2)]
        arow = [st.enter_context(nc.sbuf_tensor(f"mo_ar{j}", [128, D], F32)) for j in range(2)]
        b1s = st.enter_context(nc.sbuf_tensor("mo_b1", [128, HM], F32))
        b2s = st.enter_context(nc.sbuf_tensor("mo_b2", [128, KT], F32))
        ps_h = [st.enter_context(nc.psum_tensor(f"mo_ph{j}", [128, 512], F32)) for j in range(4)]
        ps_y = [st.enter_context(nc.psum_tensor(f"mo_py{j}", [128, 512], F32)) for j in range(4)]
        s_xld = c.sem("mo_xld")
        s_bld = c.sem("mo_bld")
        s_w1 = c.sem("mo_w1")
        s_w2 = c.sem("mo_w2")
        s_cwm = c.sem("mo_cwm")
        s_cwe = c.sem("mo_cwe")
        s_hmm = c.sem("mo_hmm")
        s_hev = c.sem("mo_hev")
        s_ymm = c.sem("mo_ymm")
        s_tev = c.sem("mo_tev")
        s_acc = c.sem("mo_acc")
        s_trm = c.sem("mo_trm")
        s_tre = c.sem("mo_tre")
        s_mst = [c.sem("mo_st0"), c.sem("mo_st1")]

        @blk.sync
        def _(s):
            for hf in range(HF):
                for kt in range(KT):
                    if hf >= 1 and kt == 0:
                        s.wait_ge(s_hmm, 64 * E * hf)
                    s.dma_start(out=xnh[:, kt * NTH:(kt + 1) * NTH],
                                in_=xn_fm.ap()[kt * 128:(kt + 1) * 128, hf * NTH:(hf + 1) * NTH]).then_inc(s_xld, 16)
                for e in range(E):
                    ie = hf * E + e
                    if ie >= 1:
                        s.wait_ge(s_hev, 64 * ie)
                        s.wait_ge(s_tev, 32 * ie)
                    s.dma_start(out=b1s[:, :], in_=b1e_in.ap()[e]).then_inc(s_bld, 16)
                    s.dma_start(out=b2s[:, :], in_=b2e_in.ap()[e]).then_inc(s_bld, 16)
                for i in range(E):
                    gi = hf * E + i
                    s.wait_ge(s_tre, 8 * (gi + 1))
                    s.dma_start(out=moe_s.ap()[hf * NTH + i * 128: hf * NTH + (i + 1) * 128, :],
                                in_=arow[gi % 2][:, :]).then_inc(s_mst[gi % 2], 16)
            s.wait_ge(s_mst[0], 16 * E)
            s.wait_ge(s_mst[1], 16 * E)

        @blk.gpsimd
        def _(g):
            nacc = 0
            for hf in range(HF):
                if hf >= 1:
                    g.wait_ge(s_trm, 128 * hf)
                g.memset(acc[:, :], 0.0)
                for e in range(E):
                    ie = hf * E + e
                    if ie >= 1:
                        g.wait_ge(s_hmm, 64 * ie)
                    g.dma_start(out=w1_sb[:, :].rearrange("p (kt o) -> p kt o", kt=KT),
                                in_=w1e_in.ap()[e].rearrange("(kt p) o -> p kt o", p=128)).then_inc(s_w1, 16)
                    if ie >= 1:
                        g.wait_ge(s_ymm, 32 * ie)
                    g.dma_start(out=w2_sb[:, :].rearrange("p (kt o) -> p kt o", kt=HM),
                                in_=w2e_in.ap()[e].rearrange("(kt p) o -> p kt o", p=128)).then_inc(s_w2, 16)
                    for cc in range(CCH):
                        for ym in range(KT):
                            nacc += 1
                            g.wait_ge(s_tev, nacc)
                            g.tensor_tensor(acc[:, ym * NTH + cc * 512: ym * NTH + cc * 512 + 512],
                                            acc[:, ym * NTH + cc * 512: ym * NTH + cc * 512 + 512],
                                            tmp[nacc % 2][:, :], op=ALU.add).then_inc(s_acc, 1)

        @blk.tensor
        def _(t):
            ngh = 0
            ngy = 0
            for hf in range(HF):
                for e in range(E):
                    ie = hf * E + e
                    t.wait_ge(s_tev, 32 * ie)
                    if hf >= 1 and e == 0:
                        t.wait_ge(s_trm, 128 * hf)
                    for cc in range(CCH):
                        t.wait_ge(s_cwe, 4 * ie + cc)
                        t.matmul(ps_y[0][:, 0:512], selE[:, e * 128:(e + 1) * 128],
                                 cwT[:, hf * NTH + cc * 512: hf * NTH + cc * 512 + 512],
                                 start=True, stop=True).then_inc(s_cwm, 1)
                    t.wait_ge(s_w1, 16 * (ie + 1))
                    t.wait_ge(s_xld, 16 * KT * (hf + 1))
                    for cc in range(CCH):
                        for hm in range(HM):
                            ngh += 1
                            if ngh > 4:
                                t.wait_ge(s_hev, ngh - 4)
                            for kt in range(KT):
                                ins = t.matmul(ps_h[(ngh - 1) % 4][:, :],
                                               w1_sb[:, kt * DFF + hm * 128: kt * DFF + (hm + 1) * 128],
                                               xnh[:, kt * NTH + cc * 512: kt * NTH + cc * 512 + 512],
                                               start=(kt == 0), stop=(kt == KT - 1))
                            ins.then_inc(s_hmm, 1)
                        if cc == 0:
                            t.wait_ge(s_w2, 16 * (ie + 1))
                        t.wait_ge(s_hev, 16 * (4 * ie + cc + 1))
                        for ym in range(KT):
                            ngy += 1
                            if ngy > 4:
                                t.wait_ge(s_tev, ngy - 4)
                            for kt2 in range(HM):
                                ins = t.matmul(ps_y[(ngy - 1) % 4][:, :],
                                               w2_sb[:, kt2 * D + ym * 128: kt2 * D + (ym + 1) * 128],
                                               h_sb[:, kt2 * 512:(kt2 + 1) * 512],
                                               start=(kt2 == 0), stop=(kt2 == HM - 1))
                            ins.then_inc(s_ymm, 1)
                t.wait_ge(s_acc, 512 * (hf + 1))
                for i in range(E):
                    gi = hf * E + i
                    for m in range(KT):
                        tr = gi * KT + m
                        if tr >= 2:
                            t.wait_ge(s_tre, tr - 1)
                        t.transpose(ps_h[tr % 2][:, 0:128],
                                    acc[:, m * NTH + i * 128: m * NTH + (i + 1) * 128],
                                    identf[:, :]).then_inc(s_trm, 1)

        @blk.scalar
        def _(sc):
            for hf in range(HF):
                for e in range(E):
                    ie = hf * E + e
                    sc.wait_ge(s_bld, 32 * ie + 16)
                    for cc in range(CCH):
                        for hm in range(HM):
                            gh = ie * 64 + cc * HM + hm
                            sc.wait_ge(s_hmm, gh + 1)
                            if cc >= 1 and hm == 0:
                                sc.wait_ge(s_ymm, ie * 32 + cc * KT)
                            sc.activation(h_sb[:, hm * 512:(hm + 1) * 512], ps_h[gh % 4][:, :],
                                          AF.Relu, bias=b1s[:, hm: hm + 1]).then_inc(s_hev, 1)

        @blk.vector
        def _(v):
            for hf in range(HF):
                for e in range(E):
                    ie = hf * E + e
                    for cc in range(CCH):
                        v.wait_ge(s_cwm, 4 * ie + cc + 1)
                        v.tensor_copy(cwbc[:, cc * 512:(cc + 1) * 512], ps_y[0][:, 0:512]).then_inc(s_cwe, 1)
                    v.wait_ge(s_bld, 32 * ie + 32)
                    for cc in range(CCH):
                        for ym in range(KT):
                            gy = ie * 32 + cc * KT + ym
                            v.wait_ge(s_ymm, gy + 1)
                            if gy >= 2:
                                v.wait_ge(s_acc, gy - 1)
                            v.scalar_tensor_tensor(tmp[(gy + 1) % 2][:, :], ps_y[gy % 4][:, :],
                                                   b2s[:, ym: ym + 1],
                                                   cwbc[:, cc * 512:(cc + 1) * 512],
                                                   op0=ALU.add, op1=ALU.mult).then_inc(s_tev, 1)
                for i in range(E):
                    gi = hf * E + i
                    if gi >= 2:
                        v.wait_ge(s_mst[gi % 2], 16 * (gi // 2))
                    for m in range(KT):
                        tr = gi * KT + m
                        v.wait_ge(s_trm, tr + 1)
                        v.tensor_copy(arow[gi % 2][:, m * 128:(m + 1) * 128],
                                      ps_h[tr % 2][:, 0:128]).then_inc(s_tre, 1)
    c.free("cwT") if False else None


def emit_combine_ln1(c):
    nc = c.nc
    identb, cst = c.t["identb"], c.t["cst"]
    x_in, attn_s, moe_s, x1_s = c.dram["x"], c.dram["attn_s"], c.dram["moe_s"], c.dram["x1_s"]
    c.free("cwT")
    G_bf = c.sbuf("G", [128, KT * D], BF16)
    A_bf = c.sbuf("A", [128, KT * D], BF16)
    xn1T = c.sbuf("xn1T", [128, KT * NT], BF16)
    xn1_tok = c.sbuf("xn1_tok", [128, TT * D], BF16)
    with ExitStack() as st, nc.Block() as blk:
        xt = [st.enter_context(nc.sbuf_tensor(f"cb_x{j}", [128, D], F32)) for j in range(2)]
        at = [st.enter_context(nc.sbuf_tensor(f"cb_a{j}", [128, D], F32)) for j in range(2)]
        mt = [st.enter_context(nc.sbuf_tensor(f"cb_m{j}", [128, D], F32)) for j in range(2)]
        x1t = [st.enter_context(nc.sbuf_tensor(f"cb_o{j}", [128, D], F32)) for j in range(2)]
        stat = [st.enter_context(nc.sbuf_tensor(f"cb_s{j}", [128, 8], F32)) for j in range(2)]
        sq = st.enter_context(nc.sbuf_tensor("cb_sq", [128, D], F32))
        ps = [st.enter_context(nc.psum_tensor(f"cb_ps{j}", [128, 1024], BF16)) for j in range(2)]
        s_ld = [c.sem("cb_ld0"), c.sem("cb_ld1")]
        s_sv = c.sem("cb_sv")
        s_x1 = c.sem("cb_x1")
        s_st = c.sem("cb_st")
        s_n = c.sem("cb_n")
        s_tr = c.sem("cb_tr")
        s_ev = c.sem("cb_ev")
        s_xs = [c.sem("cb_xs0"), c.sem("cb_xs1")]

        @blk.sync
        def _(s):
            for i in range(TT):
                if i >= 2:
                    s.wait_ge(s_n, i - 1)
                b = i % 2
                s.dma_start(out=xt[b][:, :], in_=x_in.ap()[i * 128:(i + 1) * 128, :]).then_inc(s_ld[b], 16)
                s.dma_start(out=at[b][:, :], in_=attn_s.ap()[i * 128:(i + 1) * 128, :]).then_inc(s_ld[b], 16)
                s.dma_start(out=mt[b][:, :], in_=moe_s.ap()[i * 128:(i + 1) * 128, :]).then_inc(s_ld[b], 16)

        @blk.gpsimd
        def _(g):
            for i in range(TT):
                g.wait_ge(s_x1, i + 1)
                g.dma_start(out=x1_s.ap()[i * 128:(i + 1) * 128, :], in_=x1t[i % 2][:, :]).then_inc(s_xs[i % 2], 16)
            g.wait_ge(s_xs[0], 16 * (TT // 2))
            g.wait_ge(s_xs[1], 16 * (TT // 2))

        @blk.vector
        def _(v):
            nsv = 0
            for i in range(TT):
                b = i % 2
                v.wait_ge(s_ld[b], 48 * (i // 2 + 1))
                if i >= 2:
                    v.wait_ge(s_xs[b], 16 * (i // 2))
                    v.wait_ge(s_tr, 8 * (i - 1))
                v.tensor_tensor(x1t[b][:, :], xt[b][:, :], at[b][:, :], op=ALU.add)
                v.tensor_tensor(x1t[b][:, :], x1t[b][:, :], mt[b][:, :], op=ALU.add).then_inc(s_x1, 1)
                v.reduce_sum(stat[b][:, 0:1], x1t[b][:, :], axis=AX.X).then_inc(s_sv, 1)
                nsv += 1
                v.wait_ge(s_st, 3 * i + 1)
                v.wait_ge(s_sv, nsv)
                v.tensor_scalar(stat[b][:, 2:3], stat[b][:, 0:1], 1.0 / D, None, op0=ALU.mult).then_inc(s_sv, 1)
                nsv += 1
                v.wait_ge(s_sv, nsv)
                v.tensor_tensor(stat[b][:, 3:4], stat[b][:, 2:3], stat[b][:, 2:3], op=ALU.mult).then_inc(s_sv, 1)
                nsv += 1
                v.wait_ge(s_sv, nsv)
                v.scalar_tensor_tensor(stat[b][:, 3:4], stat[b][:, 1:2], 1.0 / D, stat[b][:, 3:4],
                                       op0=ALU.mult, op1=ALU.subtract).then_inc(s_st, 1)
                v.wait_ge(s_st, 3 * i + 3)
                v.scalar_tensor_tensor(stat[b][:, 6:7], stat[b][:, 2:3], -1.0, stat[b][:, 5:6],
                                       op0=ALU.mult, op1=ALU.mult)
                v.tensor_scalar(xn1_tok[:, i * D:(i + 1) * D], x1t[b][:, :], stat[b][:, 5:6], stat[b][:, 6:7],
                                op0=ALU.mult, op1=ALU.add).then_inc(s_n, 1)
                if i >= 1:
                    v.wait_ge(s_tr, 8 * i)
                    pb = ps[(i - 1) % 2]
                    for kt in range(KT):
                        v.tensor_copy(xn1T[:, kt * NT + (i - 1) * 128: kt * NT + i * 128],
                                      pb[:, kt * 128:(kt + 1) * 128]).then_inc(s_ev, 1)
            v.wait_ge(s_tr, 8 * TT)
            pb = ps[(TT - 1) % 2]
            for kt in range(KT):
                v.tensor_copy(xn1T[:, kt * NT + (TT - 1) * 128: kt * NT + TT * 128],
                              pb[:, kt * 128:(kt + 1) * 128]).then_inc(s_ev, 1)

        @blk.scalar
        def _(sc):
            for i in range(TT):
                b = i % 2
                sc.wait_ge(s_x1, i + 1)
                if i >= 2:
                    sc.wait_ge(s_n, i - 1)
                sc.activation(sq[:, :], x1t[b][:, :], AF.Square, accum_out=stat[b][:, 1:2]).then_inc(s_st, 1)
                sc.wait_ge(s_st, 3 * i + 2)
                sc.activation(stat[b][:, 4:5], stat[b][:, 3:4], AF.Ln, bias=cst[:, 0:1])
                sc.activation(stat[b][:, 5:6], stat[b][:, 4:5], AF.Exp, scale=cst[:, 1:2]).then_inc(s_st, 1)

        @blk.tensor
        def _(t):
            for i in range(TT):
                t.wait_ge(s_n, i + 1)
                if i >= 2:
                    t.wait_ge(s_ev, 8 * (i - 1))
                for kt in range(KT):
                    t.transpose(ps[i % 2][:, kt * 128:(kt + 1) * 128],
                                xn1_tok[:, i * D + kt * 128: i * D + (kt + 1) * 128],
                                identb[:, :]).then_inc(s_tr, 1)


def emit_gram(c):
    nc = c.nc
    xn1_tok, G_bf = c.t["xn1_tok"], c.t["G"]
    with ExitStack() as st, nc.Block() as blk:
        ps = [st.enter_context(nc.psum_tensor(f"gr_ps{j}", [128, 512], F32)) for j in range(4)]
        s_mm = c.sem("gr_mm")
        s_ev = c.sem("gr_ev")

        @blk.tensor
        def _(t):
            for it in range(KT):
                for jc in range(2):
                    idx = it * 2 + jc
                    if idx >= 4:
                        t.wait_ge(s_ev, idx - 3)
                    for tt in range(TT):
                        ins = t.matmul(ps[idx % 4][:, :],
                                       xn1_tok[:, tt * D + it * 128: tt * D + (it + 1) * 128],
                                       xn1_tok[:, tt * D + jc * 512: tt * D + jc * 512 + 512],
                                       start=(tt == 0), stop=(tt == TT - 1))
                    ins.then_inc(s_mm, 1)

        @blk.vector
        def _(v):
            for it in range(KT):
                for jc in range(2):
                    idx = it * 2 + jc
                    v.wait_ge(s_mm, idx + 1)
                    v.tensor_copy(G_bf[:, it * D + jc * 512: it * D + jc * 512 + 512],
                                  ps[idx % 4][:, :]).then_inc(s_ev, 1)
    c.free("xn1_tok")


def emit_amat(c):
    nc = c.nc
    G_bf, A_bf = c.t["G"], c.t["A"]
    q1_in = c.dram["q1"]
    with ExitStack() as st, nc.Block() as blk:
        q1_sb = st.enter_context(nc.sbuf_tensor("am_q1", [128, KT * D], BF16))
        ps = [st.enter_context(nc.psum_tensor(f"am_ps{j}", [128, 512], F32)) for j in range(4)]
        s_ld = c.sem("am_ld")
        s_mm = c.sem("am_mm")
        s_ev = c.sem("am_ev")

        @blk.gpsimd
        def _(g):
            g.dma_start(out=q1_sb[:, :].rearrange("p (kt o) -> p kt o", kt=KT),
                        in_=q1_in.ap().rearrange("(kt p) o -> p kt o", p=128)).then_inc(s_ld, 16)

        @blk.tensor
        def _(t):
            t.wait_ge(s_ld, 16)
            for m in range(KT):
                for jc in range(2):
                    idx = m * 2 + jc
                    if idx >= 4:
                        t.wait_ge(s_ev, idx - 3)
                    for kt in range(KT):
                        ins = t.matmul(ps[idx % 4][:, :],
                                       q1_sb[:, kt * D + m * 128: kt * D + (m + 1) * 128],
                                       G_bf[:, kt * D + jc * 512: kt * D + jc * 512 + 512],
                                       start=(kt == 0), stop=(kt == KT - 1))
                    ins.then_inc(s_mm, 1)

        @blk.vector
        def _(v):
            for m in range(KT):
                for jc in range(2):
                    idx = m * 2 + jc
                    v.wait_ge(s_mm, idx + 1)
                    v.tensor_copy(A_bf[:, m * D + jc * 512: m * D + jc * 512 + 512],
                                  ps[idx % 4][:, :]).then_inc(s_ev, 1)


def emit_attn1(c):
    nc = c.nc
    A_bf, xn1T, identf = c.t["A"], c.t["xn1T"], c.t["identf"]
    attn1_s = c.dram["attn1_s"]
    with ExitStack() as st, nc.Block() as blk:
        of = [st.enter_context(nc.sbuf_tensor(f"b2_of{j}", [128, 512], F32)) for j in range(2)]
        arow = [st.enter_context(nc.sbuf_tensor(f"b2_ar{j}", [128, 4 * D], F32)) for j in range(2)]
        ps_mm = [st.enter_context(nc.psum_tensor(f"b2_pm{j}", [128, 512], F32)) for j in range(4)]
        ps_tr = [st.enter_context(nc.psum_tensor(f"b2_pt{j}", [128, 512], F32)) for j in range(4)]
        s_mm = c.sem("b2_mm")
        s_ev = c.sem("b2_ev")
        s_trm = c.sem("b2_trm")
        s_tre = c.sem("b2_tre")
        s_st = [c.sem("b2_st0"), c.sem("b2_st1")]

        @blk.sync
        def _(s):
            for cch in range(CH):
                s.wait_ge(s_tre, 32 * cch + 32)
                s.dma_start(out=attn1_s.ap()[cch * 512:(cch + 1) * 512, :].rearrange("(j p) d -> p j d", p=128),
                            in_=arow[cch % 2][:, :].rearrange("p (j d) -> p j d", j=4)).then_inc(s_st[cch % 2], 16)
            s.wait_ge(s_st[0], 16 * (CH // 2))
            s.wait_ge(s_st[1], 16 * (CH // 2))

        @blk.tensor
        def _(t):
            for cch in range(CH):
                for m in range(KT):
                    idx = cch * KT + m
                    if idx >= 4:
                        t.wait_ge(s_ev, idx - 3)
                    for kt in range(KT):
                        ins = t.matmul(ps_mm[idx % 4][:, :],
                                       A_bf[:, kt * D + m * 128: kt * D + (m + 1) * 128],
                                       xn1T[:, kt * NT + cch * 512: kt * NT + cch * 512 + 512],
                                       start=(kt == 0), stop=(kt == KT - 1))
                    ins.then_inc(s_mm, 1)
                    if idx >= 1:
                        jdx = idx - 1
                        t.wait_ge(s_ev, jdx + 1)
                        if jdx >= 1:
                            t.wait_ge(s_tre, 4 * jdx)
                        for j in range(4):
                            t.transpose(ps_tr[j][:, 0:128], of[jdx % 2][:, j * 128:(j + 1) * 128],
                                        identf[:, :]).then_inc(s_trm, 1)
            jdx = CH * KT - 1
            t.wait_ge(s_ev, jdx + 1)
            t.wait_ge(s_tre, 4 * jdx)
            for j in range(4):
                t.transpose(ps_tr[j][:, 0:128], of[jdx % 2][:, j * 128:(j + 1) * 128],
                            identf[:, :]).then_inc(s_trm, 1)

        @blk.vector
        def _(v):
            for cch in range(CH):
                for m in range(KT):
                    idx = cch * KT + m
                    v.wait_ge(s_mm, idx + 1)
                    if idx >= 2:
                        v.wait_ge(s_trm, 4 * (idx - 1))
                    v.tensor_copy(of[idx % 2][:, :], ps_mm[idx % 4][:, :]).then_inc(s_ev, 1)
                    if idx >= 1:
                        jdx = idx - 1
                        jc, jm = jdx // KT, jdx % KT
                        if jc >= 2:
                            v.wait_ge(s_st[jc % 2], 16 * (jc // 2))
                        for j in range(4):
                            v.wait_ge(s_trm, 4 * jdx + j + 1)
                            v.tensor_copy(arow[jc % 2][:, j * D + jm * 128: j * D + (jm + 1) * 128],
                                          ps_tr[j][:, 0:128]).then_inc(s_tre, 1)
            jdx = CH * KT - 1
            jc, jm = jdx // KT, jdx % KT
            for j in range(4):
                v.wait_ge(s_trm, 4 * jdx + j + 1)
                v.tensor_copy(arow[jc % 2][:, j * D + jm * 128: j * D + (jm + 1) * 128],
                              ps_tr[j][:, 0:128]).then_inc(s_tre, 1)


def emit_f1(c):
    nc = c.nc
    xn1T = c.t["xn1T"]
    w1T_in, w2T_in, s1_in, s2_in, h_s = (c.dram["w1T"], c.dram["w2T"],
                                         c.dram["s1"], c.dram["s2"], c.dram["h_s"])
    with ExitStack() as st, nc.Block() as blk:
        w1m = [st.enter_context(nc.sbuf_tensor(f"f1_w1{j}", [128, KT * 128], BF16)) for j in range(2)]
        w2m = [st.enter_context(nc.sbuf_tensor(f"f1_w2{j}", [128, KT * 128], BF16)) for j in range(2)]
        s1s = st.enter_context(nc.sbuf_tensor("f1_s1", [128, KH], F32))
        s2s = st.enter_context(nc.sbuf_tensor("f1_s2", [128, KH], F32))
        t1s = [st.enter_context(nc.sbuf_tensor(f"f1_t1{j}", [128, 512], F32)) for j in range(2)]
        hrow = [st.enter_context(nc.sbuf_tensor(f"f1_hr{j}", [128, NT], BF16)) for j in range(2)]
        ps_1 = [st.enter_context(nc.psum_tensor(f"f1_p1{j}", [128, 512], F32)) for j in range(2)]
        ps_2 = [st.enter_context(nc.psum_tensor(f"f1_p2{j}", [128, 512], F32)) for j in range(2)]
        s_sld = c.sem("f1_sld")
        s_wld = c.sem("f1_wld")
        s_mm1 = c.sem("f1_mm1")
        s_mm2 = c.sem("f1_mm2")
        s_t1 = c.sem("f1_t1")
        s_h = c.sem("f1_h")
        s_hs = [c.sem("f1_hs0"), c.sem("f1_hs1")]

        @blk.sync
        def _(s):
            s.dma_start(out=s1s[:, :], in_=s1_in.ap()).then_inc(s_sld, 16)
            s.dma_start(out=s2s[:, :], in_=s2_in.ap()).then_inc(s_sld, 16)
            for m in range(KH):
                s.wait_ge(s_h, m * CH + CH)
                s.dma_start(out=h_s.ap()[m * 128:(m + 1) * 128, :], in_=hrow[m % 2][:, :]).then_inc(s_hs[m % 2], 16)
            s.wait_ge(s_hs[0], 16 * (KH // 2))
            s.wait_ge(s_hs[1], 16 * (KH // 2))

        @blk.gpsimd
        def _(g):
            for m in range(KH):
                if m >= 2:
                    g.wait_ge(s_mm1, (m - 1) * CH)
                    g.wait_ge(s_mm2, (m - 1) * CH)
                g.dma_start(out=w1m[m % 2][:, :].rearrange("p (kt o) -> p kt o", kt=KT),
                            in_=w1T_in.ap()[:, m * 128:(m + 1) * 128].rearrange("(kt p) o -> p kt o", p=128)).then_inc(s_wld, 16)
                g.dma_start(out=w2m[m % 2][:, :].rearrange("p (kt o) -> p kt o", kt=KT),
                            in_=w2T_in.ap()[:, m * 128:(m + 1) * 128].rearrange("(kt p) o -> p kt o", p=128)).then_inc(s_wld, 16)

        @blk.tensor
        def _(t):
            for m in range(KH):
                t.wait_ge(s_wld, 32 * (m + 1))
                for ch in range(CH):
                    idx = m * CH + ch
                    if idx >= 2:
                        t.wait_ge(s_t1, idx - 1)
                        t.wait_ge(s_h, idx - 1)
                    for kt in range(KT):
                        ins = t.matmul(ps_1[idx % 2][:, :],
                                       w1m[m % 2][:, kt * 128:(kt + 1) * 128],
                                       xn1T[:, kt * NT + ch * 512: kt * NT + ch * 512 + 512],
                                       start=(kt == 0), stop=(kt == KT - 1))
                    ins.then_inc(s_mm1, 1)
                    for kt in range(KT):
                        ins = t.matmul(ps_2[idx % 2][:, :],
                                       w2m[m % 2][:, kt * 128:(kt + 1) * 128],
                                       xn1T[:, kt * NT + ch * 512: kt * NT + ch * 512 + 512],
                                       start=(kt == 0), stop=(kt == KT - 1))
                    ins.then_inc(s_mm2, 1)

        @blk.scalar
        def _(sc):
            sc.wait_ge(s_sld, 16)
            for m in range(KH):
                for ch in range(CH):
                    idx = m * CH + ch
                    sc.wait_ge(s_mm1, idx + 1)
                    if idx >= 2:
                        sc.wait_ge(s_h, idx - 1)
                    sc.activation(t1s[idx % 2][:, :], ps_1[idx % 2][:, :], AF.Silu,
                                  scale=s1s[:, m: m + 1]).then_inc(s_t1, 1)

        @blk.vector
        def _(v):
            v.wait_ge(s_sld, 32)
            for m in range(KH):
                if m >= 2:
                    v.wait_ge(s_hs[m % 2], 16 * (m // 2))
                for ch in range(CH):
                    idx = m * CH + ch
                    v.wait_ge(s_mm2, idx + 1)
                    v.wait_ge(s_t1, idx + 1)
                    v.scalar_tensor_tensor(hrow[m % 2][:, ch * 512:(ch + 1) * 512],
                                           ps_2[idx % 2][:, :], s2s[:, m: m + 1], t1s[idx % 2][:, :],
                                           op0=ALU.mult, op1=ALU.mult).then_inc(s_h, 1)
    c.free("xn1T", "A", "G")


def emit_f23(c, src_name, w_name, s_name, dst_name, tag):
    nc = c.nc
    src_s, wT_in, ss_in, dst_s = (c.dram[src_name], c.dram[w_name],
                                  c.dram[s_name], c.dram[dst_name])
    HF = 2
    NTH = NT // HF
    CCH = NTH // 512
    with ExitStack() as st, nc.Block() as blk:
        srch = st.enter_context(nc.sbuf_tensor(f"{tag}_src", [128, KH * NTH], BF16))
        wm = [st.enter_context(nc.sbuf_tensor(f"{tag}_w{j}", [128, KH * 128], BF16)) for j in range(2)]
        sss = st.enter_context(nc.sbuf_tensor(f"{tag}_s", [128, KH], F32))
        grow = [st.enter_context(nc.sbuf_tensor(f"{tag}_g{j}", [128, NTH], BF16)) for j in range(2)]
        ps = [st.enter_context(nc.psum_tensor(f"{tag}_p{j}", [128, 512], F32)) for j in range(4)]
        s_sld = c.sem(f"{tag}_sld")
        s_xld = c.sem(f"{tag}_xld")
        s_wld = c.sem(f"{tag}_wld")
        s_mm = c.sem(f"{tag}_mm")
        s_g = c.sem(f"{tag}_g")
        s_gs = [c.sem(f"{tag}_gs0"), c.sem(f"{tag}_gs1")]

        @blk.sync
        def _(s):
            s.dma_start(out=sss[:, :], in_=ss_in.ap()).then_inc(s_sld, 16)
            for hf in range(HF):
                for kt in range(KH):
                    if hf >= 1 and kt == 0:
                        s.wait_ge(s_mm, KH * CCH * hf)
                    s.dma_start(out=srch[:, kt * NTH:(kt + 1) * NTH],
                                in_=src_s.ap()[kt * 128:(kt + 1) * 128, hf * NTH:(hf + 1) * NTH]).then_inc(s_xld, 16)
                for m in range(KH):
                    gm = hf * KH + m
                    s.wait_ge(s_g, gm * CCH + CCH)
                    s.dma_start(out=dst_s.ap()[m * 128:(m + 1) * 128, hf * NTH:(hf + 1) * NTH],
                                in_=grow[gm % 2][:, :]).then_inc(s_gs[gm % 2], 16)
            s.wait_ge(s_gs[0], 16 * KH)
            s.wait_ge(s_gs[1], 16 * KH)

        @blk.gpsimd
        def _(g):
            for hf in range(HF):
                for m in range(KH):
                    gm = hf * KH + m
                    if gm >= 2:
                        g.wait_ge(s_mm, (gm - 1) * CCH)
                    g.dma_start(out=wm[gm % 2][:, :].rearrange("p (kt o) -> p kt o", kt=KH),
                                in_=wT_in.ap()[:, m * 128:(m + 1) * 128].rearrange("(kt p) o -> p kt o", p=128)).then_inc(s_wld, 16)

        @blk.tensor
        def _(t):
            for hf in range(HF):
                t.wait_ge(s_xld, 16 * KH * (hf + 1))
                for m in range(KH):
                    gm = hf * KH + m
                    t.wait_ge(s_wld, 16 * (gm + 1))
                    for cc in range(CCH):
                        idx = gm * CCH + cc
                        if idx >= 4:
                            t.wait_ge(s_g, idx - 3)
                        for kt in range(KH):
                            ins = t.matmul(ps[idx % 4][:, :],
                                           wm[gm % 2][:, kt * 128:(kt + 1) * 128],
                                           srch[:, kt * NTH + cc * 512: kt * NTH + cc * 512 + 512],
                                           start=(kt == 0), stop=(kt == KH - 1))
                        ins.then_inc(s_mm, 1)

        @blk.scalar
        def _(sc):
            sc.wait_ge(s_sld, 16)
            for hf in range(HF):
                for m in range(KH):
                    gm = hf * KH + m
                    if gm >= 2:
                        sc.wait_ge(s_gs[gm % 2], 16 * (gm // 2))
                    for cc in range(CCH):
                        idx = gm * CCH + cc
                        sc.wait_ge(s_mm, idx + 1)
                        sc.activation(grow[gm % 2][:, cc * 512:(cc + 1) * 512], ps[idx % 4][:, :],
                                      AF.Gelu, scale=sss[:, m: m + 1]).then_inc(s_g, 1)


def emit_f4(c):
    nc = c.nc
    identf = c.t["identf"]
    g2_s, w3T_in, s3_in = c.dram["g2_s"], c.dram["w3T"], c.dram["s3"]
    x1_s, attn1_s, out_t = c.dram["x1_s"], c.dram["attn1_s"], c.dram["out"]
    QF = 4
    NTQ = NT // QF
    TQ = NTQ // 128
    with ExitStack() as st, nc.Block() as blk:
        g2q = st.enter_context(nc.sbuf_tensor("f4_g2", [128, KH * NTQ], BF16))
        w3sb = st.enter_context(nc.sbuf_tensor("f4_w3", [128, KH * D], BF16))
        s3s = st.enter_context(nc.sbuf_tensor("f4_s3", [128, KT], F32))
        acc4 = st.enter_context(nc.sbuf_tensor("f4_acc", [128, KT * NTQ], F32))
        x1t = [st.enter_context(nc.sbuf_tensor(f"f4_x{j}", [128, D], F32)) for j in range(2)]
        a1t = [st.enter_context(nc.sbuf_tensor(f"f4_a{j}", [128, D], F32)) for j in range(2)]
        orow = [st.enter_context(nc.sbuf_tensor(f"f4_o{j}", [128, D], F32)) for j in range(2)]
        ps = [st.enter_context(nc.psum_tensor(f"f4_p{j}", [128, 512], F32)) for j in range(4)]
        ps_tr = [st.enter_context(nc.psum_tensor(f"f4_pt{j}", [128, 512], F32)) for j in range(2)]
        s_sld = c.sem("f4_sld")
        s_xld = c.sem("f4_xld")
        s_rld = [c.sem("f4_rl0"), c.sem("f4_rl1")]
        s_wld = c.sem("f4_wld")
        s_mm = c.sem("f4_mm")
        s_ev = c.sem("f4_ev")
        s_trm = c.sem("f4_trm")
        s_tre = c.sem("f4_tre")
        s_or = c.sem("f4_or")
        s_ost = [c.sem("f4_os0"), c.sem("f4_os1")]

        @blk.sync
        def _(s):
            s.dma_start(out=s3s[:, :], in_=s3_in.ap()).then_inc(s_sld, 16)
            for qt in range(QF):
                for kt in range(KH):
                    if qt >= 1 and kt == 0:
                        s.wait_ge(s_mm, 16 * qt)
                    s.dma_start(out=g2q[:, kt * NTQ:(kt + 1) * NTQ],
                                in_=g2_s.ap()[kt * 128:(kt + 1) * 128, qt * NTQ:(qt + 1) * NTQ]).then_inc(s_xld, 16)
                for i in range(TQ):
                    gi = qt * TQ + i
                    if gi >= 2:
                        s.wait_ge(s_or, gi - 1)
                    s.dma_start(out=x1t[gi % 2][:, :],
                                in_=x1_s.ap()[gi * 128:(gi + 1) * 128, :]).then_inc(s_rld[gi % 2], 16)
                    s.dma_start(out=a1t[gi % 2][:, :],
                                in_=attn1_s.ap()[gi * 128:(gi + 1) * 128, :]).then_inc(s_rld[gi % 2], 16)

        @blk.gpsimd
        def _(g):
            g.dma_start(out=w3sb[:, :].rearrange("p (kt o) -> p kt o", kt=KH),
                        in_=w3T_in.ap().rearrange("(kt p) o -> p kt o", p=128)).then_inc(s_wld, 16)
            for qt in range(QF):
                for i in range(TQ):
                    gi = qt * TQ + i
                    g.wait_ge(s_or, gi + 1)
                    g.dma_start(out=out_t.ap()[gi * 128:(gi + 1) * 128, :],
                                in_=orow[gi % 2][:, :]).then_inc(s_ost[gi % 2], 16)
            g.wait_ge(s_ost[0], 16 * (NT // 256))
            g.wait_ge(s_ost[1], 16 * (NT // 256))

        @blk.tensor
        def _(t):
            t.wait_ge(s_wld, 16)
            for qt in range(QF):
                t.wait_ge(s_xld, 16 * KH * (qt + 1))
                for m in range(KT):
                    for ccc in range(2):
                        idx = qt * 16 + m * 2 + ccc
                        if idx >= 4:
                            t.wait_ge(s_ev, idx - 3)
                        for kt in range(KH):
                            ins = t.matmul(ps[idx % 4][:, :],
                                           w3sb[:, kt * D + m * 128: kt * D + (m + 1) * 128],
                                           g2q[:, kt * NTQ + ccc * 512: kt * NTQ + ccc * 512 + 512],
                                           start=(kt == 0), stop=(kt == KH - 1))
                        ins.then_inc(s_mm, 1)
                t.wait_ge(s_ev, 16 * (qt + 1))
                for i in range(TQ):
                    gi = qt * TQ + i
                    for m in range(KT):
                        tr = gi * KT + m
                        if tr >= 2:
                            t.wait_ge(s_tre, tr - 1)
                        t.transpose(ps_tr[tr % 2][:, 0:128],
                                    acc4[:, m * NTQ + i * 128: m * NTQ + (i + 1) * 128],
                                    identf[:, :]).then_inc(s_trm, 1)

        @blk.vector
        def _(v):
            v.wait_ge(s_sld, 16)
            for qt in range(QF):
                for m in range(KT):
                    for ccc in range(2):
                        idx = qt * 16 + m * 2 + ccc
                        if qt >= 1 and m == 0 and ccc == 0:
                            v.wait_ge(s_trm, 64 * qt)
                        v.wait_ge(s_mm, idx + 1)
                        v.tensor_scalar(acc4[:, m * NTQ + ccc * 512: m * NTQ + ccc * 512 + 512],
                                        ps[idx % 4][:, :], s3s[:, m: m + 1], None,
                                        op0=ALU.mult).then_inc(s_ev, 1)
                for i in range(TQ):
                    gi = qt * TQ + i
                    if gi >= 2:
                        v.wait_ge(s_ost[gi % 2], 16 * (gi // 2))
                    for m in range(KT):
                        tr = gi * KT + m
                        v.wait_ge(s_trm, tr + 1)
                        v.tensor_copy(orow[gi % 2][:, m * 128:(m + 1) * 128],
                                      ps_tr[tr % 2][:, 0:128]).then_inc(s_tre, 1)
                    v.wait_ge(s_tre, 8 * (gi + 1))
                    v.wait_ge(s_rld[gi % 2], 32 * (gi // 2 + 1))
                    v.tensor_tensor(orow[gi % 2][:, :], orow[gi % 2][:, :], x1t[gi % 2][:, :], op=ALU.add)
                    v.tensor_tensor(orow[gi % 2][:, :], orow[gi % 2][:, :], a1t[gi % 2][:, :], op=ALU.add).then_inc(s_or, 1)


def make_in_maps(inputs):
    f32 = np.float32
    asc = np.ascontiguousarray

    def pp(v, cols):
        return asc(np.asarray(v, f32).reshape(cols, 128).T)

    common = {
        "wq": asc(np.asarray(inputs["L0_attn_wq"], f32)),
        "wk": asc(np.asarray(inputs["L0_attn_wk"], f32)),
        "wv": asc(np.asarray(inputs["L0_attn_wv"], f32)),
        "wo": asc(np.asarray(inputs["L0_attn_wo"], f32)),
        "bo": pp(inputs["L0_attn_bo"], KT),
        "projT": asc(np.asarray(inputs["L0_attn_proj"], f32).T),
        "wg": asc(np.asarray(inputs["L0_ffn_Wg"], f32)),
        "bg": asc(np.asarray(inputs["L0_ffn_bg"], f32).reshape(E, 1)),
        "gbias": asc(np.asarray(inputs["L0_ffn_bias"], f32).reshape(E, 1)),
        "w1e": asc(np.asarray(inputs["L0_ffn_W1"], f32)),
        "b1e": asc(np.asarray(inputs["L0_ffn_b1"], f32).reshape(E, DFF // 128, 128).transpose(0, 2, 1)),
        "w2e": asc(np.asarray(inputs["L0_ffn_W2"], f32)),
        "b2e": asc(np.asarray(inputs["L0_ffn_b2"], f32).reshape(E, KT, 128).transpose(0, 2, 1)),
        "q1": asc(np.asarray(inputs["L1_attn_queries"], f32)),
        "w1T": asc(np.asarray(inputs["L1_ffn_w1_w"], f32).T),
        "w2T": asc(np.asarray(inputs["L1_ffn_w2_w"], f32).T),
        "h1T": asc(np.asarray(inputs["L1_ffn_h1_w"], f32).T),
        "h2T": asc(np.asarray(inputs["L1_ffn_h2_w"], f32).T),
        "w3T": asc(np.asarray(inputs["L1_ffn_w3_w"], f32).T),
        "s1": pp(inputs["L1_ffn_w1_s"], KH),
        "s2": pp(inputs["L1_ffn_w2_s"], KH),
        "sh1": pp(inputs["L1_ffn_h1_s"], KH),
        "sh2": pp(inputs["L1_ffn_h2_s"], KH),
        "s3": pp(inputs["L1_ffn_w3_s"], KT),
        "identf": np.eye(128, dtype=f32),
        "identb": np.eye(128).astype(ml_dtypes.bfloat16),
        "sel64": asc(np.kron(np.eye(2, dtype=f32), np.full((64, 1), 0.5 * DN * DN, f32))),
        "ones64s": np.full((DH, 1), 0.5 * DN * DN, f32),
        "selE": np.kron(np.eye(E), np.ones((1, 128))).astype(ml_dtypes.bfloat16),
        "onesr": np.ones((1, 128), ml_dtypes.bfloat16),
        "onesrf": np.ones((1, 128), f32),
    }
    x = np.asarray(inputs["L0_attn_x"], f32)
    return [dict(common, x=asc(x[b])) for b in range(8)]


def kernel(**inputs):
    in_maps = make_in_maps(inputs)
    nc = build_program()
    res = run_bass_kernel_spmd(nc, in_maps, core_ids=list(range(8)))
    out = np.stack([np.asarray(r["out"]) for r in res.results], axis=0)
    return out.astype(np.float32)
